# version 03
# baseline (speedup 1.0000x reference)
import math
import os
import numpy as np
import ml_dtypes
from contextlib import ExitStack

import concourse.bass as bass
import concourse.mybir as mybir
from concourse.bass_utils import run_bass_kernel_spmd

F32 = mybir.dt.float32
BF16 = mybir.dt.bfloat16
AF = mybir.ActivationFunctionType
ALU = mybir.AluOpType
AX = mybir.AxisListType
BF = ml_dtypes.bfloat16

NCORES = 8
SEQ = 2
L = 2048
T = SEQ * L
D = 1024
FC = D // 128
DI = 2048
IC = DI // 128
DST = 16
NH = 16
DH = 64
W = 256
DF = 2816
JP = DF // 128
NEG = -1.0e30

DBG = int(os.environ.get("MK_DBG", "0"))


def _rot_tables():
    half = np.arange(0, DH, 2, dtype=np.float64)
    inv_freq = 1.0 / 10000.0 ** (half / DH)
    t = np.arange(L, dtype=np.float64)
    freqs = t[:, None] * inv_freq[None, :]
    freqs = np.concatenate([freqs, freqs], -1)
    base_scale = (half + 0.4 * DH) / (1.4 * DH)
    power = (t - L // 2) / (W // 2)
    scale = base_scale[None, :] ** power[:, None]
    scale = np.concatenate([scale, scale], -1)
    cs, sn = np.cos(freqs), np.sin(freqs)
    qk = 8.0
    csq = (cs * scale * qk).T
    ssq = (sn * scale * qk).T
    csi = (cs / scale).T
    ssi = (sn / scale).T
    f = lambda a: np.ascontiguousarray(a.astype(np.float32))
    return f(csq), f(ssq), f(csi), f(ssi)


def _consts():
    c = {}
    c["ident_bf"] = np.eye(128, dtype=np.float32).astype(BF)
    c["ident_f32"] = np.eye(128, dtype=np.float32)
    R = np.zeros((64, 64), np.float32)
    for m in range(32):
        R[m + 32, m] = -1.0
        R[m, m + 32] = 1.0
    c["rot"] = R.astype(BF)
    csq, ssq, csi, ssi = _rot_tables()
    c["csq"], c["ssq"] = csq.astype(BF), ssq.astype(BF)
    c["csi"], c["ssi"] = csi.astype(BF), ssi.astype(BF)
    sel1 = np.zeros((64, 16, 16), np.float32)
    for h in range(16):
        sel1[:, h, h] = 1.0
    c["selsum"] = sel1.reshape(64, 256).astype(BF)
    sel2 = np.zeros((16, 16, 64), np.float32)
    for h in range(16):
        sel2[h, h, :] = 1.0
    c["selbc"] = sel2.reshape(16, 1024).astype(BF)
    mq = np.full((3, 128, 384), NEG, np.float32)
    for v, clip in enumerate([0, 128, 256]):
        for i in range(128):
            lo, hi = max(i, clip), i + 257
            mq[v, i, lo:hi] = 0.0
    c["maskq"] = mq.astype(BF)
    mt = np.full((4, 128, 256), NEG, np.float32)
    for cc in range(2):
        for j in range(128):
            mt[cc, j, 0:min(128 * cc + j + 1, 256)] = 0.0
            mt[2 + cc, j, min(128 * cc + j, 256):256] = 0.0
    c["maskt"] = mt.astype(BF)
    c["ones_1x128"] = np.ones((1, 128), np.float32).astype(BF)
    c["ones_128x1"] = np.ones((128, 1), np.float32).astype(BF)
    c["ones_1x64"] = np.ones((1, 64), np.float32).astype(BF)
    return c


def _chunk2d(w, kc, oc):
    K, N = w.shape
    assert K == kc * 128 and N == oc * 128, (w.shape, kc, oc)
    return np.ascontiguousarray(
        w.reshape(kc, 128, oc, 128).transpose(0, 2, 1, 3))


def prep_core_inputs(core, inputs, consts):
    d = {}
    x = np.asarray(inputs["L0_attn_x"], np.float32)
    d["x_in"] = np.ascontiguousarray(
        x[core * SEQ:(core + 1) * SEQ].reshape(T, D))
    return d


def prep_shared_inputs(inputs, consts):
    f32 = lambda k: np.asarray(inputs[k], np.float32)
    d = {}

    def pm(w, kc, oc):
        ch = _chunk2d(w, kc, oc)
        return np.ascontiguousarray(
            ch.transpose(2, 0, 1, 3).reshape(128, kc * oc * 128))

    d["w_in"] = pm(f32("L0_attn_W_in"), FC, 32)
    d["w_out"] = pm(f32("L0_attn_W_out"), IC, FC)
    wx = f32("L0_attn_W_x")
    d["w_x"] = np.ascontiguousarray(
        wx.reshape(IC, 128, 32).transpose(1, 0, 2).reshape(128, IC * 32))
    d["w_dt"] = np.ascontiguousarray(f32("L0_attn_W_dt"))
    d["w_bp"] = np.ascontiguousarray(f32("L0_attn_W_Bp"))
    d["b_dt"] = np.ascontiguousarray(
        f32("L0_attn_b_dt").reshape(IC, 128).T)
    d["d_param"] = np.ascontiguousarray(
        f32("L0_attn_D_param").reshape(IC, 128).T)
    cw = f32("L0_attn_conv_w")[:, 0, :]
    d["conv_w"] = np.ascontiguousarray(
        cw.reshape(IC, 128, 4).transpose(1, 0, 2).reshape(128, IC * 4))
    wqkv = f32("L0_ffn_Wqkv")
    d["wqkv"] = np.ascontiguousarray(
        np.stack([pm(wqkv[i], FC, 24) for i in range(6)]).reshape(
            6 * 128, FC * 24 * 128))
    wo = f32("L0_ffn_Wo")
    d["wo"] = np.ascontiguousarray(
        np.stack([pm(wo[i], FC, FC) for i in range(6)]).reshape(
            6 * 128, FC * FC * 128))
    w1 = f32("L0_ffn_W1")
    w1p = np.zeros((6, D, 2, DF), np.float32)
    w1p[:, :, 0, 0:2730] = w1[:, :, 0:2730]
    w1p[:, :, 1, 0:2730] = w1[:, :, 2730:5460]
    w1c = np.zeros((6, JP, 128, 2 * FC * 128), np.float32)
    for i in range(6):
        for h in range(2):
            ch = _chunk2d(w1p[i, :, h], FC, JP)
            for jp in range(JP):
                blk = ch[:, jp].transpose(1, 0, 2).reshape(128, FC * 128)
                w1c[i, jp, :, h * FC * 128:(h + 1) * FC * 128] = blk
    d["w1"] = np.ascontiguousarray(w1c.reshape(6 * JP * 128, 2 * FC * 128))
    w2 = f32("L0_ffn_W2")
    w2p = np.zeros((6, DF, D), np.float32)
    w2p[:, 0:2730] = w2
    w2c = np.zeros((6, FC, 128, JP * 128), np.float32)
    for i in range(6):
        ch = _chunk2d(w2p[i], JP, FC)
        for oc in range(FC):
            w2c[i, oc] = ch[:, oc].transpose(1, 0, 2).reshape(128, JP * 128)
    d["w2"] = np.ascontiguousarray(w2c.reshape(6 * FC * 128, JP * 128))
    b1 = f32("L0_ffn_b1")
    b1p = np.zeros((6, 2, DF), np.float32)
    b1p[:, 0, 0:2730] = b1[:, 0:2730]
    b1p[:, 1, 0:2730] = b1[:, 2730:5460]
    d["b1"] = np.ascontiguousarray(
        b1p.reshape(6, 2, JP, 128).transpose(3, 0, 1, 2).reshape(
            128, 6 * 2 * JP))
    d["b2"] = np.ascontiguousarray(
        f32("L0_ffn_b2").reshape(6, FC, 128).transpose(2, 0, 1).reshape(
            128, 6 * FC))
    d["ln_g"] = np.ascontiguousarray(
        f32("L0_ffn_ln_g").reshape(6, FC, 128).transpose(2, 0, 1).reshape(
            128, 6 * FC))
    d["ln_b"] = np.ascontiguousarray(
        f32("L0_ffn_ln_b").reshape(6, FC, 128).transpose(2, 0, 1).reshape(
            128, 6 * FC))
    qs = f32("L0_ffn_q_scale")
    ks = f32("L0_ffn_k_scale")
    d["qs2"] = np.ascontiguousarray(qs.T)
    d["ks2"] = np.ascontiguousarray(ks.T)
    for k, v in consts.items():
        d["c_" + k] = v
    return d


class Em:
    ENGS = ("pe", "dve", "act", "sp", "pool")
    DMA_ENGS = ("sp", "pool")

    def __init__(self, sync_same=False):
        self.steps = []
        self.last_w = {}
        self.readers = {}
        self._fence = []
        self.sync_same = sync_same

    def fence(self):
        last = {}
        for i, s in enumerate(self.steps):
            last[s["eng"]] = i
        self._fence = sorted(last.values())

    def begin_collect(self):
        assert not hasattr(self, "_collect") or self._collect is None
        self._collect = []

    def end_collect(self):
        c = self._collect
        self._collect = None
        return c

    @staticmethod
    def merge(a, b):
        from collections import defaultdict
        atouch = defaultdict(int)
        awrite = defaultdict(int)
        for st in a:
            for t in set(st["reads"]) | set(st["writes"]):
                atouch[t] += 1
            for t in set(st["writes"]):
                awrite[t] += 1
        out = []
        ia = ib = 0
        while ia < len(a) or ib < len(b):
            pick_b = False
            if ib < len(b):
                if ia >= len(a):
                    pick_b = True
                elif ib / len(b) <= ia / len(a):
                    st = b[ib]
                    if (all(awrite[t] == 0 for t in set(st["reads"]))
                            and all(atouch[t] == 0
                                    for t in set(st["writes"]))):
                        pick_b = True
            if pick_b:
                out.append(b[ib]); ib += 1
            else:
                st = a[ia]
                for t in set(st["reads"]) | set(st["writes"]):
                    atouch[t] -= 1
                for t in set(st["writes"]):
                    awrite[t] -= 1
                out.append(st); ia += 1
        return out

    def emit_collected(self, ordered):
        mapping = {}
        for st in ordered:
            deps = tuple(mapping[d] if isinstance(d, str) else d
                         for d in st["xdeps"])
            i = self.step(st["eng"], st["fn"], reads=st["reads"],
                          writes=st["writes"], deps=deps, tag=st["tag"],
                          small=st["small"])
            mapping[st["uid"]] = i
            if st["is_dma"]:
                self.steps[i]["is_dma"] = True
        return mapping

    def step(self, eng, fn, reads=(), writes=(), deps=(), tag="", small=False):
        if getattr(self, "_collect", None) is not None:
            uid = f"c{len(self._collect)}_{tag}"
            self._collect.append(dict(
                eng=eng, fn=fn, reads=tuple(reads), writes=tuple(writes),
                xdeps=tuple(d for d in deps if d is not None), tag=tag,
                small=small, uid=uid, is_dma=False))
            return uid
        assert eng in self.ENGS, eng
        i = len(self.steps)
        dd = set(d for d in deps if d is not None)
        dd.update(self._fence)
        for t in reads:
            w = self.last_w.get(t)
            if w is not None:
                dd.add(w)
        for t in writes:
            w = self.last_w.get(t)
            if w is not None:
                dd.add(w)
            for r in self.readers.get(t, ()):
                dd.add(r)
        for t in reads:
            self.readers.setdefault(t, []).append(i)
        for t in writes:
            self.last_w[t] = i
            self.readers[t] = []
        self.steps.append(dict(eng=eng, fn=fn, deps=sorted(dd), tag=tag,
                               small=small, is_dma=False))
        return i

    def dma(self, queue, dst, src, reads=(), writes=(), deps=(), tag=""):
        def fn(eng):
            return eng.dma_start(dst, src)
        i = self.step(queue, fn, reads=reads, writes=writes, deps=deps,
                      tag=tag)
        if isinstance(i, str):
            self._collect[-1]["is_dma"] = True
        else:
            self.steps[i]["is_dma"] = True
        return i

    COMPUTE_ENGS = ("dve", "act", "pool")

    def finalize(self, nc, ctx):
        sems = {e: ctx.enter_context(nc.semaphore(f"sem_{e}"))
                for e in self.ENGS}
        inc_amt = {e: (16 if e in self.DMA_ENGS else 1) for e in self.ENGS}
        need_inc = [s["eng"] in self.DMA_ENGS for s in self.steps]
        for s in self.steps:
            for dd in s["deps"]:
                if self.sync_same or self.steps[dd]["eng"] != s["eng"]:
                    need_inc[dd] = True
        last_of = {}
        for i, s in enumerate(self.steps):
            last_of[s["eng"]] = i
        for i in last_of.values():
            need_inc[i] = True
        ordinal = [0] * len(self.steps)
        cnt = {e: 0 for e in self.ENGS}
        for i, s in enumerate(self.steps):
            if need_inc[i]:
                cnt[s["eng"]] += 1
            ordinal[i] = cnt[s["eng"]]
        self._sems, self._need_inc, self._ordinal = sems, need_inc, ordinal
        self._inc_amt = inc_amt
        self._final_counts = {e: cnt[e] * inc_amt[e] for e in self.ENGS}

    def run_engine(self, eng_name, eng_obj):
        sems = self._sems
        hw = {e: 0 for e in self.ENGS}
        n = 0
        for i, s in enumerate(self.steps):
            if s["eng"] != eng_name:
                continue
            waitmax = {}
            for dd in s["deps"]:
                de = self.steps[dd]["eng"]
                if (de == eng_name and eng_name not in self.DMA_ENGS
                        and not self.sync_same):
                    continue
                assert self._need_inc[dd], (i, dd)
                v = self._ordinal[dd] * self._inc_amt[de]
                waitmax[de] = max(waitmax.get(de, 0), v)
            for de, v in waitmax.items():
                if v > hw[de]:
                    eng_obj.wait_ge(sems[de], v)
                    hw[de] = v
            if (eng_name in self.COMPUTE_ENGS and not s["is_dma"] and any(
                    self.steps[dd]["eng"] == eng_name
                    and self.steps[dd]["small"] for dd in s["deps"])):
                eng_obj.drain()
            inst = s["fn"](eng_obj)
            if self._need_inc[i]:
                assert inst is not None, (i, s["tag"])
                if (eng_name in self.COMPUTE_ENGS and not s["is_dma"]
                        and (s["small"] or self.sync_same)):
                    inst = eng_obj.drain()
                inst.then_inc(sems[eng_name], self._inc_amt[eng_name])
            n += 1
        return n

    def final_barrier(self, eng_obj, skip=()):
        for e in self.ENGS:
            if e in skip:
                continue
            v = self._final_counts[e]
            if v:
                eng_obj.wait_ge(self._sems[e], v)


class MK:
    def __init__(self, nc, n_phases=99):
        self.nc = nc
        self.em = Em(sync_same=bool(int(os.environ.get("MK_SYNCSAME", "0"))))
        self.root = ExitStack()
        self.scopes = [self.root]
        self.n_phases = n_phases
        self.dram = {}
        self.sb = {}
        self.ps = {}
        self._declare()

    def emit_pair(self, fn, k0):
        if os.environ.get("MK_MERGE", "1") == "1":
            em = self.em
            em.begin_collect(); fn(k0); la = em.end_collect()
            em.begin_collect(); fn(k0 + 1); lb = em.end_collect()
            em.emit_collected(Em.merge(la, lb))
        else:
            fn(k0); fn(k0 + 1)

    def push_scope(self):
        self.scopes.append(ExitStack())

    def pop_scope(self):
        self.scopes.pop().close()
        self.em.fence()

    def _din(self, name, shape, dt):
        self.dram[name] = self.nc.dram_tensor(name, shape, dt,
                                              kind="ExternalInput").ap()
        return self.dram[name]

    def _dscratch(self, name, shape, dt, dump=False):
        kind = "ExternalOutput" if (DBG and dump) else "Internal"
        self.dram[name] = self.nc.dram_tensor(name, shape, dt, kind=kind).ap()
        return self.dram[name]

    def _sbuf(self, name, shape, dt):
        t = self.scopes[-1].enter_context(self.nc.sbuf_tensor(name, shape, dt))
        self.sb[name] = t
        return t

    def _psum(self, name, shape=(128, 512), dt=F32):
        t = self.root.enter_context(self.nc.psum_tensor(name, list(shape), dt))
        self.ps[name] = t
        return t

    def _declare(self):
        din = self._din
        din("x_in", [T, D], F32)
        din("w_in", [128, FC * 32 * 128], F32)
        din("w_out", [128, IC * FC * 128], F32)
        din("w_x", [128, IC * 32], F32)
        din("w_dt", [DST, IC * 128], F32)
        din("w_bp", [DST, IC * 128], F32)
        din("b_dt", [128, IC], F32)
        din("d_param", [128, IC], F32)
        din("conv_w", [128, IC * 4], F32)
        din("wqkv", [6 * 128, FC * 24 * 128], F32)
        din("wo", [6 * 128, FC * FC * 128], F32)
        din("w1", [6 * JP * 128, 2 * FC * 128], F32)
        din("w2", [6 * FC * 128, JP * 128], F32)
        din("b1", [128, 6 * 2 * JP], F32)
        din("b2", [128, 6 * FC], F32)
        din("ln_g", [128, 6 * FC], F32)
        din("ln_b", [128, 6 * FC], F32)
        din("qs2", [64, 6], F32)
        din("ks2", [64, 6], F32)
        din("c_ident_bf", [128, 128], BF16)
        din("c_ident_f32", [128, 128], F32)
        din("c_rot", [64, 64], BF16)
        din("c_csq", [64, L], BF16)
        din("c_ssq", [64, L], BF16)
        din("c_csi", [64, L], BF16)
        din("c_ssi", [64, L], BF16)
        din("c_selsum", [64, 256], BF16)
        din("c_selbc", [16, 1024], BF16)
        din("c_maskq", [3 * 128, 384], BF16)
        din("c_maskt", [4 * 128, 256], BF16)
        din("c_ones_1x128", [1, 128], BF16)
        din("c_ones_128x1", [128, 1], BF16)
        din("c_ones_1x64", [1, 64], BF16)
        self.dram["out"] = self.nc.dram_tensor(
            "out", [T, D], F32, kind="ExternalOutput").ap()
        self._dscratch("hx0", [D, T], F32, dump=True)
        self._dscratch("hx1", [D, T], F32, dump=True)
        self._dscratch("hx2a", [D, T], F32, dump=True)
        self._dscratch("hx2b", [D, T], F32, dump=True)
        self._dscratch("hu", [DI, T], BF16, dump=True)
        self._dscratch("hsz", [DI, T], BF16, dump=True)
        self._dscratch("hglu", [DF, T], BF16, dump=True)
        if DBG:
            self._dscratch("dbg_st", [8 * 128, 8], F32, dump=True)
            self._dscratch("dbg_ab", [8 * 128, 8], F32, dump=True)
            self._dscratch("dbg_row", [8, 2048], F32, dump=True)
            self._dscratch("dbg_xn", [128, FC * 512], BF16, dump=True)
        for i in range(4):
            self._psum(f"pb{i}")
        self._psum("pbA", (128, 1024))
        self._psum("pbB", (128, 1024))

        class _PV:
            def __init__(self, t, off):
                self.t, self.off = t, off

            def __getitem__(self, key):
                if key == slice(None):
                    return self.t[:, self.off:self.off + 512]
                rows, cols = key
                c0 = cols.start or 0
                c1 = cols.stop if cols.stop is not None else 512
                return self.t[rows, self.off + c0:self.off + c1]
        self.ps["pb4"] = _PV(self.ps["pbA"], 0)
        self.ps["pb5"] = _PV(self.ps["pbA"], 512)
        self.ps["pb6"] = _PV(self.ps["pbB"], 0)
        self.ps["pb7"] = _PV(self.ps["pbB"], 512)
        sbn = self._sbuf
        sbn("k_ident_bf", [128, 128], BF16)
        sbn("k_ident_f32", [128, 128], F32)
        sbn("k_ones_1x128", [1, 128], BF16)
        sbn("k_ones_128x1", [128, 1], BF16)
        sbn("k_ones_1x64", [1, 64], BF16)

    def load_consts(self):
        em = self.em
        for name in ["ident_bf", "ident_f32", "ones_1x128", "ones_128x1",
                     "ones_1x64"]:
            em.dma("sp", self.sb["k_" + name][:], self.dram["c_" + name],
                   writes=(f"k_{name}",), tag=f"ld_{name}")

    def cast_load(self, dst_ap, src_ap, writes, tag, deps=(), reads=()):
        return self.em.dma("pool", dst_ap, src_ap, reads=reads, writes=writes,
                           deps=deps, tag=tag)

    def cast_load_split(self, dst_ap, src_ap, ncols, writes, tag,
                        chunk=16384, sub=None):
        if sub is not None:
            ncols = sub[1]
        r = None
        for c0 in range(0, ncols, chunk):
            c1 = min(c0 + chunk, ncols)
            r = self.em.dma("pool", dst_ap[:, c0:c1], src_ap[:, c0:c1],
                            writes=writes, tag=f"{tag}_{c0}")
        return r

    def emit_ln(self, pref, stat_sx, stat_sq, stat_tokens, SM, bcA, bcB,
                bufs, n_d=D):
        em = self.em
        btok, (s_srow, s_st, s_ab, s_t8, s_arow, s_brow) = bufs
        nc8 = 2 * SM // 128
        nc4 = nc8 // 2
        def sevfn(eng):
            eng.copy(s_srow[:, 0:SM], stat_sx)
            return eng.copy(s_srow[:, SM:2 * SM], stat_sq)
        em.step("act", sevfn, reads=tuple(stat_tokens),
                writes=(btok + "_srow",), tag=pref + "_sev")
        em.dma("sp", s_st[:, 0:nc4],
               s_srow[:, 0:SM].rearrange("o (p c) -> o p c", c=nc4),
               reads=(btok + "_srow",), writes=(btok + "_st",),
               tag=pref + "_gst1")
        em.dma("sp", s_st[:, nc4:nc8],
               s_srow[:, SM:2 * SM].rearrange("o (p c) -> o p c", c=nc4),
               reads=(btok + "_srow",), writes=(btok + "_st",),
               tag=pref + "_gst2")
        def stfn(eng):
            st, ab, t8 = s_st, s_ab, s_t8
            eng.tensor_scalar_mul(t8[:, 0:nc4], st[:, 0:nc4], 1.0 / n_d)
            eng.tensor_scalar_mul(t8[:, nc4:nc8], st[:, nc4:nc8], 1.0 / n_d)
            eng.drain()
            eng.tensor_mul(ab[:, 0:nc4], t8[:, 0:nc4], t8[:, 0:nc4])
            eng.drain()
            eng.tensor_sub(ab[:, nc4:nc8], t8[:, nc4:nc8], ab[:, 0:nc4])
            eng.drain()
            return eng.tensor_scalar_add(ab[:, nc4:nc8], ab[:, nc4:nc8], 1e-5)
        em.step("dve", stfn, reads=(btok + "_st",),
                writes=(btok + "_ab", btok + "_t8"), tag=pref + "_stat",
                small=True)
        def rstdfn(eng):
            eng.activation(s_ab[:, nc4:nc8], s_ab[:, nc4:nc8], AF.Ln)
            eng.drain()
            return eng.activation(s_ab[:, 0:nc4], s_ab[:, nc4:nc8], AF.Exp,
                                  scale=-0.5)
        em.step("act", rstdfn, reads=(btok + "_ab",), writes=(btok + "_ab",),
                tag=pref + "_rstd", small=True)
        em.step("dve", lambda eng: eng.tensor_mul(
            s_ab[:, nc4:nc8], s_t8[:, 0:nc4], s_ab[:, 0:nc4]),
            reads=(btok + "_ab", btok + "_t8"), writes=(btok + "_ab",),
            tag=pref + "_b", small=True)
        em.dma("pool", s_arow[:, 0:SM].rearrange("o (p c) -> o p c", c=nc4),
               s_ab[:, 0:nc4], reads=(btok + "_ab",),
               writes=(btok + "_arow",), tag=pref + "_sca")
        em.dma("pool", s_brow[:, 0:SM].rearrange("o (p c) -> o p c", c=nc4),
               s_ab[:, nc4:nc8], reads=(btok + "_ab",),
               writes=(btok + "_brow",), tag=pref + "_scb")
        bcA_ap, bcA_bank, bcA_start = bcA
        bcB_ap, bcB_bank, bcB_start = bcB
        def bcfn(eng):
            eng.matmul(bcA_ap, self.sb["k_ones_1x128"][:],
                       s_arow[:, 0:SM], start=bcA_start, stop=True,
                       skip_group_check=True)
            return eng.matmul(bcB_ap, self.sb["k_ones_1x128"][:],
                              s_brow[:, 0:SM], start=bcB_start, stop=True,
                              skip_group_check=True)
        em.step("pe", bcfn,
                reads=(btok + "_arow", btok + "_brow", "k_ones_1x128"),
                writes=(bcA_bank, bcB_bank), tag=pref + "_bc")

    def ln_bufs(self, pref, SMmax):
        sbn = self._sbuf
        nc8 = 2 * SMmax // 128
        return (pref, (sbn(pref + "_srow", [1, 2 * SMmax], F32),
                sbn(pref + "_st", [128, nc8], F32),
                sbn(pref + "_ab", [128, nc8], F32),
                sbn(pref + "_t8", [128, nc8], F32),
                sbn(pref + "_arow", [1, SMmax], BF16),
                sbn(pref + "_brow", [1, SMmax], BF16)))

    def phase_m1(self):
        em, dr = self.em, self.dram
        sbn = self._sbuf
        SM = 512
        NS = T // SM
        self.push_scope()
        s_xin = [sbn(f"m1_xin{p}", [128, 4 * 1024], F32) for p in range(2)]
        s_x0 = [sbn(f"m1_x0{p}", [128, FC * SM], F32) for p in range(2)]
        s_sqt = [sbn(f"m1_sqt{p}", [128, SM], BF16) for p in range(2)]
        s_xbt = [sbn(f"m1_xbt{p}", [128, SM], BF16) for p in range(2)]
        lnb = self.ln_bufs("m1ln", SM)
        s_xn = [sbn(f"m1_xn{p}", [128, FC * SM], BF16) for p in range(2)]
        s_nf = sbn("m1_nf", [128, SM], F32)
        s_xc = [sbn(f"m1_xc{p}", [128, IC * (SM + 3)], BF16) for p in range(2)]
        s_ut = [sbn(f"m1_ut{p}", [128, SM], BF16) for p in range(2)]
        s_zt = [sbn(f"m1_zt{p}", [128, SM], BF16) for p in range(2)]
        s_cv = [sbn(f"m1_cv{p}", [128, SM], BF16) for p in range(2)]
        s_win = sbn("m1_win", [128, 256 * 128], BF16)
        s_convw = sbn("m1_convw", [128, IC * 4], F32)
        self.cast_load_split(s_win, dr["w_in"], 256 * 128,
                             writes=("m1_win",), tag="ld_w_in")
        em.dma("sp", s_convw[:], dr["conv_w"], writes=("m1_convw",),
               tag="ld_convw")

        hx0 = dr["hx0"].rearrange("(fc p) t -> p fc t", p=128)
        hu = dr["hu"].rearrange("(ic p) t -> p ic t", p=128)
        hsz = dr["hsz"].rearrange("(ic p) t -> p ic t", p=128)

        def stripe(ms):
            q, si = divmod(ms, L // SM)
            g0 = q * L + si * SM
            p = ms % 2
            tok = lambda n: f"{n}{p}"
            em.dma("sp",
                   s_xin[p][:].rearrange("p (tb f) -> p tb f", tb=4),
                   dr["x_in"][g0:g0 + SM, :].rearrange(
                       "(tb p) f -> p tb f", p=128),
                   writes=(tok("m1_xin"),), tag=f"m1_ld{ms}")
            for fc in range(FC):
                b = f"pb{fc % 2}"
                def tfn(eng, p_=p, fc_=fc, b_=b):
                    r = None
                    for tb in range(4):
                        r = eng.transpose(
                            self.ps[b_][:, tb * 128:(tb + 1) * 128],
                            s_xin[p_][:, tb * 1024 + fc_ * 128:
                                      tb * 1024 + (fc_ + 1) * 128],
                            self.sb["k_ident_f32"][:])
                    return r
                em.step("pe", tfn, reads=(tok("m1_xin"), "k_ident_f32"),
                        writes=(b,), tag=f"m1_tr{ms}_{fc}")
                em.step("act", lambda eng, p_=p, fc_=fc, b_=b: eng.copy(
                    s_x0[p_][:, fc_ * SM:(fc_ + 1) * SM], self.ps[b_][:]),
                    reads=(b,), writes=(tok("m1_x0"),), tag=f"m1_ev{ms}_{fc}")
                pf = fc % 2
                em.step("dve", lambda eng, p_=p, fc_=fc, pf_=pf:
                        eng.tensor_mul(
                            s_sqt[pf_][:],
                            s_x0[p_][:, fc_ * SM:(fc_ + 1) * SM],
                            s_x0[p_][:, fc_ * SM:(fc_ + 1) * SM]),
                        reads=(tok("m1_x0"),), writes=(f"m1_sqt{pf}",),
                        tag=f"m1_sq{ms}_{fc}")
                em.step("dve", lambda eng, p_=p, fc_=fc, pf_=pf:
                        eng.tensor_copy(
                            s_xbt[pf_][:],
                            s_x0[p_][:, fc_ * SM:(fc_ + 1) * SM]),
                        reads=(tok("m1_x0"),), writes=(f"m1_xbt{pf}",),
                        tag=f"m1_xb{ms}_{fc}")
                em.step("pe", lambda eng, fc_=fc, pf_=pf: eng.matmul(
                    self.ps["pb2"][0:1, :], self.sb["k_ones_128x1"][:],
                    s_xbt[pf_][:], start=(fc_ == 0), stop=(fc_ == FC - 1),
                    skip_group_check=True),
                    reads=(f"m1_xbt{pf}", "k_ones_128x1"), writes=("pb2",),
                    tag=f"m1_msx{ms}_{fc}")
                em.step("pe", lambda eng, fc_=fc, pf_=pf: eng.matmul(
                    self.ps["pb3"][0:1, :], self.sb["k_ones_128x1"][:],
                    s_sqt[pf_][:], start=(fc_ == 0), stop=(fc_ == FC - 1),
                    skip_group_check=True),
                    reads=(f"m1_sqt{pf}", "k_ones_128x1"), writes=("pb3",),
                    tag=f"m1_msq{ms}_{fc}")
            self.emit_ln(f"m1ln{ms}", self.ps["pb2"][0:1, :],
                         self.ps["pb3"][0:1, :], ("pb2", "pb3"), SM,
                         (self.ps["pb4"][:], "pb4", True),
                         (self.ps["pb5"][:], "pb5", True), lnb)
            for fc in range(FC):
                em.step("dve", lambda eng, p_=p, fc_=fc: eng.tensor_mul(
                    s_nf[:], s_x0[p_][:, fc_ * SM:(fc_ + 1) * SM],
                    self.ps["pb4"][:]),
                    reads=(tok("m1_x0"), "pb4"), writes=("m1_nf",),
                    tag=f"m1_n1{ms}_{fc}")
                em.step("dve", lambda eng, p_=p, fc_=fc: eng.tensor_sub(
                    s_xn[p_][:, fc_ * SM:(fc_ + 1) * SM], s_nf[:],
                    self.ps["pb5"][:]),
                    reads=("m1_nf", "pb5"), writes=(tok("m1_xn"),),
                    tag=f"m1_n2{ms}_{fc}")
            if DBG:
                lnbufs = lnb[1]
                em.dma("sp", dr["dbg_st"].rearrange(
                    "(s p) c -> p s c", p=128)[:, ms], lnbufs[1][:],
                    reads=("m1ln_st",), writes=(f"dbgst{ms}",),
                    tag=f"dbg_st{ms}")
                em.dma("sp", dr["dbg_ab"].rearrange(
                    "(s p) c -> p s c", p=128)[:, ms], lnbufs[2][:],
                    reads=("m1ln_ab",), writes=(f"dbgab{ms}",),
                    tag=f"dbg_ab{ms}")
                em.dma("sp", dr["dbg_row"][ms:ms + 1, 0:1024],
                       lnbufs[0][:], reads=("m1ln_srow",),
                       writes=(f"dbgrow{ms}",), tag=f"dbg_row{ms}")
                if ms == 0:
                    em.dma("sp", dr["dbg_xn"], s_xn[p][:],
                           reads=(tok("m1_xn"),), writes=("dbgxn",),
                           tag="dbg_xn")
            em.dma("sp", hx0[:, :, g0:g0 + SM],
                   s_x0[p][:].rearrange("p (fc t) -> p fc t", fc=FC),
                   reads=(tok("m1_x0"),), writes=(f"hx0_{ms}",),
                   tag=f"m1_ox0{ms}")
            for oc in range(32):
                b = f"pb{6 + (oc % 2)}"
                def wfn(eng, p_=p, oc_=oc, b_=b):
                    r = None
                    for kc in range(FC):
                        r = eng.matmul(
                            self.ps[b_][:],
                            self.sb["m1_win"][:, (kc * 32 + oc_) * 128:
                                              (kc * 32 + oc_ + 1) * 128],
                            s_xn[p_][:, kc * SM:(kc + 1) * SM],
                            start=(kc == 0), stop=(kc == FC - 1))
                    return r
                em.step("pe", wfn, reads=(tok("m1_xn"), "m1_win"),
                        writes=(b,), tag=f"m1_w{ms}_{oc}")
                if oc < IC:
                    em.step("act", lambda eng, p_=p, oc_=oc, b_=b: eng.copy(
                        s_xc[p_][:, oc_ * (SM + 3) + 3:
                                 oc_ * (SM + 3) + 3 + SM], self.ps[b_][:]),
                        reads=(b,), writes=(tok("m1_xc"),),
                        tag=f"m1_exc{ms}_{oc}")
                else:
                    zt = oc % 2
                    em.step("act", lambda eng, zt_=zt, b_=b: eng.activation(
                        s_zt[zt_][:], self.ps[b_][:], AF.Silu),
                        reads=(b,), writes=(f"m1_zt{zt}",),
                        tag=f"m1_esz{ms}_{oc}")
                    em.dma("sp", hsz[:, oc - IC, g0:g0 + SM], s_zt[zt][:],
                           reads=(f"m1_zt{zt}",), writes=(f"hsz_{ms}_{oc}",),
                           tag=f"m1_osz{ms}_{oc}")
            if si == 0:
                em.step("dve", lambda eng, p_=p: eng.memset(
                    s_xc[p_][:].rearrange(
                        "p (ic c) -> p ic c", ic=IC)[:, :, 0:3], 0.0),
                    writes=(tok("m1_xc"),), tag=f"m1_cz{ms}", small=True)
            else:
                def cfn(eng, p_=p):
                    src = s_xc[1 - p_][:].rearrange(
                        "p (ic c) -> p ic c", ic=IC)[:, :, SM:SM + 3]
                    dst = s_xc[p_][:].rearrange(
                        "p (ic c) -> p ic c", ic=IC)[:, :, 0:3]
                    return eng.tensor_copy(dst, src)
                em.step("dve", cfn, reads=(f"m1_xc{1 - p}",),
                        writes=(tok("m1_xc"),), tag=f"m1_cc{ms}", small=True)
            for ic in range(IC):
                pc = ic % 2
                def cvfn(eng, p_=p, ic_=ic, pc_=pc):
                    base = ic_ * (SM + 3)
                    xc_ = s_xc[p_]
                    cw = self.sb["m1_convw"]
                    eng.tensor_scalar_mul(
                        s_cv[pc_][:], xc_[:, base:base + SM],
                        cw[:, ic_ * 4:ic_ * 4 + 1])
                    r = None
                    for dd in range(1, 4):
                        r = eng.scalar_tensor_tensor(
                            s_cv[pc_][:], xc_[:, base + dd:base + dd + SM],
                            cw[:, ic_ * 4 + dd:ic_ * 4 + dd + 1],
                            s_cv[pc_][:], ALU.mult, ALU.add)
                    return r
                em.step("dve", cvfn, reads=(tok("m1_xc"), "m1_convw"),
                        writes=(f"m1_cv{pc}",), tag=f"m1_cv{ms}_{ic}")
                em.step("act", lambda eng, ic_=ic, pc_=pc, p_=p:
                        eng.activation(s_ut[pc_][:], s_cv[pc_][:], AF.Silu),
                        reads=(f"m1_cv{pc}",), writes=(f"m1_ut{pc}",),
                        tag=f"m1_su{ms}_{ic}")
                em.dma("sp", hu[:, ic, g0:g0 + SM], s_ut[pc][:],
                       reads=(f"m1_ut{pc}",), writes=(f"hu_{ms}_{ic}",),
                       tag=f"m1_ou{ms}_{ic}")
        for k0 in range(0, NS, 2):
            self.emit_pair(stripe, k0)
        self.pop_scope()

    def phase_m2(self):
        em, dr = self.em, self.dram
        sbn = self._sbuf
        SM = 256
        NS = T // SM
        self.push_scope()
        s_u = [sbn(f"m2_u{p}", [128, IC * SM], BF16) for p in range(2)]
        s_sz = [sbn(f"m2_sz{p}", [128, IC * SM], BF16) for p in range(2)]
        s_x0 = [sbn(f"m2_x0{p}", [128, FC * SM], F32) for p in range(2)]
        s_xp = [sbn(f"m2_xp{p}", [16, 2 * SM], BF16) for p in range(2)]
        s_e = [sbn(f"m2_e{p}", [128, SM], F32) for p in range(2)]
        s_sp = [sbn(f"m2_sp{p}", [128, SM], F32) for p in range(2)]
        s_A = [sbn(f"m2_A{p}", [128, SM], F32) for p in range(2)]
        s_ub = [sbn(f"m2_ub{p}", [128, SM], BF16) for p in range(2)]
        s_ys = sbn("m2_ys", [128, SM], F32)
        s_yt = sbn("m2_yt", [128, SM], F32)
        s_y = [sbn(f"m2_y{p}", [128, IC * SM], BF16) for p in range(2)]
        s_h = sbn("m2_h", [128, IC], F32)
        s_m = [sbn(f"m2_m{p}", [128, FC * SM], F32) for p in range(2)]
        s_mq = [sbn(f"m2_mq{p}", [128, SM], BF16) for p in range(2)]
        s_x1 = [sbn(f"m2_x1{p}", [128, FC * SM], F32) for p in range(2)]
        s_x2 = [sbn(f"m2_x2{p}", [128, FC * SM], F32) for p in range(2)]
        s_nf = sbn("m2_nf", [128, SM], F32)
        lnb2 = self.ln_bufs("m2ln2", SM)
        lnb3 = self.ln_bufs("m2ln3", SM)
        s_wx = sbn("m2_wx", [128, IC * 32], BF16)
        s_wdt = sbn("m2_wdt", [16, IC * 128], BF16)
        s_wbp = sbn("m2_wbp", [16, IC * 128], BF16)
        s_bdt = sbn("m2_bdt", [128, IC], F32)
        s_dp = sbn("m2_dp", [128, IC], F32)
        s_wout = sbn("m2_wout", [128, IC * FC * 128], BF16)
        s_wsum = sbn("m2_wsum", [128, IC], BF16)
        s_wsf = sbn("m2_wsf", [128, IC], F32)
        self.cast_load(s_wx[:], dr["w_x"], writes=("m2_wx",), tag="ld_w_x")
        self.cast_load(s_wdt[:], dr["w_dt"], writes=("m2_wdt",),
                       tag="ld_w_dt")
        self.cast_load(s_wbp[:], dr["w_bp"], writes=("m2_wbp",),
                       tag="ld_w_bp")
        em.dma("sp", s_bdt[:], dr["b_dt"], writes=("m2_bdt",), tag="ld_bdt")
        em.dma("sp", s_dp[:], dr["d_param"], writes=("m2_dp",), tag="ld_dp")
        self.cast_load(s_wout[:], dr["w_out"], writes=("m2_wout",),
                       tag="ld_w_out")
        def wsfn(eng):
            for kc in range(IC):
                eng.tensor_reduce(
                    self.sb["m2_wsf"][:, kc:kc + 1],
                    self.sb["m2_wout"][:, kc * FC * 128:(kc + 1) * FC * 128],
                    AX.X, ALU.add)
            eng.drain()
            return eng.tensor_copy(self.sb["m2_wsum"][:], self.sb["m2_wsf"][:])
        em.step("dve", wsfn, reads=("m2_wout",), writes=("m2_wsum",),
                tag="m2_wsum", small=True)

        hu = dr["hu"].rearrange("(ic p) t -> p ic t", p=128)
        hsz = dr["hsz"].rearrange("(ic p) t -> p ic t", p=128)
        hx0 = dr["hx0"].rearrange("(fc p) t -> p fc t", p=128)
        hx1 = dr["hx1"].rearrange("(fc p) t -> p fc t", p=128)
        hx2a = dr["hx2a"].rearrange("(fc p) t -> p fc t", p=128)

        def stripe(ms):
            q, si = divmod(ms, L // SM)
            g0 = q * L + si * SM
            p = ms % 2
            tok = lambda n: f"{n}{p}"
            em.dma("sp", s_u[p][:].rearrange("p (ic t) -> p ic t", ic=IC),
                   hu[:, :, g0:g0 + SM], reads=(f"hu_fin",),
                   writes=(tok("m2_u"),), tag=f"m2_lu{ms}")
            em.dma("sp", s_sz[p][:].rearrange("p (ic t) -> p ic t", ic=IC),
                   hsz[:, :, g0:g0 + SM], reads=(f"hsz_fin",),
                   writes=(tok("m2_sz"),), tag=f"m2_lsz{ms}")
            em.dma("sp", s_x0[p][:].rearrange("p (fc t) -> p fc t", fc=FC),
                   hx0[:, :, g0:g0 + SM], reads=(f"hx0_fin",),
                   writes=(tok("m2_x0"),), tag=f"m2_lx0{ms}")
            def xpfn(eng, p_=p):
                r = None
                for kc in range(IC):
                    eng.matmul(
                        self.ps["pb0"][0:16, 0:SM],
                        self.sb["m2_wx"][:, kc * 32:kc * 32 + 16],
                        s_u[p_][:, kc * SM:(kc + 1) * SM],
                        start=(kc == 0), stop=(kc == IC - 1),
                        skip_group_check=True)
                    r = eng.matmul(
                        self.ps["pb0"][0:16, SM:2 * SM],
                        self.sb["m2_wx"][:, kc * 32 + 16:kc * 32 + 32],
                        s_u[p_][:, kc * SM:(kc + 1) * SM],
                        start=False, stop=(kc == IC - 1),
                        skip_group_check=True)
                return r
            em.step("pe", xpfn, reads=(tok("m2_u"), "m2_wx"), writes=("pb0",),
                    tag=f"m2_xp{ms}")
            em.step("act", lambda eng, p_=p: eng.copy(
                s_xp[p_][:], self.ps["pb0"][0:16, 0:2 * SM]),
                reads=("pb0",), writes=(tok("m2_xp"),), tag=f"m2_exp{ms}")
            if si == 0:
                em.step("dve", lambda eng: eng.memset(s_h[:], 0.0),
                        writes=("m2_h",), tag=f"m2_h0{ms}", small=True)
            for ic in range(IC):
                bdt = f"pb{1 + (ic % 2)}"
                bbp = f"pb{3 + (ic % 2)}"
                pc = ic % 2
                em.step("pe", lambda eng, p_=p, ic_=ic, b_=bdt: eng.matmul(
                    self.ps[b_][:, 0:SM],
                    self.sb["m2_wdt"][:, ic_ * 128:(ic_ + 1) * 128],
                    s_xp[p_][:, 0:SM], start=True, stop=True),
                    reads=(tok("m2_xp"), "m2_wdt"), writes=(bdt,),
                    tag=f"m2_dt{ms}_{ic}")
                em.step("pe", lambda eng, p_=p, ic_=ic, b_=bbp: eng.matmul(
                    self.ps[b_][:, 0:SM],
                    self.sb["m2_wbp"][:, ic_ * 128:(ic_ + 1) * 128],
                    s_xp[p_][:, SM:2 * SM], start=True, stop=True),
                    reads=(tok("m2_xp"), "m2_wbp"), writes=(bbp,),
                    tag=f"m2_bp{ms}_{ic}")
                em.step("act", lambda eng, ic_=ic, pc_=pc, b_=bdt:
                        eng.activation(s_e[pc_][:], self.ps[b_][:, 0:SM],
                                       AF.Exp,
                                       bias=self.sb["m2_bdt"][:, ic_:ic_ + 1]),
                        reads=(bdt, "m2_bdt"), writes=(f"m2_e{pc}",),
                        tag=f"m2_sp1{ms}_{ic}")
                em.step("act", lambda eng, pc_=pc: eng.activation(
                    s_sp[pc_][:], s_e[pc_][:], AF.Ln, bias=1.0),
                    reads=(f"m2_e{pc}",), writes=(f"m2_sp{pc}",),
                    tag=f"m2_sp2{ms}_{ic}")
                def clipfn(eng, pc_=pc):
                    eng.tensor_scalar(s_A[pc_][:], s_sp[pc_][:], 1e-4, 1.0,
                                      ALU.max, ALU.min)
                    return eng.tensor_scalar(s_A[pc_][:], s_A[pc_][:], -1.0,
                                             1.0, ALU.mult, ALU.add)
                em.step("dve", clipfn, reads=(f"m2_sp{pc}",),
                        writes=(f"m2_A{pc}",), tag=f"m2_cl{ms}_{ic}")
                em.step("dve", lambda eng, p_=p, ic_=ic, pc_=pc, b_=bbp:
                        eng.tensor_mul(
                            s_ub[pc_][:], self.ps[b_][:, 0:SM],
                            s_u[p_][:, ic_ * SM:(ic_ + 1) * SM]),
                        reads=(bbp, tok("m2_u")), writes=(f"m2_ub{pc}",),
                        tag=f"m2_ub{ms}_{ic}")
                def scfn(eng, ic_=ic, pc_=pc):
                    eng.tensor_tensor_scan(
                        s_ys[:], s_A[pc_][:], s_ub[pc_][:],
                        s_h[:, ic_:ic_ + 1], ALU.mult, ALU.add)
                    eng.drain()
                    return eng.tensor_copy(s_h[:, ic_:ic_ + 1],
                                           s_ys[:, SM - 1:SM])
                em.step("dve", scfn,
                        reads=(f"m2_A{pc}", f"m2_ub{pc}", "m2_h"),
                        writes=("m2_ys", "m2_h"), tag=f"m2_sc{ms}_{ic}",
                        small=True)
                def yfn(eng, p_=p, ic_=ic):
                    eng.scalar_tensor_tensor(
                        s_yt[:], s_u[p_][:, ic_ * SM:(ic_ + 1) * SM],
                        self.sb["m2_dp"][:, ic_:ic_ + 1], s_ys[:],
                        ALU.mult, ALU.add)
                    return eng.tensor_mul(
                        s_y[p_][:, ic_ * SM:(ic_ + 1) * SM], s_yt[:],
                        s_sz[p_][:, ic_ * SM:(ic_ + 1) * SM])
                em.step("dve", yfn,
                        reads=("m2_ys", tok("m2_u"), tok("m2_sz"), "m2_dp"),
                        writes=(tok("m2_y"), "m2_yt"), tag=f"m2_y{ms}_{ic}")
            def msfn(eng, p_=p):
                r = None
                for kc in range(IC):
                    r = eng.matmul(
                        self.ps["pb7"][0:1, 0:SM],
                        self.sb["m2_wsum"][:, kc:kc + 1],
                        s_y[p_][:, kc * SM:(kc + 1) * SM],
                        start=(kc == 0), stop=(kc == IC - 1),
                        skip_group_check=True)
                return r
            em.step("pe", msfn, reads=(tok("m2_y"), "m2_wsum"),
                    writes=("pb7",), tag=f"m2_msum{ms}")
            for oc in range(FC):
                b = f"pb{5 + (oc % 2)}"
                def wofn(eng, p_=p, oc_=oc, b_=b):
                    r = None
                    for kc in range(IC):
                        r = eng.matmul(
                            self.ps[b_][:, 0:SM],
                            self.sb["m2_wout"][:, (kc * FC + oc_) * 128:
                                               (kc * FC + oc_ + 1) * 128],
                            s_y[p_][:, kc * SM:(kc + 1) * SM],
                            start=(kc == 0), stop=(kc == IC - 1))
                    return r
                em.step("pe", wofn, reads=(tok("m2_y"), "m2_wout"),
                        writes=(b,), tag=f"m2_wo{ms}_{oc}")
                em.step("act", lambda eng, p_=p, oc_=oc, b_=b: eng.copy(
                    s_m[p_][:, oc_ * SM:(oc_ + 1) * SM],
                    self.ps[b_][:, 0:SM]),
                    reads=(b,), writes=(tok("m2_m"),), tag=f"m2_em{ms}_{oc}")
                pc = oc % 2
                em.step("dve", lambda eng, p_=p, oc_=oc, pc_=pc:
                        eng.tensor_mul(
                            s_mq[pc_][:],
                            s_m[p_][:, oc_ * SM:(oc_ + 1) * SM],
                            s_m[p_][:, oc_ * SM:(oc_ + 1) * SM]),
                        reads=(tok("m2_m"),), writes=(f"m2_mq{pc}",),
                        tag=f"m2_mq{ms}_{oc}")
                em.step("pe", lambda eng, oc_=oc, pc_=pc: eng.matmul(
                    self.ps["pb7"][0:1, SM:2 * SM],
                    self.sb["k_ones_128x1"][:], s_mq[pc_][:],
                    start=False, stop=(oc_ == FC - 1),
                    skip_group_check=True),
                    reads=(f"m2_mq{pc}", "k_ones_128x1"), writes=("pb7",),
                    tag=f"m2_mq2{ms}_{oc}")
            self.emit_ln(f"m2ln2_{ms}", self.ps["pb7"][0:1, 0:SM],
                         self.ps["pb7"][0:1, SM:2 * SM], ("pb7",), SM,
                         (self.ps["pb1"][:, 0:SM], "pb1", True),
                         (self.ps["pb2"][:, 0:SM], "pb2", True), lnb2)
            for fc in range(FC):
                def x1fn(eng, p_=p, fc_=fc):
                    eng.tensor_mul(s_nf[:],
                                   s_m[p_][:, fc_ * SM:(fc_ + 1) * SM],
                                   self.ps["pb1"][:, 0:SM])
                    eng.tensor_sub(s_nf[:], s_nf[:], self.ps["pb2"][:, 0:SM])
                    return eng.tensor_add(
                        s_x1[p_][:, fc_ * SM:(fc_ + 1) * SM], s_nf[:],
                        s_x0[p_][:, fc_ * SM:(fc_ + 1) * SM])
                em.step("dve", x1fn,
                        reads=(tok("m2_m"), "pb1", "pb2", tok("m2_x0")),
                        writes=(tok("m2_x1"), "m2_nf"), tag=f"m2_x1{ms}_{fc}")
                pc = fc % 2
                em.step("dve", lambda eng, p_=p, fc_=fc, pc_=pc:
                        eng.tensor_copy(
                            s_ub[pc_][:],
                            s_x1[p_][:, fc_ * SM:(fc_ + 1) * SM]),
                        reads=(tok("m2_x1"),), writes=(f"m2_ub{pc}",),
                        tag=f"m2_x1b{ms}_{fc}")
                em.step("pe", lambda eng, fc_=fc, pc_=pc: eng.matmul(
                    self.ps["pb3"][0:1, 0:SM],
                    self.sb["k_ones_128x1"][:], s_ub[pc_][:],
                    start=(fc_ == 0), stop=(fc_ == FC - 1),
                    skip_group_check=True),
                    reads=(f"m2_ub{pc}", "k_ones_128x1"), writes=("pb3",),
                    tag=f"m2_x1s{ms}_{fc}")
                em.step("dve", lambda eng, p_=p, fc_=fc, pc_=pc:
                        eng.tensor_mul(
                            s_mq[pc_][:],
                            s_x1[p_][:, fc_ * SM:(fc_ + 1) * SM],
                            s_x1[p_][:, fc_ * SM:(fc_ + 1) * SM]),
                        reads=(tok("m2_x1"),), writes=(f"m2_mq{pc}",),
                        tag=f"m2_x1q{ms}_{fc}")
                em.step("pe", lambda eng, fc_=fc, pc_=pc: eng.matmul(
                    self.ps["pb3"][0:1, SM:2 * SM],
                    self.sb["k_ones_128x1"][:], s_mq[pc_][:],
                    start=False, stop=(fc_ == FC - 1),
                    skip_group_check=True),
                    reads=(f"m2_mq{pc}", "k_ones_128x1"), writes=("pb3",),
                    tag=f"m2_x1q2{ms}_{fc}")
            em.dma("sp", hx1[:, :, g0:g0 + SM],
                   s_x1[p][:].rearrange("p (fc t) -> p fc t", fc=FC),
                   reads=(tok("m2_x1"),), writes=(f"hx1_{ms}",),
                   tag=f"m2_ox1{ms}")
            self.emit_ln(f"m2ln3_{ms}", self.ps["pb3"][0:1, 0:SM],
                         self.ps["pb3"][0:1, SM:2 * SM], ("pb3",), SM,
                         (self.ps["pb4"][:, 0:SM], "pb4", True),
                         (self.ps["pb5"][:, 0:SM], "pb5", True), lnb3)
            for fc in range(FC):
                def x2fn(eng, p_=p, fc_=fc):
                    eng.tensor_mul(s_nf[:],
                                   s_x1[p_][:, fc_ * SM:(fc_ + 1) * SM],
                                   self.ps["pb4"][:, 0:SM])
                    return eng.tensor_sub(
                        s_x2[p_][:, fc_ * SM:(fc_ + 1) * SM], s_nf[:],
                        self.ps["pb5"][:, 0:SM])
                em.step("dve", x2fn,
                        reads=(tok("m2_x1"), "pb4", "pb5"),
                        writes=(tok("m2_x2"), "m2_nf"), tag=f"m2_x2{ms}_{fc}")
            em.dma("sp", hx2a[:, :, g0:g0 + SM],
                   s_x2[p][:].rearrange("p (fc t) -> p fc t", fc=FC),
                   reads=(tok("m2_x2"),), writes=(f"hx2_{ms}",),
                   tag=f"m2_ox2{ms}")
        for k0 in range(0, NS, 2):
            self.emit_pair(stripe, k0)
        self.pop_scope()


    def section_begin(self):
        em, dr = self.em, self.dram
        sbn = self._sbuf
        self.push_scope()
        sbn("sec_wqkv", [128, 192 * 128], BF16)
        sbn("sec_wo", [128, 64 * 128], BF16)
        sbn("sec_Rq", [64, 64], BF16)
        sbn("sec_Rk", [64, 64], BF16)
        sbn("sec_rot", [64, 64], BF16)
        sbn("sec_qs2", [64, 6], F32)
        sbn("sec_ks2", [64, 6], F32)
        sbn("sec_selsum", [64, 256], BF16)
        sbn("sec_selbc", [16, 1024], BF16)
        sbn("sec_maskq", [128, 3 * 384], BF16)
        sbn("sec_maskt", [128, 4 * 256], BF16)
        for name, src in [("sec_rot", "c_rot"), ("sec_selsum", "c_selsum"),
                          ("sec_selbc", "c_selbc")]:
            em.dma("sp", self.sb[name][:], dr[src], writes=(name,),
                   tag="ld_" + name)
        em.dma("sp", self.sb["sec_qs2"][:], dr["qs2"], writes=("sec_qs2",),
               tag="ld_qs2")
        em.dma("sp", self.sb["sec_ks2"][:], dr["ks2"], writes=("sec_ks2",),
               tag="ld_ks2")
        em.dma("sp", self.sb["sec_maskq"][:].rearrange(
            "p (v n) -> p v n", v=3),
            dr["c_maskq"].rearrange("(v p) n -> p v n", p=128),
            writes=("sec_maskq",), tag="ld_maskq")
        em.dma("sp", self.sb["sec_maskt"][:].rearrange(
            "p (v n) -> p v n", v=4),
            dr["c_maskt"].rearrange("(v p) n -> p v n", p=128),
            writes=("sec_maskt",), tag="ld_maskt")
        self.x2cur = "hx2a"
        self.x2nxt = "hx2b"

    def section_end(self):
        self.pop_scope()

    def load_layer_weights(self, li):
        em, dr = self.em, self.dram
        self.cast_load_split(self.sb["sec_wqkv"],
                             dr["wqkv"][li * 128:(li + 1) * 128, :],
                             192 * 128, writes=("sec_wqkv",),
                             tag=f"ld_wqkv{li}")
        self.cast_load_split(self.sb["sec_wo"],
                             dr["wo"][li * 128:(li + 1) * 128, :],
                             64 * 128, writes=("sec_wo",), tag=f"ld_wo{li}")
        em.step("dve", lambda eng, li_=li: eng.tensor_scalar_mul(
            self.sb["sec_Rq"][:], self.sb["sec_rot"][:],
            self.sb["sec_qs2"][:, li_:li_ + 1]),
            reads=("sec_rot", "sec_qs2"), writes=("sec_Rq",),
            tag=f"bld_Rq{li}")
        em.step("dve", lambda eng, li_=li: eng.tensor_scalar_mul(
            self.sb["sec_Rk"][:], self.sb["sec_rot"][:],
            self.sb["sec_ks2"][:, li_:li_ + 1]),
            reads=("sec_rot", "sec_ks2"), writes=("sec_Rk",),
            tag=f"bld_Rk{li}")

    def phase_attn(self, li):
        em, dr = self.em, self.dram
        sbn = self._sbuf
        FS = 512
        WS = 256
        NF = T // FS
        NW = T // WS
        WPS = L // WS
        P = f"a{li}"
        self.push_scope()
        if li == 0:
            self.load_layer_weights(0)
        s_xsh = [sbn(f"{P}_xsh{p}", [128, FC * FS], BF16) for p in range(2)]
        s_qaug = [sbn(f"{P}_qaug{p}", [66, NH * FS], BF16) for p in range(2)]
        s_kaug = [sbn(f"{P}_kaug{p}", [66, NH * 768], BF16) for p in range(2)]
        s_vt = sbn(f"{P}_vt", [128, 6 * 1024], BF16)
        s_sq = [sbn(f"{P}_sq{p}", [64, FS], BF16) for p in range(2)]
        s_tA = sbn(f"{P}_tA", [64, FS], BF16)
        s_tC = sbn(f"{P}_tC", [64, FS], BF16)
        s_sA = sbn(f"{P}_sA", [64, FS], BF16)
        s_sC = sbn(f"{P}_sC", [64, FS], BF16)
        s_nq = sbn(f"{P}_nq", [16, FS], F32)
        s_rn = sbn(f"{P}_rn", [16, FS], BF16)
        s_mcol = [sbn(f"{P}_mcol{w_}", [128, 32], F32) for w_ in range(2)]
        s_scol = [sbn(f"{P}_scol{w_}", [128, 32], F32) for w_ in range(2)]
        s_r12 = sbn(f"{P}_r12", [128, 64], F32)
        s_rT = sbn(f"{P}_rT", [64, 128], F32)
        s_et = [sbn(f"{P}_et{w_}", [128, 2 * 256], BF16) for w_ in range(2)]
        s_ao = [sbn(f"{P}_ao{w_}", [128, FC * WS], BF16) for w_ in range(2)]
        s_xr = [[sbn(f"{P}_xr{w_}_{j}", [128, WS], F32) for j in range(2)]
                for w_ in range(2)]
        s_xo = [sbn(f"{P}_xo{w_}", [128, WS], F32) for w_ in range(2)]
        hxc = dr[self.x2cur].rearrange("(fc p) t -> p fc t", p=128)
        hxn = dr[self.x2nxt].rearrange("(fc p) t -> p fc t", p=128)
        for p in range(2):
            em.step("dve", lambda eng, p_=p: eng.memset(
                s_kaug[p_][64:66, :], 1.0), writes=(f"{P}_kones{p}",),
                tag=f"{P}_kones{p}")

        def slot_of(kk, h, p):
            if kk == 0:
                return (s_qaug[p][0:64, h * FS:(h + 1) * FS],
                        f"{P}_q{p}_{h}")
            return (s_kaug[p][0:64, :].rearrange(
                "pp (hh t) -> pp hh t", hh=NH)[:, h, 256:768],
                f"{P}_k{p}_{h}")

        def front(ss):
            q, fi = divmod(ss, L // FS)
            g0 = q * L + fi * FS
            p = ss % 2
            xsh = s_xsh[p]
            xtok = f"{P}_xsh{p}"
            em.dma("pool", xsh[:].rearrange(
                "pp (fc t) -> pp fc t", fc=FC)[:, 0:4],
                hxc[:, 0:4, g0:g0 + FS], writes=(xtok,),
                tag=f"{P}_lx1_{ss}")
            if fi == 0:
                em.dma("pool", xsh[:].rearrange(
                    "pp (fc t) -> pp fc t", fc=FC)[:, 4:8, 1:FS],
                    hxc[:, 4:8, g0:g0 + FS - 1], writes=(xtok,),
                    tag=f"{P}_lx2_{ss}")
                em.step("dve", lambda eng, p_=p: eng.memset(
                    s_xsh[p_][:].rearrange(
                        "pp (fc t) -> pp fc t", fc=FC)[:, 4:8, 0:1], 0.0),
                    writes=(xtok,), tag=f"{P}_z0_{ss}", small=True)
            else:
                em.dma("pool", xsh[:].rearrange(
                    "pp (fc t) -> pp fc t", fc=FC)[:, 4:8],
                    hxc[:, 4:8, g0 - 1:g0 + FS - 1], writes=(xtok,),
                    tag=f"{P}_lx2_{ss}")
            kall = tuple(f"{P}_k{p}_{hh}" for hh in range(NH))
            if fi == 0:
                em.step("dve", lambda eng, p_=p: eng.memset(
                    s_kaug[p_][0:64, :].rearrange(
                        "pp (h t) -> pp h t", h=NH)[:, :, 0:256], 0.0),
                    reads=kall, writes=kall, tag=f"{P}_ktz_{ss}")
            else:
                kprev = tuple(f"{P}_k{1 - p}_{hh}" for hh in range(NH))
                def ktail(eng, p_=p):
                    src = s_kaug[1 - p_][0:64, :].rearrange(
                        "pp (h t) -> pp h t", h=NH)[:, :, 512:768]
                    dst = s_kaug[p_][0:64, :].rearrange(
                        "pp (h t) -> pp h t", h=NH)[:, :, 0:256]
                    return eng.tensor_copy(dst, src)
                em.step("dve", ktail, reads=kall + kprev, writes=kall,
                        tag=f"{P}_ktl_{ss}")
            tbase = fi * FS
            em.dma("sp", s_tA[:], dr["c_csq"][:, tbase:tbase + FS],
                   writes=(f"{P}_tA",), tag=f"{P}_ltA_{ss}")
            em.step("dve", lambda eng, li_=li: eng.tensor_scalar_mul(
                s_tA[:], s_tA[:],
                self.sb["sec_qs2"][:, li_:li_ + 1]),
                reads=(f"{P}_tA", "sec_qs2"), writes=(f"{P}_tA",),
                tag=f"{P}_btA_{ss}")
            em.dma("sp", s_tC[:], dr["c_csi"][:, tbase:tbase + FS],
                   writes=(f"{P}_tC",), tag=f"{P}_ltC_{ss}")
            em.step("dve", lambda eng, li_=li: eng.tensor_scalar_mul(
                s_tC[:], s_tC[:],
                self.sb["sec_ks2"][:, li_:li_ + 1]),
                reads=(f"{P}_tC", "sec_ks2"), writes=(f"{P}_tC",),
                tag=f"{P}_btC_{ss}")
            em.dma("sp", s_sA[:], dr["c_ssq"][:, tbase:tbase + FS],
                   writes=(f"{P}_sA",), tag=f"{P}_lsA_{ss}")
            em.dma("sp", s_sC[:], dr["c_ssi"][:, tbase:tbase + FS],
                   writes=(f"{P}_sC",), tag=f"{P}_lsC_{ss}")
            for kk in range(2):
                for h in range(NH):
                    colbase = (kk * 8 + h // 2) * 128 + (h % 2) * 64
                    b = f"pb{(kk * NH + h) % 2}"
                    def qmm(eng, p_=p, cb=colbase, b_=b):
                        r = None
                        for kc in range(FC):
                            r = eng.matmul(
                                self.ps[b_][0:64, :],
                                self.sb["sec_wqkv"][:, kc * 24 * 128 + cb:
                                                    kc * 24 * 128 + cb + 64],
                                s_xsh[p_][:, kc * FS:(kc + 1) * FS],
                                start=(kc == 0), stop=(kc == FC - 1))
                        return r
                    em.step("pe", qmm, reads=(xtok, "sec_wqkv"),
                            writes=(b,), tag=f"{P}_qkv{ss}_{kk}_{h}")
                    dsta, dtok = slot_of(kk, h, p)
                    em.step("act", lambda eng, b_=b, dst=dsta: eng.copy(
                        dst, self.ps[b_][0:64, :]),
                        reads=(b,), writes=(dtok, b),
                        tag=f"{P}_qe{ss}_{kk}_{h}")
            for kk in range(2):
                Rname = "sec_Rq" if kk == 0 else "sec_Rk"
                tA_b = s_tA if kk == 0 else s_tC
                tB_b = s_sA if kk == 0 else s_sC
                tAn = f"{P}_tA" if kk == 0 else f"{P}_tC"
                tBn = f"{P}_sA" if kk == 0 else f"{P}_sC"
                for h in range(NH):
                    t = h % 2
                    dsta, dtok = slot_of(kk, h, p)
                    em.step("dve", lambda eng, t_=t, src=dsta:
                            eng.tensor_mul(s_sq[t_][:], src, src),
                            reads=(dtok,), writes=(f"{P}_sq{t}",),
                            tag=f"{P}_qs{ss}_{kk}_{h}")
                    em.step("pe", lambda eng, t_=t, h_=h: eng.matmul(
                        self.ps["pb2"][0:16, :],
                        self.sb["sec_selsum"][:, h_ * 16:(h_ + 1) * 16],
                        s_sq[t_][:], start=(h_ == 0), stop=(h_ == NH - 1),
                        skip_group_check=True),
                        reads=(f"{P}_sq{t}", "sec_selsum"), writes=("pb2",),
                        tag=f"{P}_qn{ss}_{kk}_{h}")
                em.step("dve", lambda eng: eng.tensor_scalar_max(
                    s_nq[:], self.ps["pb2"][0:16, :], 1e-24),
                    reads=("pb2",), writes=(f"{P}_nq", "pb2"),
                    tag=f"{P}_rn1_{ss}_{kk}")
                def rnact(eng):
                    eng.activation(s_nq[:], s_nq[:], AF.Ln)
                    eng.drain()
                    return eng.activation(s_rn[:], s_nq[:], AF.Exp,
                                          scale=-0.5)
                em.step("act", rnact, reads=(f"{P}_nq",),
                        writes=(f"{P}_rn", f"{P}_nq"),
                        tag=f"{P}_rn2_{ss}_{kk}", small=True)
                for h in range(NH):
                    t = h % 2
                    rb = f"pb{t}"
                    bb = f"pb{2 + t}"
                    dsta, dtok = slot_of(kk, h, p)
                    em.step("pe", lambda eng, rb_=rb, Rn=Rname, src=dsta:
                            eng.matmul(self.ps[rb_][0:64, :],
                                       self.sb[Rn][:], src,
                                       start=True, stop=True),
                            reads=(dtok, Rname), writes=(rb,),
                            tag=f"{P}_qr{ss}_{kk}_{h}")
                    em.step("pe", lambda eng, h_=h, bb_=bb: eng.matmul(
                        self.ps[bb_][0:64, :],
                        self.sb["sec_selbc"][:, h_ * 64:(h_ + 1) * 64],
                        s_rn[:], start=True, stop=True),
                        reads=(f"{P}_rn", "sec_selbc"), writes=(bb,),
                        tag=f"{P}_rb{ss}_{kk}_{h}")
                    def comb(eng, t_=t, rb_=rb, bb_=bb, dst=dsta,
                             tA=tA_b, tB=tB_b):
                        eng.tensor_mul(s_sq[t_][:], dst, tA[:])
                        eng.tensor_tensor(
                            dst, self.ps[rb_][0:64, :], tB[:], ALU.mult)
                        eng.tensor_add(dst, dst, s_sq[t_][:])
                        return eng.tensor_mul(dst, dst,
                                              self.ps[bb_][0:64, :])
                    em.step("dve", comb,
                            reads=(dtok, rb, bb, tAn, tBn),
                            writes=(dtok, f"{P}_sq{t}", rb, bb),
                            tag=f"{P}_qc{ss}_{kk}_{h}")

        def back(w):
            q, wi = divmod(w, WPS)
            g0 = q * L + wi * WS
            ss = w // 2
            p = ss % 2
            w2 = w % 2
            xtok = f"{P}_xsh{p}"
            PB = ("pbA", "pbB")[w2]
            mcol, scol = s_mcol[w2], s_scol[w2]
            et, ao = s_et[w2], s_ao[w2]
            for tb in range(2):
                def vmm(eng, p_=p, w2_=w2, tb_=tb):
                    r = None
                    for kc in range(FC):
                        base = kc * 24 * 128 + 16 * 128
                        stat = s_xsh[p_][:, kc * FS + w2_ * WS + tb_ * 128:
                                         kc * FS + w2_ * WS +
                                         (tb_ + 1) * 128]
                        eng.matmul(
                            self.ps[PB][:, 0:512], stat,
                            self.sb["sec_wqkv"][:, base:base + 512],
                            start=(kc == 0), stop=(kc == FC - 1),
                            skip_group_check=True)
                        r = eng.matmul(
                            self.ps[PB][:, 512:1024], stat,
                            self.sb["sec_wqkv"][:, base + 512:base + 1024],
                            start=(kc == 0), stop=(kc == FC - 1),
                            skip_group_check=True)
                    return r
                em.step("pe", vmm, reads=(xtok, "sec_wqkv"),
                        writes=(PB,), tag=f"{P}_v{w}_{tb}")
                rc = (w * 2 + tb) % 6
                em.step("act", lambda eng, rc_=rc: eng.copy(
                    s_vt[:, rc_ * 1024:(rc_ + 1) * 1024],
                    self.ps[PB][:, 0:1024]),
                    reads=(PB,), writes=(f"{P}_vt{rc}", PB),
                    tag=f"{P}_ve{w}_{tb}")
            units = [(qb, h) for qb in range(2) for h in range(NH)]
            for gi in range(0, len(units), 2):
                grp = units[gi:gi + 2]
                for ui, (qb, h) in enumerate(grp):
                    qb_abs = wi * 2 + qb
                    mv = 2 if qb_abs == 0 else (1 if qb_abs == 1 else 0)
                    def p1mm(eng, h_=h, qb_=qb, mv_=mv, p_=p, w2_=w2,
                             ui_=ui):
                        eng.matmul(
                            self.ps[PB][:, ui_ * 512:ui_ * 512 + 384],
                            s_qaug[p_][0:64,
                                       h_ * FS + w2_ * WS + qb_ * 128:
                                       h_ * FS + w2_ * WS +
                                       (qb_ + 1) * 128],
                            s_kaug[p_][0:64, :].rearrange(
                                "pp (hh t) -> pp hh t", hh=NH)
                            [:, h_, w2_ * WS + qb_ * 128:
                             w2_ * WS + qb_ * 128 + 384],
                            start=True, stop=False, skip_group_check=True)
                        return eng.matmul(
                            self.ps[PB][:, ui_ * 512:ui_ * 512 + 384],
                            self.sb["k_ident_bf"][:],
                            self.sb["sec_maskq"][:, mv_ * 384:
                                                 (mv_ + 1) * 384],
                            start=False, stop=True, skip_group_check=True)
                    em.step("pe", p1mm,
                            reads=(f"{P}_q{p}_{h}", f"{P}_k{p}_{h}",
                                   f"{P}_kones{p}", "sec_maskq",
                                   "k_ident_bf"),
                            writes=(PB,), tag=f"{P}_p1{w}_{qb}_{h}")
                rsteps = []
                for ui, (qb, h) in enumerate(grp):
                    cc = qb * 16 + h
                    last = ui == len(grp) - 1
                    rsteps.append(em.step(
                        "dve", lambda eng, cc_=cc, ui_=ui:
                        eng.tensor_reduce(
                            mcol[:, cc_:cc_ + 1],
                            self.ps[PB][:, ui_ * 512:ui_ * 512 + 384],
                            AX.X, ALU.max, negate=True),
                        reads=(PB,), writes=(f"{P}_mc{w2}_{cc}",),
                        tag=f"{P}_p1r{w}_{qb}_{h}", small=last))
                rlast = rsteps[-1]
                for ui, (qb, h) in enumerate(grp):
                    cc = qb * 16 + h
                    def p1e(eng, cc_=cc, ui_=ui):
                        return eng.activation(
                            et[:, 0:384],
                            self.ps[PB][:, ui_ * 512:ui_ * 512 + 384],
                            AF.Exp, bias=mcol[:, cc_:cc_ + 1],
                            accum_out=scol[:, cc_:cc_ + 1])
                    em.step("act", p1e, reads=(PB,), deps=(rlast,),
                            writes=(f"{P}_sc{w2}_{cc}", f"{P}_et{w2}", PB),
                            tag=f"{P}_p1e{w}_{qb}_{h}",
                            small=(ui == len(grp) - 1))
            mtoks = tuple(f"{P}_mc{w2}_{c}" for c in range(32))
            stoks = tuple(f"{P}_sc{w2}_{c}" for c in range(32))
            em.step("act", lambda eng: eng.activation(
                s_r12[:, 32:64], scol[:], AF.Ln),
                reads=stoks, writes=(f"{P}_r12",),
                tag=f"{P}_c1_{w}", small=True)
            def cfn2(eng):
                eng.tensor_sub(s_r12[:, 32:64], mcol[:], s_r12[:, 32:64])
                eng.drain()
                eng.tensor_copy(s_r12[:, 0:32], s_r12[:, 32:64])
                eng.drain()
                return eng.tensor_sub(s_r12[:, 32:64], s_r12[:, 32:64],
                                      s_r12[:, 0:32])
            em.step("dve", cfn2, reads=mtoks + (f"{P}_r12",),
                    writes=(f"{P}_r12",),
                    tag=f"{P}_c2_{w}", small=True)
            em.step("pe", lambda eng: eng.transpose(
                self.ps[PB][0:64, 0:128], s_r12[:],
                self.sb["k_ident_f32"][:]),
                reads=(f"{P}_r12", "k_ident_f32"), writes=(PB,),
                tag=f"{P}_ct_{w}")
            em.step("act", lambda eng: eng.copy(
                s_rT[:], self.ps[PB][0:64, 0:128]),
                reads=(PB,), writes=(f"{P}_rT", PB),
                tag=f"{P}_cte_{w}", small=True)
            for rr in range(2):
                for qb in range(2):
                    em.dma("pool",
                           s_qaug[p][64 + rr:65 + rr, :].rearrange(
                               "o (hh t) -> o hh t", hh=NH)
                           [:, :, w2 * WS + qb * 128:
                            w2 * WS + (qb + 1) * 128],
                           s_rT[rr * 32 + qb * 16:rr * 32 + (qb + 1) * 16, :],
                           reads=(f"{P}_rT",),
                           writes=(f"{P}_qr_{p}_{w2}",),
                           tag=f"{P}_cd{rr}{qb}_{w}")
            for h in range(NH):
                tiles = [(ki, kb, c) for ki, kb in
                         enumerate([wi - 1, wi]) if kb >= 0
                         for c in range(2)]
                for ti, (ki, kb, c) in enumerate(tiles):
                    koff = w2 * WS + ki * 256 + c * 128
                    mslot = ki * 2 + c if len(tiles) == 4 else 2 + c
                    es = ki * 2 + c
                    def p2(eng, h_=h, koff_=koff, mslot_=mslot, p_=p,
                           w2_=w2, es_=es):
                        eng.matmul(
                            self.ps[PB][:, es_ * 256:es_ * 256 + 256],
                            s_kaug[p_][0:66, :].rearrange(
                                "pp (hh t) -> pp hh t", hh=NH)
                            [:, h_, koff_:koff_ + 128],
                            s_qaug[p_][0:66, h_ * FS + w2_ * WS:
                                       h_ * FS + (w2_ + 1) * WS],
                            start=True, stop=False, skip_group_check=True)
                        return eng.matmul(
                            self.ps[PB][:, es_ * 256:es_ * 256 + 256],
                            self.sb["k_ident_bf"][:],
                            self.sb["sec_maskt"][:, mslot_ * 256:
                                                 (mslot_ + 1) * 256],
                            start=False, stop=True, skip_group_check=True)
                    em.step("pe", p2,
                            reads=(f"{P}_q{p}_{h}", f"{P}_qr_{p}_{w2}",
                                   f"{P}_k{p}_{h}", f"{P}_kones{p}",
                                   "sec_maskt", "k_ident_bf"),
                            writes=(PB,), tag=f"{P}_p2{w}_{h}_{ki}_{c}")
                kbs = [kb for kb in (wi - 1, wi) if kb >= 0]
                for ci, kb in enumerate(kbs):
                    ki = 1 - (wi - kb)
                    lo = ki * 2 * 256
                    def p2e(eng, lo_=lo):
                        return eng.activation(
                            et[:, 0:512],
                            self.ps[PB][:, lo_:lo_ + 512], AF.Exp)
                    em.step("act", p2e, reads=(PB,),
                            writes=(f"{P}_et{w2}", PB),
                            tag=f"{P}_p2e{w}_{h}_{ki}")
                    vtoks = tuple(f"{P}_vt{((q * WPS + kb) * 2 + c) % 6}"
                                  for c in range(2))
                    def avc(eng, h_=h, kb_=kb, q_=q, ci_=ci,
                            last=(ci == len(kbs) - 1)):
                        r = None
                        for c in range(2):
                            rc = ((q_ * WPS + kb_) * 2 + c) % 6
                            r = eng.matmul(
                                self.ps[PB][0:64, 0:256],
                                s_vt[:, rc * 1024 + h_ * 64:
                                     rc * 1024 + (h_ + 1) * 64],
                                et[:, c * 256:(c + 1) * 256],
                                start=(ci_ == 0 and c == 0),
                                stop=(last and c == 1),
                                skip_group_check=True)
                        return r
                    em.step("pe", avc, reads=vtoks + (f"{P}_et{w2}",),
                            writes=(PB,), tag=f"{P}_av{w}_{h}_{ci}")
                def ave(eng, h_=h):
                    dst = ao[64 * (h_ % 2):64 * (h_ % 2) + 64,
                             (h_ // 2) * WS:(h_ // 2 + 1) * WS]
                    return eng.copy(dst, self.ps[PB][0:64, 0:256])
                em.step("act", ave, reads=(PB,),
                        writes=(f"{P}_ao{w2}", PB),
                        tag=f"{P}_ave{w}_{h}")
            for oc in range(FC):
                xp2 = oc % 2
                em.dma("sp", s_xr[w2][xp2][:],
                       hxc[:, oc, g0:g0 + WS],
                       writes=(f"{P}_xr{w2}_{xp2}",),
                       tag=f"{P}_lxr{w}_{oc}")
                def womm(eng, oc_=oc):
                    r = None
                    for kc in range(FC):
                        r = eng.matmul(
                            self.ps[PB][:, 512:768],
                            self.sb["sec_wo"][:, (kc * FC + oc_) * 128:
                                              (kc * FC + oc_ + 1) * 128],
                            ao[:, kc * WS:(kc + 1) * WS],
                            start=(kc == 0), stop=(kc == FC - 1))
                    return r
                em.step("pe", womm, reads=(f"{P}_ao{w2}", "sec_wo"),
                        writes=(PB,), tag=f"{P}_wo{w}_{oc}")
                def resid(eng, xp_=xp2):
                    return eng.tensor_add(
                        s_xo[w2][:], self.ps[PB][:, 512:768],
                        s_xr[w2][xp_][:])
                em.step("dve", resid, reads=(PB, f"{P}_xr{w2}_{xp2}"),
                        writes=(f"{P}_xo{w2}", PB),
                        tag=f"{P}_wr{w}_{oc}")
                em.dma("pool", hxn[:, oc, g0:g0 + WS], s_xo[w2][:],
                       reads=(f"{P}_xo{w2}",),
                       writes=(f"hx2n_{w}_{oc}",),
                       tag=f"{P}_wod{w}_{oc}")

        em.begin_collect(); front(0); em.emit_collected(em.end_collect())
        for ss in range(1, NF + 1):
            em.begin_collect()
            if ss < NF:
                front(ss)
            lf = em.end_collect()
            em.begin_collect()
            back(2 * ss - 2)
            le = em.end_collect()
            em.begin_collect()
            back(2 * ss - 1)
            lo = em.end_collect()
            em.emit_collected(Em.merge(Em.merge(le, lo), lf))
        self.pop_scope()
        self.x2cur, self.x2nxt = self.x2nxt, self.x2cur

    def phase_f1(self, li):
        em, dr = self.em, self.dram
        sbn = self._sbuf
        SM = 512
        NS = T // SM
        P = f"f{li}"
        self.push_scope()
        s_xsh = [sbn(f"{P}_xsh{p}", [128, FC * SM], BF16) for p in range(2)]
        s_sqt = [sbn(f"{P}_sqt{p}", [128, SM], BF16) for p in range(2)]
        s_nf = sbn(f"{P}_nf", [128, SM], F32)
        s_xn = sbn(f"{P}_xn", [128, FC * T], BF16)
        lnb = self.ln_bufs(f"{P}ln", SM)
        s_w1 = [sbn(f"{P}_w1{p}", [128, 2 * FC * 128], BF16) for p in range(2)]
        s_gel = [sbn(f"{P}_gel{p}", [128, 4 * SM], BF16) for p in range(2)]
        s_glu = [sbn(f"{P}_glu{p}", [128, 4 * SM], BF16) for p in range(2)]
        s_lng = sbn(f"{P}_lng", [128, FC], F32)
        s_lnb = sbn(f"{P}_lnb", [128, FC], F32)
        s_b1 = sbn(f"{P}_b1", [128, 2 * JP], F32)
        em.dma("sp", s_lng[:], dr["ln_g"][:, li * FC:(li + 1) * FC],
               writes=(f"{P}_lng",), tag=f"{P}_llng")
        em.dma("sp", s_lnb[:], dr["ln_b"][:, li * FC:(li + 1) * FC],
               writes=(f"{P}_lnb",), tag=f"{P}_llnb")
        em.dma("sp", s_b1[:], dr["b1"][:, li * 2 * JP:(li + 1) * 2 * JP],
               writes=(f"{P}_b1",), tag=f"{P}_lb1")
        hxc = dr[self.x2cur].rearrange("(fc p) t -> p fc t", p=128)
        hglu = dr["hglu"].rearrange("(jp p) t -> p jp t", p=128)
        w1r = dr["w1"]

        def lnstripe(ms):
            q, si = divmod(ms, L // SM)
            g0 = q * L + si * SM
            p = ms % 2
            tok = lambda n: f"{P}_{n}{p}"
            xsh = s_xsh[p]
            em.dma("pool", xsh[:].rearrange(
                "pp (fc t) -> pp fc t", fc=FC)[:, 0:4],
                hxc[:, 0:4, g0:g0 + SM], writes=(tok("xsh"),),
                tag=f"{P}_lx1_{ms}")
            if si == 0:
                em.dma("pool", xsh[:].rearrange(
                    "pp (fc t) -> pp fc t", fc=FC)[:, 4:8, 1:SM],
                    hxc[:, 4:8, g0:g0 + SM - 1], writes=(tok("xsh"),),
                    tag=f"{P}_lx2_{ms}")
                em.step("dve", lambda eng, p_=p: eng.memset(
                    s_xsh[p_][:].rearrange(
                        "pp (fc t) -> pp fc t", fc=FC)[:, 4:8, 0:1], 0.0),
                    writes=(tok("xsh"),), tag=f"{P}_z0_{ms}", small=True)
            else:
                em.dma("pool", xsh[:].rearrange(
                    "pp (fc t) -> pp fc t", fc=FC)[:, 4:8],
                    hxc[:, 4:8, g0 - 1:g0 + SM - 1], writes=(tok("xsh"),),
                    tag=f"{P}_lx2_{ms}")
            for fc in range(FC):
                pf = fc % 2
                em.step("dve", lambda eng, p_=p, fc_=fc, pf_=pf:
                        eng.tensor_mul(
                            s_sqt[pf_][:],
                            s_xsh[p_][:, fc_ * SM:(fc_ + 1) * SM],
                            s_xsh[p_][:, fc_ * SM:(fc_ + 1) * SM]),
                        reads=(tok("xsh"),), writes=(f"{P}_sqt{pf}",),
                        tag=f"{P}_sq{ms}_{fc}")
                em.step("pe", lambda eng, p_=p, fc_=fc: eng.matmul(
                    self.ps["pb6"][0:1, :], self.sb["k_ones_128x1"][:],
                    s_xsh[p_][:, fc_ * SM:(fc_ + 1) * SM],
                    start=(fc_ == 0), stop=(fc_ == FC - 1),
                    skip_group_check=True),
                    reads=(tok("xsh"), "k_ones_128x1"), writes=("pb6",),
                    tag=f"{P}_msx{ms}_{fc}")
                em.step("pe", lambda eng, fc_=fc, pf_=pf: eng.matmul(
                    self.ps["pb7"][0:1, :], self.sb["k_ones_128x1"][:],
                    s_sqt[pf_][:], start=(fc_ == 0), stop=(fc_ == FC - 1),
                    skip_group_check=True),
                    reads=(f"{P}_sqt{pf}", "k_ones_128x1"),
                    writes=("pb7",), tag=f"{P}_msq{ms}_{fc}")
            self.emit_ln(f"{P}ln{ms}", self.ps["pb6"][0:1, :],
                         self.ps["pb7"][0:1, :], ("pb6", "pb7"), SM,
                         (self.ps["pb6"][:], "pb6", True),
                         (self.ps["pb7"][:], "pb7", True), lnb)
            for fc in range(FC):
                em.step("dve", lambda eng, p_=p, fc_=fc: eng.tensor_mul(
                    s_nf[:], s_xsh[p_][:, fc_ * SM:(fc_ + 1) * SM],
                    self.ps["pb6"][:]),
                    reads=(tok("xsh"), "pb6"), writes=(f"{P}_nf",),
                    tag=f"{P}_n1{ms}_{fc}")
                em.step("dve", lambda eng, fc_=fc: eng.tensor_sub(
                    s_nf[:], s_nf[:], self.ps["pb7"][:]),
                    reads=(f"{P}_nf", "pb7"), writes=(f"{P}_nf",),
                    tag=f"{P}_n2{ms}_{fc}")
                def aff(eng, fc_=fc, ms_=ms):
                    return eng.tensor_scalar(
                        s_xn[:, fc_ * T + ms_ * SM:fc_ * T + (ms_ + 1) * SM],
                        s_nf[:],
                        s_lng[:, fc_:fc_ + 1],
                        s_lnb[:, fc_:fc_ + 1],
                        ALU.mult, ALU.add)
                em.step("dve", aff,
                        reads=(f"{P}_nf", f"{P}_lng", f"{P}_lnb"),
                        writes=(f"{P}_xn_{ms}",), tag=f"{P}_af{ms}_{fc}")

        for k0 in range(0, NS, 2):
            self.emit_pair(lnstripe, k0)

        for jp in range(JP):
            slot = jp % 2
            self.cast_load_split(
                self.sb[f"{P}_w1{slot}"],
                w1r[(li * JP + jp) * 128:(li * JP + jp + 1) * 128, :],
                None, writes=(f"{P}_w1_{slot}",), tag=f"{P}_lw1_{jp}",
                sub=(0, 2048))
            for half in range(2):
                pass
            for mq in range(2):
                bq = mq % 2
                pg = mq
                for half in (1, 0):
                    for m4 in range(4):
                        m = mq * 4 + m4
                        b = f"pb{(0 if bq == 0 else 4) + m4}"
                        xtok = f"{P}_xn_{m}"
                        def w1mm(eng, jp_=jp, sl=slot, h_=half, m_=m, b_=b):
                            r = None
                            for kc in range(FC):
                                r = eng.matmul(
                                    self.ps[b_][:],
                                    self.sb[f"{P}_w1{sl}"][
                                        :, h_ * FC * 128 + kc * 128:
                                        h_ * FC * 128 + (kc + 1) * 128],
                                    s_xn[:, kc * T + m_ * SM:
                                         kc * T + (m_ + 1) * SM],
                                    start=(kc == 0), stop=(kc == FC - 1))
                            return r
                        em.step("pe", w1mm,
                                reads=(xtok, f"{P}_w1_{slot}"),
                                writes=(b,),
                                tag=f"{P}_w1m{jp}_{half}_{m}")
                        if half == 1:
                            def gelf(eng, jp_=jp, m4_=m4, b_=b,
                                     pg_=pg):
                                return eng.activation(
                                    s_gel[pg_][:, m4_ * SM:(m4_ + 1) * SM],
                                    self.ps[b_][:], AF.Gelu,
                                    bias=s_b1[:, JP + jp_:JP + jp_ + 1])
                            em.step("act", gelf, reads=(b, f"{P}_b1"),
                                    writes=(f"{P}_gel{pg}", b),
                                    tag=f"{P}_gel{jp}_{m4}_{mq}")
                        else:
                            def gluf(eng, jp_=jp, m4_=m4, b_=b,
                                     pg_=pg):
                                return eng.scalar_tensor_tensor(
                                    s_glu[pg_][:, m4_ * SM:(m4_ + 1) * SM],
                                    self.ps[b_][:],
                                    s_b1[:, jp_:jp_ + 1],
                                    s_gel[pg_][:, m4_ * SM:(m4_ + 1) * SM],
                                    ALU.add, ALU.mult)
                            em.step("dve", gluf,
                                    reads=(b, f"{P}_gel{pg}", f"{P}_b1"),
                                    writes=(f"{P}_glu{pg}", b),
                                    tag=f"{P}_glu{jp}_{m4}_{mq}")
                em.dma("sp", hglu[:, jp, mq * 4 * SM:(mq + 1) * 4 * SM],
                       s_glu[pg][:], reads=(f"{P}_glu{pg}",),
                       writes=(f"hglu_{jp}_{mq}",),
                       tag=f"{P}_od{jp}_{mq}")
        self.pop_scope()

    def phase_f2(self, li):
        em, dr = self.em, self.dram
        sbn = self._sbuf
        SM = 512
        G = 1024
        P = f"g{li}"
        self.push_scope()
        s_glu = [sbn(f"{P}_glu{p}", [128, JP * G], BF16) for p in range(2)]
        s_w2 = [sbn(f"{P}_w2{p}", [128, JP * 128], BF16) for p in range(2)]
        s_xr = [sbn(f"{P}_xr{p}", [128, SM], F32) for p in range(2)]
        s_xo = [sbn(f"{P}_xo{p}", [128, SM], F32) for p in range(2)]
        s_b2 = sbn(f"{P}_b2", [128, FC], F32)
        em.dma("sp", s_b2[:], dr["b2"][:, li * FC:(li + 1) * FC],
               writes=(f"{P}_b2",), tag=f"{P}_lb2")
        hxc = dr[self.x2cur].rearrange("(fc p) t -> p fc t", p=128)
        hxn = dr[self.x2nxt].rearrange("(fc p) t -> p fc t", p=128)
        hglu = dr["hglu"].rearrange("(jp p) t -> p jp t", p=128)
        w2r = dr["w2"]

        def group(g):
            g0 = g * G
            gp = g % 2
            em.dma("sp", s_glu[gp][:].rearrange("p (jp t) -> p jp t", jp=JP),
                   hglu[:, :, g0:g0 + G], reads=(f"hglu_fin",),
                   writes=(f"{P}_glu{gp}",), tag=f"{P}_lg{g}")
            for oc in range(FC):
                slot = oc % 2
                self.cast_load_split(
                    self.sb[f"{P}_w2{slot}"],
                    w2r[(li * FC + oc) * 128:(li * FC + oc + 1) * 128, :],
                    None, writes=(f"{P}_w2_{slot}",),
                    tag=f"{P}_lw2_{g}_{oc}", sub=(0, JP * 128))
                for m in range(2):
                    b = f"pb{(oc % 2) * 4 + m}"
                    xp2 = m % 2
                    em.dma("sp", s_xr[xp2][:],
                           hxc[:, oc, g0 + m * SM:g0 + (m + 1) * SM],
                           writes=(f"{P}_xr{xp2}",),
                           tag=f"{P}_lxr{g}_{oc}_{m}")
                    def w2mm(eng, sl=slot, m_=m, b_=b, gp_=gp):
                        r = None
                        for kc in range(JP):
                            r = eng.matmul(
                                self.ps[b_][:],
                                self.sb[f"{P}_w2{sl}"][:, kc * 128:
                                                       (kc + 1) * 128],
                                s_glu[gp_][:, kc * G + m_ * SM:
                                           kc * G + (m_ + 1) * SM],
                                start=(kc == 0), stop=(kc == JP - 1))
                        return r
                    em.step("pe", w2mm,
                            reads=(f"{P}_glu{gp}", f"{P}_w2_{slot}"),
                            writes=(b,), tag=f"{P}_w2m{g}_{oc}_{m}")
                    def resid(eng, oc_=oc, b_=b, xp_=xp2):
                        return eng.scalar_tensor_tensor(
                            s_xo[xp_][:], self.ps[b_][:],
                            s_b2[:, oc_:oc_ + 1],
                            s_xr[xp_][:], ALU.add, ALU.add)
                    em.step("dve", resid,
                            reads=(b, f"{P}_b2", f"{P}_xr{xp2}"),
                            writes=(f"{P}_xo{xp2}", b),
                            tag=f"{P}_r{g}_{oc}_{m}")
                    em.dma("sp", hxn[:, oc, g0 + m * SM:g0 + (m + 1) * SM],
                           s_xo[xp2][:], reads=(f"{P}_xo{xp2}",),
                           writes=(f"hx2f_{g}_{oc}_{m}",),
                           tag=f"{P}_od{g}_{oc}_{m}")

        for k0 in range(0, T // G, 2):
            self.emit_pair(group, k0)
        self.pop_scope()
        self.x2cur, self.x2nxt = self.x2nxt, self.x2cur

    def phase_fin(self):
        em, dr = self.em, self.dram
        sbn = self._sbuf
        SM = 512
        NS = T // SM
        P = "fin"
        self.push_scope()
        s_x2 = [sbn(f"{P}_x2{p}", [128, FC * SM], F32) for p in range(2)]
        s_x1 = [sbn(f"{P}_x1{p}", [128, FC * SM], F32) for p in range(2)]
        s_sqt = [sbn(f"{P}_sqt{p}", [128, SM], BF16) for p in range(2)]
        s_xbt = [sbn(f"{P}_xbt{p}", [128, SM], BF16) for p in range(2)]
        s_nf = sbn(f"{P}_nf", [128, SM], F32)
        s_xf = [sbn(f"{P}_xf{p}", [128, FC * SM], F32) for p in range(2)]
        s_to = [sbn(f"{P}_to{p}", [128, 1024], F32) for p in range(2)]
        lnb = self.ln_bufs(f"{P}ln", SM)
        hxc = dr[self.x2cur].rearrange("(fc p) t -> p fc t", p=128)
        hx1 = dr["hx1"].rearrange("(fc p) t -> p fc t", p=128)
        def stripe(ms):
            q, si = divmod(ms, L // SM)
            g0 = q * L + si * SM
            p = ms % 2
            tok = lambda n: f"{P}_{n}{p}"
            em.dma("sp", s_x2[p][:].rearrange("p (fc t) -> p fc t", fc=FC),
                   hxc[:, :, g0:g0 + SM], writes=(tok("x2"),),
                   tag=f"{P}_lx{ms}")
            em.dma("sp", s_x1[p][:].rearrange("p (fc t) -> p fc t", fc=FC),
                   hx1[:, :, g0:g0 + SM], reads=("hx1_fin",),
                   writes=(tok("x1"),), tag=f"{P}_l1{ms}")
            for fc in range(FC):
                pf = fc % 2
                em.step("dve", lambda eng, p_=p, fc_=fc, pf_=pf:
                        eng.tensor_mul(
                            s_sqt[pf_][:],
                            s_x2[p_][:, fc_ * SM:(fc_ + 1) * SM],
                            s_x2[p_][:, fc_ * SM:(fc_ + 1) * SM]),
                        reads=(tok("x2"),), writes=(f"{P}_sqt{pf}",),
                        tag=f"{P}_sq{ms}_{fc}")
                em.step("dve", lambda eng, p_=p, fc_=fc, pf_=pf:
                        eng.tensor_copy(
                            s_xbt[pf_][:],
                            s_x2[p_][:, fc_ * SM:(fc_ + 1) * SM]),
                        reads=(tok("x2"),), writes=(f"{P}_xbt{pf}",),
                        tag=f"{P}_xb{ms}_{fc}")
                em.step("pe", lambda eng, fc_=fc, pf_=pf: eng.matmul(
                    self.ps["pb0"][0:1, :], self.sb["k_ones_128x1"][:],
                    s_xbt[pf_][:], start=(fc_ == 0), stop=(fc_ == FC - 1),
                    skip_group_check=True),
                    reads=(f"{P}_xbt{pf}", "k_ones_128x1"), writes=("pb0",),
                    tag=f"{P}_msx{ms}_{fc}")
                em.step("pe", lambda eng, fc_=fc, pf_=pf: eng.matmul(
                    self.ps["pb1"][0:1, :], self.sb["k_ones_128x1"][:],
                    s_sqt[pf_][:], start=(fc_ == 0), stop=(fc_ == FC - 1),
                    skip_group_check=True),
                    reads=(f"{P}_sqt{pf}", "k_ones_128x1"), writes=("pb1",),
                    tag=f"{P}_msq{ms}_{fc}")
            self.emit_ln(f"{P}ln{ms}", self.ps["pb0"][0:1, :],
                         self.ps["pb1"][0:1, :], ("pb0", "pb1"), SM,
                         (self.ps["pb2"][:], "pb2", True),
                         (self.ps["pb3"][:], "pb3", True), lnb)
            for fc in range(FC):
                def xffn(eng, p_=p, fc_=fc):
                    eng.tensor_mul(s_nf[:],
                                   s_x2[p_][:, fc_ * SM:(fc_ + 1) * SM],
                                   self.ps["pb2"][:])
                    eng.tensor_sub(s_nf[:], s_nf[:], self.ps["pb3"][:])
                    return eng.tensor_add(
                        s_xf[p_][:, fc_ * SM:(fc_ + 1) * SM], s_nf[:],
                        s_x1[p_][:, fc_ * SM:(fc_ + 1) * SM])
                em.step("dve", xffn,
                        reads=(tok("x2"), tok("x1"), "pb2", "pb3"),
                        writes=(tok("xf"), f"{P}_nf"),
                        tag=f"{P}_xf{ms}_{fc}")
            for tb in range(4):
                bb = 4 + 2 * (tb % 2)
                def trf(eng, p_=p, tb_=tb, bb_=bb):
                    r = None
                    for fc in range(FC):
                        bank = f"pb{bb_ + fc // 4}"
                        r = eng.transpose(
                            self.ps[bank][:, (fc % 4) * 128:
                                          (fc % 4 + 1) * 128],
                            s_xf[p_][:, fc * SM + tb_ * 128:
                                     fc * SM + (tb_ + 1) * 128],
                            self.sb["k_ident_f32"][:])
                    return r
                em.step("pe", trf, reads=(tok("xf"), "k_ident_f32"),
                        writes=(f"pb{bb}", f"pb{bb + 1}"),
                        tag=f"{P}_tr{ms}_{tb}")
                pt = tb % 2
                def tev(eng, bb_=bb, pt_=pt):
                    eng.copy(s_to[pt_][:, 0:512], self.ps[f"pb{bb_}"][:])
                    return eng.copy(s_to[pt_][:, 512:1024],
                                    self.ps[f"pb{bb_ + 1}"][:])
                em.step("act", tev, reads=(f"pb{bb}", f"pb{bb + 1}"),
                        writes=(f"{P}_to{pt}",), tag=f"{P}_te{ms}_{tb}")
                em.dma("sp", dr["out"][g0 + tb * 128:g0 + (tb + 1) * 128, :],
                       s_to[pt][:], reads=(f"{P}_to{pt}",),
                       writes=(f"out_{ms}_{tb}",), tag=f"{P}_od{ms}_{tb}")
        for k0 in range(0, NS, 2):
            self.emit_pair(stripe, k0)
        self.pop_scope()

    def finish(self):
        em = self.em
        em.finalize(self.nc, self.root)
        with self.nc.Block() as block:
            @block.tensor
            def _(eng):
                em.run_engine("pe", eng)
            @block.vector
            def _(eng):
                em.run_engine("dve", eng)
            @block.scalar
            def _(eng):
                em.run_engine("act", eng)
            @block.gpsimd
            def _(eng):
                em.run_engine("pool", eng)
            @block.sync
            def _(eng):
                em.run_engine("sp", eng)
                em.final_barrier(eng)


def build_nc(n_phases=99):
    nc = bass.Bass("TRN2", target_bir_lowering=False,
                   detect_race_conditions=False)
    mk = MK(nc, n_phases)
    mk.load_consts()
    mk.phase_m1()
    if n_phases >= 2:
        mk.phase_m2()
    if n_phases >= 3:
        mk.section_begin()
        for li in range(6):
            if n_phases >= 3 + li * 3:
                mk.phase_attn(li)
            if n_phases >= 4 + li * 3:
                mk.phase_f1(li)
                if li < 5 and n_phases >= 6 + li * 3:
                    mk.load_layer_weights(li + 1)
            if n_phases >= 5 + li * 3:
                mk.phase_f2(li)
        mk.section_end()
    if n_phases >= 21:
        mk.phase_fin()
    mk.finish()
    nc.finalize()
    return mk, nc


def kernel(**inputs):
    consts = _consts()
    shared = prep_shared_inputs(inputs, consts)
    in_maps = []
    for core in range(NCORES):
        m = dict(shared)
        m.update(prep_core_inputs(core, inputs, consts))
        in_maps.append(m)
    mk, nc = build_nc(int(os.environ.get("MK_PHASES", "99")))
    res = run_bass_kernel_spmd(nc, in_maps, core_ids=list(range(NCORES)))
    out = np.concatenate(
        [r["out"].reshape(SEQ, L, D) for r in res.results], axis=0)
    return out.astype(np.float32)


# version 08
# speedup vs baseline: 1.0156x; 1.0156x over previous
import math
import os
import numpy as np
import ml_dtypes
from contextlib import ExitStack

import concourse.bass as bass
import concourse.mybir as mybir
from concourse.bass_utils import run_bass_kernel_spmd

F32 = mybir.dt.float32
BF16 = mybir.dt.bfloat16
AF = mybir.ActivationFunctionType
ALU = mybir.AluOpType
AX = mybir.AxisListType
BF = ml_dtypes.bfloat16

NCORES = 8
SEQ = 2
L = 2048
T = SEQ * L
D = 1024
FC = D // 128
DI = 2048
IC = DI // 128
DST = 16
NH = 16
DH = 64
W = 256
DF = 2816
JP = DF // 128
NEG = -1.0e30

DBG = int(os.environ.get("MK_DBG", "0"))


def _rot_tables():
    half = np.arange(0, DH, 2, dtype=np.float64)
    inv_freq = 1.0 / 10000.0 ** (half / DH)
    t = np.arange(L, dtype=np.float64)
    freqs = t[:, None] * inv_freq[None, :]
    freqs = np.concatenate([freqs, freqs], -1)
    base_scale = (half + 0.4 * DH) / (1.4 * DH)
    power = (t - L // 2) / (W // 2)
    scale = base_scale[None, :] ** power[:, None]
    scale = np.concatenate([scale, scale], -1)
    cs, sn = np.cos(freqs), np.sin(freqs)
    qk = 8.0
    csq = (cs * scale * qk).T
    ssq = (sn * scale * qk).T
    csi = (cs / scale).T
    ssi = (sn / scale).T
    f = lambda a: np.ascontiguousarray(a.astype(np.float32))
    return f(csq), f(ssq), f(csi), f(ssi)


def _consts():
    c = {}
    c["ident_bf"] = np.eye(128, dtype=np.float32).astype(BF)
    c["ident_f32"] = np.eye(128, dtype=np.float32)
    R = np.zeros((64, 64), np.float32)
    for m in range(32):
        R[m + 32, m] = -1.0
        R[m, m + 32] = 1.0
    c["rot"] = R.astype(BF)
    csq, ssq, csi, ssi = _rot_tables()
    c["csq"], c["ssq"] = csq.astype(BF), ssq.astype(BF)
    c["csi"], c["ssi"] = csi.astype(BF), ssi.astype(BF)
    sel1 = np.zeros((64, 16, 16), np.float32)
    for h in range(16):
        sel1[:, h, h] = 1.0
    c["selsum"] = sel1.reshape(64, 256).astype(BF)
    sel2 = np.zeros((16, 16, 64), np.float32)
    for h in range(16):
        sel2[h, h, :] = 1.0
    c["selbc"] = sel2.reshape(16, 1024).astype(BF)
    mq = np.full((3, 128, 384), NEG, np.float32)
    for v, clip in enumerate([0, 128, 256]):
        for i in range(128):
            lo, hi = max(i, clip), i + 257
            mq[v, i, lo:hi] = 0.0
    c["maskq"] = mq.astype(BF)
    mt = np.full((4, 128, 256), NEG, np.float32)
    for cc in range(2):
        for j in range(128):
            mt[cc, j, 0:min(128 * cc + j + 1, 256)] = 0.0
            mt[2 + cc, j, min(128 * cc + j, 256):256] = 0.0
    c["maskt"] = mt.astype(BF)
    c["ones_1x128"] = np.ones((1, 128), np.float32).astype(BF)
    c["ones_128x1"] = np.ones((128, 1), np.float32).astype(BF)
    c["ones_1x64"] = np.ones((1, 64), np.float32).astype(BF)
    return c


def _chunk2d(w, kc, oc):
    K, N = w.shape
    assert K == kc * 128 and N == oc * 128, (w.shape, kc, oc)
    return np.ascontiguousarray(
        w.reshape(kc, 128, oc, 128).transpose(0, 2, 1, 3))


def prep_core_inputs(core, inputs, consts):
    d = {}
    x = np.asarray(inputs["L0_attn_x"], np.float32)
    d["x_in"] = np.ascontiguousarray(
        x[core * SEQ:(core + 1) * SEQ].reshape(T, D))
    return d


def prep_shared_inputs(inputs, consts):
    f32 = lambda k: np.asarray(inputs[k], np.float32)
    d = {}

    def pm(w, kc, oc):
        ch = _chunk2d(w, kc, oc)
        return np.ascontiguousarray(
            ch.transpose(2, 0, 1, 3).reshape(128, kc * oc * 128))

    d["w_in"] = pm(f32("L0_attn_W_in"), FC, 32)
    d["w_out"] = pm(f32("L0_attn_W_out"), IC, FC)
    wx = f32("L0_attn_W_x")
    d["w_x"] = np.ascontiguousarray(
        wx.reshape(IC, 128, 32).transpose(1, 0, 2).reshape(128, IC * 32))
    d["w_dt"] = np.ascontiguousarray(f32("L0_attn_W_dt"))
    d["w_bp"] = np.ascontiguousarray(f32("L0_attn_W_Bp"))
    d["b_dt"] = np.ascontiguousarray(
        f32("L0_attn_b_dt").reshape(IC, 128).T)
    d["d_param"] = np.ascontiguousarray(
        f32("L0_attn_D_param").reshape(IC, 128).T)
    cw = f32("L0_attn_conv_w")[:, 0, :]
    d["conv_w"] = np.ascontiguousarray(
        cw.reshape(IC, 128, 4).transpose(1, 0, 2).reshape(128, IC * 4))
    wqkv = f32("L0_ffn_Wqkv")
    d["wqkv"] = np.ascontiguousarray(
        np.stack([pm(wqkv[i], FC, 24) for i in range(6)]).reshape(
            6 * 128, FC * 24 * 128))
    wo = f32("L0_ffn_Wo")
    d["wo"] = np.ascontiguousarray(
        np.stack([pm(wo[i], FC, FC) for i in range(6)]).reshape(
            6 * 128, FC * FC * 128))
    w1 = f32("L0_ffn_W1")
    w1p = np.zeros((6, D, 2, DF), np.float32)
    w1p[:, :, 0, 0:2730] = w1[:, :, 0:2730]
    w1p[:, :, 1, 0:2730] = w1[:, :, 2730:5460]
    w1c = np.zeros((6, JP, 128, 2 * FC * 128), np.float32)
    for i in range(6):
        for h in range(2):
            ch = _chunk2d(w1p[i, :, h], FC, JP)
            for jp in range(JP):
                blk = ch[:, jp].transpose(1, 0, 2).reshape(128, FC * 128)
                w1c[i, jp, :, h * FC * 128:(h + 1) * FC * 128] = blk
    d["w1"] = np.ascontiguousarray(w1c.reshape(6 * JP * 128, 2 * FC * 128))
    w2 = f32("L0_ffn_W2")
    w2p = np.zeros((6, DF, D), np.float32)
    w2p[:, 0:2730] = w2
    w2c = np.zeros((6, FC, 128, JP * 128), np.float32)
    for i in range(6):
        ch = _chunk2d(w2p[i], JP, FC)
        for oc in range(FC):
            w2c[i, oc] = ch[:, oc].transpose(1, 0, 2).reshape(128, JP * 128)
    d["w2"] = np.ascontiguousarray(w2c.reshape(6 * FC * 128, JP * 128))
    b1 = f32("L0_ffn_b1")
    b1p = np.zeros((6, 2, DF), np.float32)
    b1p[:, 0, 0:2730] = b1[:, 0:2730]
    b1p[:, 1, 0:2730] = b1[:, 2730:5460]
    d["b1"] = np.ascontiguousarray(
        b1p.reshape(6, 2, JP, 128).transpose(3, 0, 1, 2).reshape(
            128, 6 * 2 * JP))
    d["b2"] = np.ascontiguousarray(
        f32("L0_ffn_b2").reshape(6, FC, 128).transpose(2, 0, 1).reshape(
            128, 6 * FC))
    d["ln_g"] = np.ascontiguousarray(
        f32("L0_ffn_ln_g").reshape(6, FC, 128).transpose(2, 0, 1).reshape(
            128, 6 * FC))
    d["ln_b"] = np.ascontiguousarray(
        f32("L0_ffn_ln_b").reshape(6, FC, 128).transpose(2, 0, 1).reshape(
            128, 6 * FC))
    qs = f32("L0_ffn_q_scale")
    ks = f32("L0_ffn_k_scale")
    d["qs2"] = np.ascontiguousarray(qs.T)
    d["ks2"] = np.ascontiguousarray(ks.T)
    for k, v in consts.items():
        d["c_" + k] = v
    return d


class Em:
    ENGS = ("pe", "dve", "act", "sp", "pool")
    DMA_ENGS = ("sp", "pool")

    def __init__(self, sync_same=False):
        self.steps = []
        self.last_w = {}
        self.readers = {}
        self._fence = []
        self.sync_same = sync_same

    def fence(self):
        last = {}
        for i, s in enumerate(self.steps):
            last[s["eng"]] = i
        self._fence = sorted(last.values())

    def begin_collect(self):
        assert not hasattr(self, "_collect") or self._collect is None
        self._collect = []

    def end_collect(self):
        c = self._collect
        self._collect = None
        return c

    @staticmethod
    def merge(a, b):
        from collections import defaultdict
        atouch = defaultdict(int)
        awrite = defaultdict(int)
        for st in a:
            for t in set(st["reads"]) | set(st["writes"]):
                atouch[t] += 1
            for t in set(st["writes"]):
                awrite[t] += 1
        out = []
        ia = ib = 0
        while ia < len(a) or ib < len(b):
            pick_b = False
            if ib < len(b):
                if ia >= len(a):
                    pick_b = True
                elif ib / len(b) <= ia / len(a):
                    st = b[ib]
                    if (all(awrite[t] == 0 for t in set(st["reads"]))
                            and all(atouch[t] == 0
                                    for t in set(st["writes"]))):
                        pick_b = True
            if pick_b:
                out.append(b[ib]); ib += 1
            else:
                st = a[ia]
                for t in set(st["reads"]) | set(st["writes"]):
                    atouch[t] -= 1
                for t in set(st["writes"]):
                    awrite[t] -= 1
                out.append(st); ia += 1
        return out

    def emit_collected(self, ordered):
        mapping = {}
        for st in ordered:
            deps = tuple(mapping[d] if isinstance(d, str) else d
                         for d in st["xdeps"])
            i = self.step(st["eng"], st["fn"], reads=st["reads"],
                          writes=st["writes"], deps=deps, tag=st["tag"],
                          small=st["small"])
            mapping[st["uid"]] = i
            if st["is_dma"]:
                self.steps[i]["is_dma"] = True
        return mapping

    def step(self, eng, fn, reads=(), writes=(), deps=(), tag="", small=False):
        if getattr(self, "_collect", None) is not None:
            uid = f"c{len(self._collect)}_{tag}"
            self._collect.append(dict(
                eng=eng, fn=fn, reads=tuple(reads), writes=tuple(writes),
                xdeps=tuple(d for d in deps if d is not None), tag=tag,
                small=small, uid=uid, is_dma=False))
            return uid
        assert eng in self.ENGS, eng
        i = len(self.steps)
        dd = set(d for d in deps if d is not None)
        dd.update(self._fence)
        for t in reads:
            w = self.last_w.get(t)
            if w is not None:
                dd.add(w)
        for t in writes:
            w = self.last_w.get(t)
            if w is not None:
                dd.add(w)
            for r in self.readers.get(t, ()):
                dd.add(r)
        for t in reads:
            self.readers.setdefault(t, []).append(i)
        for t in writes:
            self.last_w[t] = i
            self.readers[t] = []
        self.steps.append(dict(eng=eng, fn=fn, deps=sorted(dd), tag=tag,
                               small=small, is_dma=False))
        return i

    def dma(self, queue, dst, src, reads=(), writes=(), deps=(), tag=""):
        def fn(eng):
            return eng.dma_start(dst, src)
        i = self.step(queue, fn, reads=reads, writes=writes, deps=deps,
                      tag=tag)
        if isinstance(i, str):
            self._collect[-1]["is_dma"] = True
        else:
            self.steps[i]["is_dma"] = True
        return i

    COMPUTE_ENGS = ("dve", "act", "pool")

    def finalize(self, nc, ctx):
        sems = {e: ctx.enter_context(nc.semaphore(f"sem_{e}"))
                for e in self.ENGS}
        inc_amt = {e: (16 if e in self.DMA_ENGS else 1) for e in self.ENGS}
        need_inc = [s["eng"] in self.DMA_ENGS for s in self.steps]
        for s in self.steps:
            for dd in s["deps"]:
                if self.sync_same or self.steps[dd]["eng"] != s["eng"]:
                    need_inc[dd] = True
        last_of = {}
        for i, s in enumerate(self.steps):
            last_of[s["eng"]] = i
        for i in last_of.values():
            need_inc[i] = True
        ordinal = [0] * len(self.steps)
        cnt = {e: 0 for e in self.ENGS}
        for i, s in enumerate(self.steps):
            if need_inc[i]:
                cnt[s["eng"]] += 1
            ordinal[i] = cnt[s["eng"]]
        self._sems, self._need_inc, self._ordinal = sems, need_inc, ordinal
        self._inc_amt = inc_amt
        self._final_counts = {e: cnt[e] * inc_amt[e] for e in self.ENGS}

    def run_engine(self, eng_name, eng_obj):
        sems = self._sems
        hw = {e: 0 for e in self.ENGS}
        n = 0
        for i, s in enumerate(self.steps):
            if s["eng"] != eng_name:
                continue
            waitmax = {}
            for dd in s["deps"]:
                de = self.steps[dd]["eng"]
                if (de == eng_name and eng_name not in self.DMA_ENGS
                        and not self.sync_same):
                    continue
                assert self._need_inc[dd], (i, dd)
                v = self._ordinal[dd] * self._inc_amt[de]
                waitmax[de] = max(waitmax.get(de, 0), v)
            for de, v in waitmax.items():
                if v > hw[de]:
                    eng_obj.wait_ge(sems[de], v)
                    hw[de] = v
            if (eng_name in self.COMPUTE_ENGS and not s["is_dma"] and any(
                    self.steps[dd]["eng"] == eng_name
                    and self.steps[dd]["small"] for dd in s["deps"])):
                eng_obj.drain()
            inst = s["fn"](eng_obj)
            if self._need_inc[i]:
                assert inst is not None, (i, s["tag"])
                if (eng_name in self.COMPUTE_ENGS and not s["is_dma"]
                        and (s["small"] or self.sync_same)):
                    inst = eng_obj.drain()
                inst.then_inc(sems[eng_name], self._inc_amt[eng_name])
            n += 1
        return n

    def final_barrier(self, eng_obj, skip=()):
        for e in self.ENGS:
            if e in skip:
                continue
            v = self._final_counts[e]
            if v:
                eng_obj.wait_ge(self._sems[e], v)


class MK:
    def __init__(self, nc, n_phases=99):
        self.nc = nc
        self.em = Em(sync_same=bool(int(os.environ.get("MK_SYNCSAME", "0"))))
        self.root = ExitStack()
        self.scopes = [self.root]
        self.n_phases = n_phases
        self.dram = {}
        self.sb = {}
        self.ps = {}
        self._declare()

    def emit_pair(self, fn, k0):
        if os.environ.get("MK_MERGE", "1") == "1":
            em = self.em
            em.begin_collect(); fn(k0); la = em.end_collect()
            em.begin_collect(); fn(k0 + 1); lb = em.end_collect()
            em.emit_collected(Em.merge(la, lb))
        else:
            fn(k0); fn(k0 + 1)

    def push_scope(self):
        self.scopes.append(ExitStack())

    def pop_scope(self):
        self.scopes.pop().close()
        self.em.fence()

    def _din(self, name, shape, dt):
        self.dram[name] = self.nc.dram_tensor(name, shape, dt,
                                              kind="ExternalInput").ap()
        return self.dram[name]

    def _dscratch(self, name, shape, dt, dump=False):
        kind = "ExternalOutput" if (DBG and dump) else "Internal"
        self.dram[name] = self.nc.dram_tensor(name, shape, dt, kind=kind).ap()
        return self.dram[name]

    def _sbuf(self, name, shape, dt):
        t = self.scopes[-1].enter_context(self.nc.sbuf_tensor(name, shape, dt))
        self.sb[name] = t
        return t

    def _psum(self, name, shape=(128, 512), dt=F32):
        t = self.root.enter_context(self.nc.psum_tensor(name, list(shape), dt))
        self.ps[name] = t
        return t

    def _declare(self):
        din = self._din
        din("x_in", [T, D], F32)
        din("w_in", [128, FC * 32 * 128], F32)
        din("w_out", [128, IC * FC * 128], F32)
        din("w_x", [128, IC * 32], F32)
        din("w_dt", [DST, IC * 128], F32)
        din("w_bp", [DST, IC * 128], F32)
        din("b_dt", [128, IC], F32)
        din("d_param", [128, IC], F32)
        din("conv_w", [128, IC * 4], F32)
        din("wqkv", [6 * 128, FC * 24 * 128], F32)
        din("wo", [6 * 128, FC * FC * 128], F32)
        din("w1", [6 * JP * 128, 2 * FC * 128], F32)
        din("w2", [6 * FC * 128, JP * 128], F32)
        din("b1", [128, 6 * 2 * JP], F32)
        din("b2", [128, 6 * FC], F32)
        din("ln_g", [128, 6 * FC], F32)
        din("ln_b", [128, 6 * FC], F32)
        din("qs2", [64, 6], F32)
        din("ks2", [64, 6], F32)
        din("c_ident_bf", [128, 128], BF16)
        din("c_ident_f32", [128, 128], F32)
        din("c_rot", [64, 64], BF16)
        din("c_csq", [64, L], BF16)
        din("c_ssq", [64, L], BF16)
        din("c_csi", [64, L], BF16)
        din("c_ssi", [64, L], BF16)
        din("c_selsum", [64, 256], BF16)
        din("c_selbc", [16, 1024], BF16)
        din("c_maskq", [3 * 128, 384], BF16)
        din("c_maskt", [4 * 128, 256], BF16)
        din("c_ones_1x128", [1, 128], BF16)
        din("c_ones_128x1", [128, 1], BF16)
        din("c_ones_1x64", [1, 64], BF16)
        self.dram["out"] = self.nc.dram_tensor(
            "out", [T, D], F32, kind="ExternalOutput").ap()
        self._dscratch("hx0", [D, T], F32, dump=True)
        self._dscratch("hx1", [D, T], F32, dump=True)
        self._dscratch("hx2a", [D, T], F32, dump=True)
        self._dscratch("hx2b", [D, T], F32, dump=True)
        self._dscratch("hu", [DI, T], BF16, dump=True)
        self._dscratch("hsz", [DI, T], BF16, dump=True)
        self._dscratch("hglu", [DF, T], BF16, dump=True)
        if DBG:
            self._dscratch("dbg_st", [8 * 128, 8], F32, dump=True)
            self._dscratch("dbg_ab", [8 * 128, 8], F32, dump=True)
            self._dscratch("dbg_row", [8, 2048], F32, dump=True)
            self._dscratch("dbg_xn", [128, FC * 512], BF16, dump=True)
        for i in range(4):
            self._psum(f"pb{i}")
        self._psum("pbA", (128, 1024))
        self._psum("pbB", (128, 1024))

        class _PV:
            def __init__(self, t, off):
                self.t, self.off = t, off

            def __getitem__(self, key):
                if key == slice(None):
                    return self.t[:, self.off:self.off + 512]
                rows, cols = key
                c0 = cols.start or 0
                c1 = cols.stop if cols.stop is not None else 512
                return self.t[rows, self.off + c0:self.off + c1]
        self.ps["pb4"] = _PV(self.ps["pbA"], 0)
        self.ps["pb5"] = _PV(self.ps["pbA"], 512)
        self.ps["pb6"] = _PV(self.ps["pbB"], 0)
        self.ps["pb7"] = _PV(self.ps["pbB"], 512)
        sbn = self._sbuf
        sbn("k_ident_bf", [128, 128], BF16)
        sbn("k_ident_f32", [128, 128], F32)
        sbn("k_ones_1x128", [1, 128], BF16)
        sbn("k_ones_128x1", [128, 1], BF16)
        sbn("k_ones_1x64", [1, 64], BF16)

    def load_consts(self):
        em = self.em
        for name in ["ident_bf", "ident_f32", "ones_1x128", "ones_128x1",
                     "ones_1x64"]:
            em.dma("sp", self.sb["k_" + name][:], self.dram["c_" + name],
                   writes=(f"k_{name}",), tag=f"ld_{name}")

    def cast_load(self, dst_ap, src_ap, writes, tag, deps=(), reads=()):
        return self.em.dma("pool", dst_ap, src_ap, reads=reads, writes=writes,
                           deps=deps, tag=tag)

    def cast_load_split(self, dst_ap, src_ap, ncols, writes, tag,
                        chunk=16384, sub=None):
        if sub is not None:
            ncols = sub[1]
        r = None
        for c0 in range(0, ncols, chunk):
            c1 = min(c0 + chunk, ncols)
            r = self.em.dma("pool", dst_ap[:, c0:c1], src_ap[:, c0:c1],
                            writes=writes, tag=f"{tag}_{c0}")
        return r

    def emit_ln(self, pref, stat_sx, stat_sq, stat_tokens, SM, bcA, bcB,
                bufs, n_d=D):
        em = self.em
        btok, (s_srow, s_st, s_ab, s_t8, s_arow, s_brow) = bufs
        nc8 = 2 * SM // 128
        nc4 = nc8 // 2
        def sevfn(eng):
            eng.copy(s_srow[:, 0:SM], stat_sx)
            return eng.copy(s_srow[:, SM:2 * SM], stat_sq)
        em.step("act", sevfn, reads=tuple(stat_tokens),
                writes=(btok + "_srow",), tag=pref + "_sev")
        em.dma("sp", s_st[:, 0:nc4],
               s_srow[:, 0:SM].rearrange("o (p c) -> o p c", c=nc4),
               reads=(btok + "_srow",), writes=(btok + "_st",),
               tag=pref + "_gst1")
        em.dma("sp", s_st[:, nc4:nc8],
               s_srow[:, SM:2 * SM].rearrange("o (p c) -> o p c", c=nc4),
               reads=(btok + "_srow",), writes=(btok + "_st",),
               tag=pref + "_gst2")
        def stfn(eng):
            st, ab, t8 = s_st, s_ab, s_t8
            eng.tensor_scalar_mul(t8[:, 0:nc4], st[:, 0:nc4], 1.0 / n_d)
            eng.tensor_scalar_mul(t8[:, nc4:nc8], st[:, nc4:nc8], 1.0 / n_d)
            eng.drain()
            eng.tensor_mul(ab[:, 0:nc4], t8[:, 0:nc4], t8[:, 0:nc4])
            eng.drain()
            eng.tensor_sub(ab[:, nc4:nc8], t8[:, nc4:nc8], ab[:, 0:nc4])
            eng.drain()
            return eng.tensor_scalar_add(ab[:, nc4:nc8], ab[:, nc4:nc8], 1e-5)
        em.step("dve", stfn, reads=(btok + "_st",),
                writes=(btok + "_ab", btok + "_t8"), tag=pref + "_stat",
                small=True)
        def rstdfn(eng):
            eng.activation(s_ab[:, nc4:nc8], s_ab[:, nc4:nc8], AF.Ln)
            eng.drain()
            return eng.activation(s_ab[:, 0:nc4], s_ab[:, nc4:nc8], AF.Exp,
                                  scale=-0.5)
        em.step("act", rstdfn, reads=(btok + "_ab",), writes=(btok + "_ab",),
                tag=pref + "_rstd", small=True)
        em.step("dve", lambda eng: eng.tensor_mul(
            s_ab[:, nc4:nc8], s_t8[:, 0:nc4], s_ab[:, 0:nc4]),
            reads=(btok + "_ab", btok + "_t8"), writes=(btok + "_ab",),
            tag=pref + "_b", small=True)
        em.dma("pool", s_arow[:, 0:SM].rearrange("o (p c) -> o p c", c=nc4),
               s_ab[:, 0:nc4], reads=(btok + "_ab",),
               writes=(btok + "_arow",), tag=pref + "_sca")
        em.dma("pool", s_brow[:, 0:SM].rearrange("o (p c) -> o p c", c=nc4),
               s_ab[:, nc4:nc8], reads=(btok + "_ab",),
               writes=(btok + "_brow",), tag=pref + "_scb")
        bcA_ap, bcA_bank, bcA_start = bcA
        bcB_ap, bcB_bank, bcB_start = bcB
        def bcfn(eng):
            eng.matmul(bcA_ap, self.sb["k_ones_1x128"][:],
                       s_arow[:, 0:SM], start=bcA_start, stop=True,
                       skip_group_check=True)
            return eng.matmul(bcB_ap, self.sb["k_ones_1x128"][:],
                              s_brow[:, 0:SM], start=bcB_start, stop=True,
                              skip_group_check=True)
        em.step("pe", bcfn,
                reads=(btok + "_arow", btok + "_brow", "k_ones_1x128"),
                writes=(bcA_bank, bcB_bank), tag=pref + "_bc")

    def ln_bufs(self, pref, SMmax):
        sbn = self._sbuf
        nc8 = 2 * SMmax // 128
        return (pref, (sbn(pref + "_srow", [1, 2 * SMmax], F32),
                sbn(pref + "_st", [128, nc8], F32),
                sbn(pref + "_ab", [128, nc8], F32),
                sbn(pref + "_t8", [128, nc8], F32),
                sbn(pref + "_arow", [1, SMmax], BF16),
                sbn(pref + "_brow", [1, SMmax], BF16)))

    def phase_m1(self):
        em, dr = self.em, self.dram
        sbn = self._sbuf
        SM = 512
        NS = T // SM
        self.push_scope()
        s_xin = [sbn(f"m1_xin{p}", [128, 4 * 1024], F32) for p in range(2)]
        s_x0 = [sbn(f"m1_x0{p}", [128, FC * SM], F32) for p in range(2)]
        s_sqt = [sbn(f"m1_sqt{p}", [128, SM], BF16) for p in range(2)]
        s_xbt = [sbn(f"m1_xbt{p}", [128, SM], BF16) for p in range(2)]
        lnb = self.ln_bufs("m1ln", SM)
        s_xn = [sbn(f"m1_xn{p}", [128, FC * SM], BF16) for p in range(2)]
        s_nf = sbn("m1_nf", [128, SM], F32)
        s_xc = [sbn(f"m1_xc{p}", [128, IC * (SM + 3)], BF16) for p in range(2)]
        s_ut = [sbn(f"m1_ut{p}", [128, SM], BF16) for p in range(2)]
        s_zt = [sbn(f"m1_zt{p}", [128, SM], BF16) for p in range(2)]
        s_cv = [sbn(f"m1_cv{p}", [128, SM], BF16) for p in range(2)]
        s_win = sbn("m1_win", [128, 256 * 128], BF16)
        s_convw = sbn("m1_convw", [128, IC * 4], F32)
        self.cast_load_split(s_win, dr["w_in"], 256 * 128,
                             writes=("m1_win",), tag="ld_w_in")
        em.dma("sp", s_convw[:], dr["conv_w"], writes=("m1_convw",),
               tag="ld_convw")

        hx0 = dr["hx0"].rearrange("(fc p) t -> p fc t", p=128)
        hu = dr["hu"].rearrange("(ic p) t -> p ic t", p=128)
        hsz = dr["hsz"].rearrange("(ic p) t -> p ic t", p=128)

        def stripe(ms):
            q, si = divmod(ms, L // SM)
            g0 = q * L + si * SM
            p = ms % 2
            tok = lambda n: f"{n}{p}"
            em.dma("sp",
                   s_xin[p][:].rearrange("p (tb f) -> p tb f", tb=4),
                   dr["x_in"][g0:g0 + SM, :].rearrange(
                       "(tb p) f -> p tb f", p=128),
                   writes=(tok("m1_xin"),), tag=f"m1_ld{ms}")
            for fc in range(FC):
                b = f"pb{fc % 2}"
                def tfn(eng, p_=p, fc_=fc, b_=b):
                    r = None
                    for tb in range(4):
                        r = eng.transpose(
                            self.ps[b_][:, tb * 128:(tb + 1) * 128],
                            s_xin[p_][:, tb * 1024 + fc_ * 128:
                                      tb * 1024 + (fc_ + 1) * 128],
                            self.sb["k_ident_f32"][:])
                    return r
                em.step("pe", tfn, reads=(tok("m1_xin"), "k_ident_f32"),
                        writes=(b,), tag=f"m1_tr{ms}_{fc}")
                em.step("act", lambda eng, p_=p, fc_=fc, b_=b: eng.copy(
                    s_x0[p_][:, fc_ * SM:(fc_ + 1) * SM], self.ps[b_][:]),
                    reads=(b,), writes=(tok("m1_x0"),), tag=f"m1_ev{ms}_{fc}")
                pf = fc % 2
                em.step("dve", lambda eng, p_=p, fc_=fc, pf_=pf:
                        eng.tensor_mul(
                            s_sqt[pf_][:],
                            s_x0[p_][:, fc_ * SM:(fc_ + 1) * SM],
                            s_x0[p_][:, fc_ * SM:(fc_ + 1) * SM]),
                        reads=(tok("m1_x0"),), writes=(f"m1_sqt{pf}",),
                        tag=f"m1_sq{ms}_{fc}")
                em.step("dve", lambda eng, p_=p, fc_=fc, pf_=pf:
                        eng.tensor_copy(
                            s_xbt[pf_][:],
                            s_x0[p_][:, fc_ * SM:(fc_ + 1) * SM]),
                        reads=(tok("m1_x0"),), writes=(f"m1_xbt{pf}",),
                        tag=f"m1_xb{ms}_{fc}")
                em.step("pe", lambda eng, fc_=fc, pf_=pf: eng.matmul(
                    self.ps["pb2"][0:1, :], self.sb["k_ones_128x1"][:],
                    s_xbt[pf_][:], start=(fc_ == 0), stop=(fc_ == FC - 1),
                    skip_group_check=True),
                    reads=(f"m1_xbt{pf}", "k_ones_128x1"), writes=("pb2",),
                    tag=f"m1_msx{ms}_{fc}")
                em.step("pe", lambda eng, fc_=fc, pf_=pf: eng.matmul(
                    self.ps["pb3"][0:1, :], self.sb["k_ones_128x1"][:],
                    s_sqt[pf_][:], start=(fc_ == 0), stop=(fc_ == FC - 1),
                    skip_group_check=True),
                    reads=(f"m1_sqt{pf}", "k_ones_128x1"), writes=("pb3",),
                    tag=f"m1_msq{ms}_{fc}")
            self.emit_ln(f"m1ln{ms}", self.ps["pb2"][0:1, :],
                         self.ps["pb3"][0:1, :], ("pb2", "pb3"), SM,
                         (self.ps["pb4"][:], "pb4", True),
                         (self.ps["pb5"][:], "pb5", True), lnb)
            for fc in range(FC):
                em.step("dve", lambda eng, p_=p, fc_=fc: eng.tensor_mul(
                    s_nf[:], s_x0[p_][:, fc_ * SM:(fc_ + 1) * SM],
                    self.ps["pb4"][:]),
                    reads=(tok("m1_x0"), "pb4"), writes=("m1_nf",),
                    tag=f"m1_n1{ms}_{fc}")
                em.step("dve", lambda eng, p_=p, fc_=fc: eng.tensor_sub(
                    s_xn[p_][:, fc_ * SM:(fc_ + 1) * SM], s_nf[:],
                    self.ps["pb5"][:]),
                    reads=("m1_nf", "pb5"), writes=(tok("m1_xn"),),
                    tag=f"m1_n2{ms}_{fc}")
            if DBG:
                lnbufs = lnb[1]
                em.dma("sp", dr["dbg_st"].rearrange(
                    "(s p) c -> p s c", p=128)[:, ms], lnbufs[1][:],
                    reads=("m1ln_st",), writes=(f"dbgst{ms}",),
                    tag=f"dbg_st{ms}")
                em.dma("sp", dr["dbg_ab"].rearrange(
                    "(s p) c -> p s c", p=128)[:, ms], lnbufs[2][:],
                    reads=("m1ln_ab",), writes=(f"dbgab{ms}",),
                    tag=f"dbg_ab{ms}")
                em.dma("sp", dr["dbg_row"][ms:ms + 1, 0:1024],
                       lnbufs[0][:], reads=("m1ln_srow",),
                       writes=(f"dbgrow{ms}",), tag=f"dbg_row{ms}")
                if ms == 0:
                    em.dma("sp", dr["dbg_xn"], s_xn[p][:],
                           reads=(tok("m1_xn"),), writes=("dbgxn",),
                           tag="dbg_xn")
            em.dma("sp", hx0[:, :, g0:g0 + SM],
                   s_x0[p][:].rearrange("p (fc t) -> p fc t", fc=FC),
                   reads=(tok("m1_x0"),), writes=(f"hx0_{ms}",),
                   tag=f"m1_ox0{ms}")
            for oc in range(32):
                b = f"pb{6 + (oc % 2)}"
                def wfn(eng, p_=p, oc_=oc, b_=b):
                    r = None
                    for kc in range(FC):
                        r = eng.matmul(
                            self.ps[b_][:],
                            self.sb["m1_win"][:, (kc * 32 + oc_) * 128:
                                              (kc * 32 + oc_ + 1) * 128],
                            s_xn[p_][:, kc * SM:(kc + 1) * SM],
                            start=(kc == 0), stop=(kc == FC - 1))
                    return r
                em.step("pe", wfn, reads=(tok("m1_xn"), "m1_win"),
                        writes=(b,), tag=f"m1_w{ms}_{oc}")
                if oc < IC:
                    em.step("act", lambda eng, p_=p, oc_=oc, b_=b: eng.copy(
                        s_xc[p_][:, oc_ * (SM + 3) + 3:
                                 oc_ * (SM + 3) + 3 + SM], self.ps[b_][:]),
                        reads=(b,), writes=(tok("m1_xc"),),
                        tag=f"m1_exc{ms}_{oc}")
                else:
                    zt = oc % 2
                    em.step("act", lambda eng, zt_=zt, b_=b: eng.activation(
                        s_zt[zt_][:], self.ps[b_][:], AF.Silu),
                        reads=(b,), writes=(f"m1_zt{zt}",),
                        tag=f"m1_esz{ms}_{oc}")
                    em.dma("sp", hsz[:, oc - IC, g0:g0 + SM], s_zt[zt][:],
                           reads=(f"m1_zt{zt}",), writes=(f"hsz_{ms}_{oc}",),
                           tag=f"m1_osz{ms}_{oc}")
            if si == 0:
                em.step("dve", lambda eng, p_=p: eng.memset(
                    s_xc[p_][:].rearrange(
                        "p (ic c) -> p ic c", ic=IC)[:, :, 0:3], 0.0),
                    writes=(tok("m1_xc"),), tag=f"m1_cz{ms}", small=True)
            else:
                def cfn(eng, p_=p):
                    src = s_xc[1 - p_][:].rearrange(
                        "p (ic c) -> p ic c", ic=IC)[:, :, SM:SM + 3]
                    dst = s_xc[p_][:].rearrange(
                        "p (ic c) -> p ic c", ic=IC)[:, :, 0:3]
                    return eng.tensor_copy(dst, src)
                em.step("dve", cfn, reads=(f"m1_xc{1 - p}",),
                        writes=(tok("m1_xc"),), tag=f"m1_cc{ms}", small=True)
            for ic in range(IC):
                pc = ic % 2
                def cvfn(eng, p_=p, ic_=ic, pc_=pc):
                    base = ic_ * (SM + 3)
                    xc_ = s_xc[p_]
                    cw = self.sb["m1_convw"]
                    eng.tensor_scalar_mul(
                        s_cv[pc_][:], xc_[:, base:base + SM],
                        cw[:, ic_ * 4:ic_ * 4 + 1])
                    r = None
                    for dd in range(1, 4):
                        r = eng.scalar_tensor_tensor(
                            s_cv[pc_][:], xc_[:, base + dd:base + dd + SM],
                            cw[:, ic_ * 4 + dd:ic_ * 4 + dd + 1],
                            s_cv[pc_][:], ALU.mult, ALU.add)
                    return r
                em.step("dve", cvfn, reads=(tok("m1_xc"), "m1_convw"),
                        writes=(f"m1_cv{pc}",), tag=f"m1_cv{ms}_{ic}")
                em.step("act", lambda eng, ic_=ic, pc_=pc, p_=p:
                        eng.activation(s_ut[pc_][:], s_cv[pc_][:], AF.Silu),
                        reads=(f"m1_cv{pc}",), writes=(f"m1_ut{pc}",),
                        tag=f"m1_su{ms}_{ic}")
                em.dma("sp", hu[:, ic, g0:g0 + SM], s_ut[pc][:],
                       reads=(f"m1_ut{pc}",), writes=(f"hu_{ms}_{ic}",),
                       tag=f"m1_ou{ms}_{ic}")
        for k0 in range(0, NS, 2):
            self.emit_pair(stripe, k0)
        self.pop_scope()

    def phase_m2(self):
        em, dr = self.em, self.dram
        sbn = self._sbuf
        SM = 256
        NS = T // SM
        self.push_scope()
        s_u = [sbn(f"m2_u{p}", [128, IC * SM], BF16) for p in range(2)]
        s_sz = [sbn(f"m2_sz{p}", [128, IC * SM], BF16) for p in range(2)]
        s_x0 = [sbn(f"m2_x0{p}", [128, FC * SM], F32) for p in range(2)]
        s_xp = [sbn(f"m2_xp{p}", [16, 2 * SM], BF16) for p in range(2)]
        s_e = [sbn(f"m2_e{p}", [128, SM], F32) for p in range(2)]
        s_sp = [sbn(f"m2_sp{p}", [128, SM], F32) for p in range(2)]
        s_A = [sbn(f"m2_A{p}", [128, SM], F32) for p in range(2)]
        s_ub = [sbn(f"m2_ub{p}", [128, SM], BF16) for p in range(2)]
        s_ys = sbn("m2_ys", [128, SM], F32)
        s_yt = sbn("m2_yt", [128, SM], F32)
        s_y = [sbn(f"m2_y{p}", [128, IC * SM], BF16) for p in range(2)]
        s_h = sbn("m2_h", [128, IC], F32)
        s_m = [sbn(f"m2_m{p}", [128, FC * SM], F32) for p in range(2)]
        s_mq = [sbn(f"m2_mq{p}", [128, SM], BF16) for p in range(2)]
        s_x1 = [sbn(f"m2_x1{p}", [128, FC * SM], F32) for p in range(2)]
        s_x2 = [sbn(f"m2_x2{p}", [128, FC * SM], F32) for p in range(2)]
        s_nf = sbn("m2_nf", [128, SM], F32)
        lnb2 = self.ln_bufs("m2ln2", SM)
        lnb3 = self.ln_bufs("m2ln3", SM)
        s_wx = sbn("m2_wx", [128, IC * 32], BF16)
        s_wdt = sbn("m2_wdt", [16, IC * 128], BF16)
        s_wbp = sbn("m2_wbp", [16, IC * 128], BF16)
        s_bdt = sbn("m2_bdt", [128, IC], F32)
        s_dp = sbn("m2_dp", [128, IC], F32)
        s_wout = sbn("m2_wout", [128, IC * FC * 128], BF16)
        s_wsum = sbn("m2_wsum", [128, IC], BF16)
        s_wsf = sbn("m2_wsf", [128, IC], F32)
        self.cast_load(s_wx[:], dr["w_x"], writes=("m2_wx",), tag="ld_w_x")
        self.cast_load(s_wdt[:], dr["w_dt"], writes=("m2_wdt",),
                       tag="ld_w_dt")
        self.cast_load(s_wbp[:], dr["w_bp"], writes=("m2_wbp",),
                       tag="ld_w_bp")
        em.dma("sp", s_bdt[:], dr["b_dt"], writes=("m2_bdt",), tag="ld_bdt")
        em.dma("sp", s_dp[:], dr["d_param"], writes=("m2_dp",), tag="ld_dp")
        self.cast_load(s_wout[:], dr["w_out"], writes=("m2_wout",),
                       tag="ld_w_out")
        def wsfn(eng):
            for kc in range(IC):
                eng.tensor_reduce(
                    self.sb["m2_wsf"][:, kc:kc + 1],
                    self.sb["m2_wout"][:, kc * FC * 128:(kc + 1) * FC * 128],
                    AX.X, ALU.add)
            eng.drain()
            return eng.tensor_copy(self.sb["m2_wsum"][:], self.sb["m2_wsf"][:])
        em.step("dve", wsfn, reads=("m2_wout",), writes=("m2_wsum",),
                tag="m2_wsum", small=True)

        hu = dr["hu"].rearrange("(ic p) t -> p ic t", p=128)
        hsz = dr["hsz"].rearrange("(ic p) t -> p ic t", p=128)
        hx0 = dr["hx0"].rearrange("(fc p) t -> p fc t", p=128)
        hx1 = dr["hx1"].rearrange("(fc p) t -> p fc t", p=128)
        hx2a = dr["hx2a"].rearrange("(fc p) t -> p fc t", p=128)

        def stripe(ms):
            q, si = divmod(ms, L // SM)
            g0 = q * L + si * SM
            p = ms % 2
            tok = lambda n: f"{n}{p}"
            em.dma("sp", s_u[p][:].rearrange("p (ic t) -> p ic t", ic=IC),
                   hu[:, :, g0:g0 + SM], reads=(f"hu_fin",),
                   writes=(tok("m2_u"),), tag=f"m2_lu{ms}")
            em.dma("sp", s_sz[p][:].rearrange("p (ic t) -> p ic t", ic=IC),
                   hsz[:, :, g0:g0 + SM], reads=(f"hsz_fin",),
                   writes=(tok("m2_sz"),), tag=f"m2_lsz{ms}")
            em.dma("sp", s_x0[p][:].rearrange("p (fc t) -> p fc t", fc=FC),
                   hx0[:, :, g0:g0 + SM], reads=(f"hx0_fin",),
                   writes=(tok("m2_x0"),), tag=f"m2_lx0{ms}")
            def xpfn(eng, p_=p):
                r = None
                for kc in range(IC):
                    eng.matmul(
                        self.ps["pb0"][0:16, 0:SM],
                        self.sb["m2_wx"][:, kc * 32:kc * 32 + 16],
                        s_u[p_][:, kc * SM:(kc + 1) * SM],
                        start=(kc == 0), stop=(kc == IC - 1),
                        skip_group_check=True)
                    r = eng.matmul(
                        self.ps["pb0"][0:16, SM:2 * SM],
                        self.sb["m2_wx"][:, kc * 32 + 16:kc * 32 + 32],
                        s_u[p_][:, kc * SM:(kc + 1) * SM],
                        start=False, stop=(kc == IC - 1),
                        skip_group_check=True)
                return r
            em.step("pe", xpfn, reads=(tok("m2_u"), "m2_wx"), writes=("pb0",),
                    tag=f"m2_xp{ms}")
            em.step("act", lambda eng, p_=p: eng.copy(
                s_xp[p_][:], self.ps["pb0"][0:16, 0:2 * SM]),
                reads=("pb0",), writes=(tok("m2_xp"),), tag=f"m2_exp{ms}")
            if si == 0:
                em.step("dve", lambda eng: eng.memset(s_h[:], 0.0),
                        writes=("m2_h",), tag=f"m2_h0{ms}", small=True)
            for ic in range(IC):
                bdt = f"pb{1 + (ic % 2)}"
                bbp = f"pb{3 + (ic % 2)}"
                pc = ic % 2
                em.step("pe", lambda eng, p_=p, ic_=ic, b_=bdt: eng.matmul(
                    self.ps[b_][:, 0:SM],
                    self.sb["m2_wdt"][:, ic_ * 128:(ic_ + 1) * 128],
                    s_xp[p_][:, 0:SM], start=True, stop=True),
                    reads=(tok("m2_xp"), "m2_wdt"), writes=(bdt,),
                    tag=f"m2_dt{ms}_{ic}")
                em.step("pe", lambda eng, p_=p, ic_=ic, b_=bbp: eng.matmul(
                    self.ps[b_][:, 0:SM],
                    self.sb["m2_wbp"][:, ic_ * 128:(ic_ + 1) * 128],
                    s_xp[p_][:, SM:2 * SM], start=True, stop=True),
                    reads=(tok("m2_xp"), "m2_wbp"), writes=(bbp,),
                    tag=f"m2_bp{ms}_{ic}")
                em.step("act", lambda eng, ic_=ic, pc_=pc, b_=bdt:
                        eng.activation(s_e[pc_][:], self.ps[b_][:, 0:SM],
                                       AF.Exp,
                                       bias=self.sb["m2_bdt"][:, ic_:ic_ + 1]),
                        reads=(bdt, "m2_bdt"), writes=(f"m2_e{pc}",),
                        tag=f"m2_sp1{ms}_{ic}")
                em.step("act", lambda eng, pc_=pc: eng.activation(
                    s_sp[pc_][:], s_e[pc_][:], AF.Ln, bias=1.0),
                    reads=(f"m2_e{pc}",), writes=(f"m2_sp{pc}",),
                    tag=f"m2_sp2{ms}_{ic}")
                def clipfn(eng, pc_=pc):
                    eng.tensor_scalar(s_A[pc_][:], s_sp[pc_][:], 1e-4, 1.0,
                                      ALU.max, ALU.min)
                    return eng.tensor_scalar(s_A[pc_][:], s_A[pc_][:], -1.0,
                                             1.0, ALU.mult, ALU.add)
                em.step("dve", clipfn, reads=(f"m2_sp{pc}",),
                        writes=(f"m2_A{pc}",), tag=f"m2_cl{ms}_{ic}")
                em.step("dve", lambda eng, p_=p, ic_=ic, pc_=pc, b_=bbp:
                        eng.tensor_mul(
                            s_ub[pc_][:], self.ps[b_][:, 0:SM],
                            s_u[p_][:, ic_ * SM:(ic_ + 1) * SM]),
                        reads=(bbp, tok("m2_u")), writes=(f"m2_ub{pc}",),
                        tag=f"m2_ub{ms}_{ic}")
                def scfn(eng, ic_=ic, pc_=pc):
                    eng.tensor_tensor_scan(
                        s_ys[:], s_A[pc_][:], s_ub[pc_][:],
                        s_h[:, ic_:ic_ + 1], ALU.mult, ALU.add)
                    eng.drain()
                    return eng.tensor_copy(s_h[:, ic_:ic_ + 1],
                                           s_ys[:, SM - 1:SM])
                em.step("dve", scfn,
                        reads=(f"m2_A{pc}", f"m2_ub{pc}", "m2_h"),
                        writes=("m2_ys", "m2_h"), tag=f"m2_sc{ms}_{ic}")
                def yfn(eng, p_=p, ic_=ic):
                    eng.scalar_tensor_tensor(
                        s_yt[:], s_u[p_][:, ic_ * SM:(ic_ + 1) * SM],
                        self.sb["m2_dp"][:, ic_:ic_ + 1], s_ys[:],
                        ALU.mult, ALU.add)
                    return eng.tensor_mul(
                        s_y[p_][:, ic_ * SM:(ic_ + 1) * SM], s_yt[:],
                        s_sz[p_][:, ic_ * SM:(ic_ + 1) * SM])
                em.step("dve", yfn,
                        reads=("m2_ys", tok("m2_u"), tok("m2_sz"), "m2_dp"),
                        writes=(tok("m2_y"), "m2_yt"), tag=f"m2_y{ms}_{ic}")
            def msfn(eng, p_=p):
                r = None
                for kc in range(IC):
                    r = eng.matmul(
                        self.ps["pb7"][0:1, 0:SM],
                        self.sb["m2_wsum"][:, kc:kc + 1],
                        s_y[p_][:, kc * SM:(kc + 1) * SM],
                        start=(kc == 0), stop=(kc == IC - 1),
                        skip_group_check=True)
                return r
            em.step("pe", msfn, reads=(tok("m2_y"), "m2_wsum"),
                    writes=("pb7",), tag=f"m2_msum{ms}")
            for oc in range(FC):
                b = f"pb{5 + (oc % 2)}"
                def wofn(eng, p_=p, oc_=oc, b_=b):
                    r = None
                    for kc in range(IC):
                        r = eng.matmul(
                            self.ps[b_][:, 0:SM],
                            self.sb["m2_wout"][:, (kc * FC + oc_) * 128:
                                               (kc * FC + oc_ + 1) * 128],
                            s_y[p_][:, kc * SM:(kc + 1) * SM],
                            start=(kc == 0), stop=(kc == IC - 1))
                    return r
                em.step("pe", wofn, reads=(tok("m2_y"), "m2_wout"),
                        writes=(b,), tag=f"m2_wo{ms}_{oc}")
                em.step("act", lambda eng, p_=p, oc_=oc, b_=b: eng.copy(
                    s_m[p_][:, oc_ * SM:(oc_ + 1) * SM],
                    self.ps[b_][:, 0:SM]),
                    reads=(b,), writes=(tok("m2_m"),), tag=f"m2_em{ms}_{oc}")
                pc = oc % 2
                em.step("dve", lambda eng, p_=p, oc_=oc, pc_=pc:
                        eng.tensor_mul(
                            s_mq[pc_][:],
                            s_m[p_][:, oc_ * SM:(oc_ + 1) * SM],
                            s_m[p_][:, oc_ * SM:(oc_ + 1) * SM]),
                        reads=(tok("m2_m"),), writes=(f"m2_mq{pc}",),
                        tag=f"m2_mq{ms}_{oc}")
                em.step("pe", lambda eng, oc_=oc, pc_=pc: eng.matmul(
                    self.ps["pb7"][0:1, SM:2 * SM],
                    self.sb["k_ones_128x1"][:], s_mq[pc_][:],
                    start=False, stop=(oc_ == FC - 1),
                    skip_group_check=True),
                    reads=(f"m2_mq{pc}", "k_ones_128x1"), writes=("pb7",),
                    tag=f"m2_mq2{ms}_{oc}")
            self.emit_ln(f"m2ln2_{ms}", self.ps["pb7"][0:1, 0:SM],
                         self.ps["pb7"][0:1, SM:2 * SM], ("pb7",), SM,
                         (self.ps["pb1"][:, 0:SM], "pb1", True),
                         (self.ps["pb2"][:, 0:SM], "pb2", True), lnb2)
            for fc in range(FC):
                def x1fn(eng, p_=p, fc_=fc):
                    eng.tensor_mul(s_nf[:],
                                   s_m[p_][:, fc_ * SM:(fc_ + 1) * SM],
                                   self.ps["pb1"][:, 0:SM])
                    eng.tensor_sub(s_nf[:], s_nf[:], self.ps["pb2"][:, 0:SM])
                    return eng.tensor_add(
                        s_x1[p_][:, fc_ * SM:(fc_ + 1) * SM], s_nf[:],
                        s_x0[p_][:, fc_ * SM:(fc_ + 1) * SM])
                em.step("dve", x1fn,
                        reads=(tok("m2_m"), "pb1", "pb2", tok("m2_x0")),
                        writes=(tok("m2_x1"), "m2_nf"), tag=f"m2_x1{ms}_{fc}")
                pc = fc % 2
                em.step("dve", lambda eng, p_=p, fc_=fc, pc_=pc:
                        eng.tensor_copy(
                            s_ub[pc_][:],
                            s_x1[p_][:, fc_ * SM:(fc_ + 1) * SM]),
                        reads=(tok("m2_x1"),), writes=(f"m2_ub{pc}",),
                        tag=f"m2_x1b{ms}_{fc}")
                em.step("pe", lambda eng, fc_=fc, pc_=pc: eng.matmul(
                    self.ps["pb3"][0:1, 0:SM],
                    self.sb["k_ones_128x1"][:], s_ub[pc_][:],
                    start=(fc_ == 0), stop=(fc_ == FC - 1),
                    skip_group_check=True),
                    reads=(f"m2_ub{pc}", "k_ones_128x1"), writes=("pb3",),
                    tag=f"m2_x1s{ms}_{fc}")
                em.step("dve", lambda eng, p_=p, fc_=fc, pc_=pc:
                        eng.tensor_mul(
                            s_mq[pc_][:],
                            s_x1[p_][:, fc_ * SM:(fc_ + 1) * SM],
                            s_x1[p_][:, fc_ * SM:(fc_ + 1) * SM]),
                        reads=(tok("m2_x1"),), writes=(f"m2_mq{pc}",),
                        tag=f"m2_x1q{ms}_{fc}")
                em.step("pe", lambda eng, fc_=fc, pc_=pc: eng.matmul(
                    self.ps["pb3"][0:1, SM:2 * SM],
                    self.sb["k_ones_128x1"][:], s_mq[pc_][:],
                    start=False, stop=(fc_ == FC - 1),
                    skip_group_check=True),
                    reads=(f"m2_mq{pc}", "k_ones_128x1"), writes=("pb3",),
                    tag=f"m2_x1q2{ms}_{fc}")
            em.dma("sp", hx1[:, :, g0:g0 + SM],
                   s_x1[p][:].rearrange("p (fc t) -> p fc t", fc=FC),
                   reads=(tok("m2_x1"),), writes=(f"hx1_{ms}",),
                   tag=f"m2_ox1{ms}")
            self.emit_ln(f"m2ln3_{ms}", self.ps["pb3"][0:1, 0:SM],
                         self.ps["pb3"][0:1, SM:2 * SM], ("pb3",), SM,
                         (self.ps["pb4"][:, 0:SM], "pb4", True),
                         (self.ps["pb5"][:, 0:SM], "pb5", True), lnb3)
            for fc in range(FC):
                def x2fn(eng, p_=p, fc_=fc):
                    eng.tensor_mul(s_nf[:],
                                   s_x1[p_][:, fc_ * SM:(fc_ + 1) * SM],
                                   self.ps["pb4"][:, 0:SM])
                    return eng.tensor_sub(
                        s_x2[p_][:, fc_ * SM:(fc_ + 1) * SM], s_nf[:],
                        self.ps["pb5"][:, 0:SM])
                em.step("dve", x2fn,
                        reads=(tok("m2_x1"), "pb4", "pb5"),
                        writes=(tok("m2_x2"), "m2_nf"), tag=f"m2_x2{ms}_{fc}")
            em.dma("sp", hx2a[:, :, g0:g0 + SM],
                   s_x2[p][:].rearrange("p (fc t) -> p fc t", fc=FC),
                   reads=(tok("m2_x2"),), writes=(f"hx2_{ms}",),
                   tag=f"m2_ox2{ms}")
        for k0 in range(0, NS, 2):
            self.emit_pair(stripe, k0)
        self.pop_scope()


    def section_begin(self):
        em, dr = self.em, self.dram
        sbn = self._sbuf
        self.push_scope()
        sbn("sec_wqkv", [128, 192 * 128], BF16)
        sbn("sec_wo", [128, 64 * 128], BF16)
        sbn("sec_Rq", [64, 64], BF16)
        sbn("sec_Rk", [64, 64], BF16)
        sbn("sec_rot", [64, 64], BF16)
        sbn("sec_qs2", [64, 6], F32)
        sbn("sec_ks2", [64, 6], F32)
        sbn("sec_selsum", [64, 256], BF16)
        sbn("sec_selbc", [16, 1024], BF16)
        sbn("sec_maskq", [128, 3 * 384], BF16)
        sbn("sec_maskt", [128, 4 * 256], BF16)
        for name, src in [("sec_rot", "c_rot"), ("sec_selsum", "c_selsum"),
                          ("sec_selbc", "c_selbc")]:
            em.dma("sp", self.sb[name][:], dr[src], writes=(name,),
                   tag="ld_" + name)
        em.dma("sp", self.sb["sec_qs2"][:], dr["qs2"], writes=("sec_qs2",),
               tag="ld_qs2")
        em.dma("sp", self.sb["sec_ks2"][:], dr["ks2"], writes=("sec_ks2",),
               tag="ld_ks2")
        em.dma("sp", self.sb["sec_maskq"][:].rearrange(
            "p (v n) -> p v n", v=3),
            dr["c_maskq"].rearrange("(v p) n -> p v n", p=128),
            writes=("sec_maskq",), tag="ld_maskq")
        em.dma("sp", self.sb["sec_maskt"][:].rearrange(
            "p (v n) -> p v n", v=4),
            dr["c_maskt"].rearrange("(v p) n -> p v n", p=128),
            writes=("sec_maskt",), tag="ld_maskt")
        self.x2cur = "hx2a"
        self.x2nxt = "hx2b"

    def section_end(self):
        self.pop_scope()

    def load_layer_weights(self, li):
        em, dr = self.em, self.dram
        self.cast_load_split(self.sb["sec_wqkv"],
                             dr["wqkv"][li * 128:(li + 1) * 128, :],
                             192 * 128, writes=("sec_wqkv",),
                             tag=f"ld_wqkv{li}")
        self.cast_load_split(self.sb["sec_wo"],
                             dr["wo"][li * 128:(li + 1) * 128, :],
                             64 * 128, writes=("sec_wo",), tag=f"ld_wo{li}")
        em.step("dve", lambda eng, li_=li: eng.tensor_scalar_mul(
            self.sb["sec_Rq"][:], self.sb["sec_rot"][:],
            self.sb["sec_qs2"][:, li_:li_ + 1]),
            reads=("sec_rot", "sec_qs2"), writes=("sec_Rq",),
            tag=f"bld_Rq{li}")
        em.step("dve", lambda eng, li_=li: eng.tensor_scalar_mul(
            self.sb["sec_Rk"][:], self.sb["sec_rot"][:],
            self.sb["sec_ks2"][:, li_:li_ + 1]),
            reads=("sec_rot", "sec_ks2"), writes=("sec_Rk",),
            tag=f"bld_Rk{li}")

    def phase_attn(self, li):
        em, dr = self.em, self.dram
        sbn = self._sbuf
        FS = 512
        WS = 256
        NF = T // FS
        NW = T // WS
        WPS = L // WS
        P = f"a{li}"
        self.push_scope()
        if li == 0:
            self.load_layer_weights(0)
        s_xsh = [sbn(f"{P}_xsh{p}", [128, FC * FS], BF16) for p in range(2)]
        s_qaug = [sbn(f"{P}_qaug{p}", [66, NH * FS], BF16) for p in range(2)]
        s_kaug = [sbn(f"{P}_kaug{p}", [66, NH * 768], BF16) for p in range(2)]
        s_vt = sbn(f"{P}_vt", [128, 6 * 1024], BF16)
        s_sq = [sbn(f"{P}_sq{p}", [64, FS], BF16) for p in range(2)]
        s_tA = sbn(f"{P}_tA", [64, FS], BF16)
        s_tC = sbn(f"{P}_tC", [64, FS], BF16)
        s_sA = sbn(f"{P}_sA", [64, FS], BF16)
        s_sC = sbn(f"{P}_sC", [64, FS], BF16)
        s_nq = sbn(f"{P}_nq", [16, FS], F32)
        s_rn = sbn(f"{P}_rn", [16, FS], BF16)
        s_mcol = [sbn(f"{P}_mcol{w_}", [128, 32], F32) for w_ in range(2)]
        s_scol = [sbn(f"{P}_scol{w_}", [128, 32], F32) for w_ in range(2)]
        s_r12 = sbn(f"{P}_r12", [128, 64], F32)
        s_rT = sbn(f"{P}_rT", [64, 128], F32)
        s_et = [sbn(f"{P}_et{w_}", [128, 2 * 256], BF16) for w_ in range(2)]
        s_ao = [sbn(f"{P}_ao{w_}", [128, FC * WS], BF16) for w_ in range(2)]
        s_xr = [[sbn(f"{P}_xr{w_}_{j}", [128, WS], F32) for j in range(2)]
                for w_ in range(2)]
        s_xo = [sbn(f"{P}_xo{w_}", [128, WS], F32) for w_ in range(2)]
        hxc = dr[self.x2cur].rearrange("(fc p) t -> p fc t", p=128)
        hxn = dr[self.x2nxt].rearrange("(fc p) t -> p fc t", p=128)
        for p in range(2):
            em.step("dve", lambda eng, p_=p: eng.memset(
                s_kaug[p_][64:66, :], 1.0), writes=(f"{P}_kones{p}",),
                tag=f"{P}_kones{p}")

        def slot_of(kk, h, p):
            if kk == 0:
                return (s_qaug[p][0:64, h * FS:(h + 1) * FS],
                        f"{P}_q{p}_{h}")
            return (s_kaug[p][0:64, :].rearrange(
                "pp (hh t) -> pp hh t", hh=NH)[:, h, 256:768],
                f"{P}_k{p}_{h}")

        def front(ss):
            q, fi = divmod(ss, L // FS)
            g0 = q * L + fi * FS
            p = ss % 2
            xsh = s_xsh[p]
            xtok = f"{P}_xsh{p}"
            em.dma("pool", xsh[:].rearrange(
                "pp (fc t) -> pp fc t", fc=FC)[:, 0:4],
                hxc[:, 0:4, g0:g0 + FS], writes=(xtok,),
                tag=f"{P}_lx1_{ss}")
            if fi == 0:
                em.dma("pool", xsh[:].rearrange(
                    "pp (fc t) -> pp fc t", fc=FC)[:, 4:8, 1:FS],
                    hxc[:, 4:8, g0:g0 + FS - 1], writes=(xtok,),
                    tag=f"{P}_lx2_{ss}")
                em.step("dve", lambda eng, p_=p: eng.memset(
                    s_xsh[p_][:].rearrange(
                        "pp (fc t) -> pp fc t", fc=FC)[:, 4:8, 0:1], 0.0),
                    writes=(xtok,), tag=f"{P}_z0_{ss}", small=True)
            else:
                em.dma("pool", xsh[:].rearrange(
                    "pp (fc t) -> pp fc t", fc=FC)[:, 4:8],
                    hxc[:, 4:8, g0 - 1:g0 + FS - 1], writes=(xtok,),
                    tag=f"{P}_lx2_{ss}")
            kall = tuple(f"{P}_k{p}_{hh}" for hh in range(NH))
            if fi == 0:
                em.step("dve", lambda eng, p_=p: eng.memset(
                    s_kaug[p_][0:64, :].rearrange(
                        "pp (h t) -> pp h t", h=NH)[:, :, 0:256], 0.0),
                    reads=kall, writes=kall, tag=f"{P}_ktz_{ss}")
            else:
                kprev = tuple(f"{P}_k{1 - p}_{hh}" for hh in range(NH))
                def ktail(eng, p_=p):
                    src = s_kaug[1 - p_][0:64, :].rearrange(
                        "pp (h t) -> pp h t", h=NH)[:, :, 512:768]
                    dst = s_kaug[p_][0:64, :].rearrange(
                        "pp (h t) -> pp h t", h=NH)[:, :, 0:256]
                    return eng.tensor_copy(dst, src)
                em.step("dve", ktail, reads=kall + kprev, writes=kall,
                        tag=f"{P}_ktl_{ss}")
            tbase = fi * FS
            em.dma("sp", s_tA[:], dr["c_csq"][:, tbase:tbase + FS],
                   writes=(f"{P}_tA",), tag=f"{P}_ltA_{ss}")
            em.step("dve", lambda eng, li_=li: eng.tensor_scalar_mul(
                s_tA[:], s_tA[:],
                self.sb["sec_qs2"][:, li_:li_ + 1]),
                reads=(f"{P}_tA", "sec_qs2"), writes=(f"{P}_tA",),
                tag=f"{P}_btA_{ss}")
            em.dma("sp", s_tC[:], dr["c_csi"][:, tbase:tbase + FS],
                   writes=(f"{P}_tC",), tag=f"{P}_ltC_{ss}")
            em.step("dve", lambda eng, li_=li: eng.tensor_scalar_mul(
                s_tC[:], s_tC[:],
                self.sb["sec_ks2"][:, li_:li_ + 1]),
                reads=(f"{P}_tC", "sec_ks2"), writes=(f"{P}_tC",),
                tag=f"{P}_btC_{ss}")
            em.dma("sp", s_sA[:], dr["c_ssq"][:, tbase:tbase + FS],
                   writes=(f"{P}_sA",), tag=f"{P}_lsA_{ss}")
            em.dma("sp", s_sC[:], dr["c_ssi"][:, tbase:tbase + FS],
                   writes=(f"{P}_sC",), tag=f"{P}_lsC_{ss}")
            for kk in range(2):
                for j in range(NH // 2):
                    colbase = (kk * 8 + j) * 128
                    b = f"pb{(kk * 8 + j) % 2}"
                    def qmm(eng, p_=p, cb=colbase, b_=b):
                        r = None
                        for kc in range(FC):
                            r = eng.matmul(
                                self.ps[b_][:],
                                self.sb["sec_wqkv"][:, kc * 24 * 128 + cb:
                                                    kc * 24 * 128 + cb + 128],
                                s_xsh[p_][:, kc * FS:(kc + 1) * FS],
                                start=(kc == 0), stop=(kc == FC - 1))
                        return r
                    em.step("pe", qmm, reads=(xtok, "sec_wqkv"),
                            writes=(b,), tag=f"{P}_qkv{ss}_{kk}_{j}")
                    d0, t0 = slot_of(kk, 2 * j, p)
                    d1, t1 = slot_of(kk, 2 * j + 1, p)
                    def qev(eng, b_=b, d0_=d0, d1_=d1):
                        eng.copy(d0_, self.ps[b_][0:64, :])
                        return eng.copy(d1_, self.ps[b_][64:128, :])
                    em.step("act", qev, reads=(b,), writes=(t0, t1, b),
                            tag=f"{P}_qe{ss}_{kk}_{j}")
            for kk in range(2):
                Rname = "sec_Rq" if kk == 0 else "sec_Rk"
                tA_b = s_tA if kk == 0 else s_tC
                tB_b = s_sA if kk == 0 else s_sC
                tAn = f"{P}_tA" if kk == 0 else f"{P}_tC"
                tBn = f"{P}_sA" if kk == 0 else f"{P}_sC"
                for h in range(NH):
                    t = h % 2
                    dsta, dtok = slot_of(kk, h, p)
                    em.step("dve", lambda eng, t_=t, src=dsta:
                            eng.tensor_mul(s_sq[t_][:], src, src),
                            reads=(dtok,), writes=(f"{P}_sq{t}",),
                            tag=f"{P}_qs{ss}_{kk}_{h}")
                    em.step("pe", lambda eng, t_=t, h_=h: eng.matmul(
                        self.ps["pb2"][0:16, :],
                        self.sb["sec_selsum"][:, h_ * 16:(h_ + 1) * 16],
                        s_sq[t_][:], start=(h_ == 0), stop=(h_ == NH - 1),
                        skip_group_check=True),
                        reads=(f"{P}_sq{t}", "sec_selsum"), writes=("pb2",),
                        tag=f"{P}_qn{ss}_{kk}_{h}")
                em.step("dve", lambda eng: eng.tensor_scalar_max(
                    s_nq[:], self.ps["pb2"][0:16, :], 1e-24),
                    reads=("pb2",), writes=(f"{P}_nq", "pb2"),
                    tag=f"{P}_rn1_{ss}_{kk}")
                def rnact(eng):
                    eng.activation(s_nq[:], s_nq[:], AF.Ln)
                    eng.drain()
                    return eng.activation(s_rn[:], s_nq[:], AF.Exp,
                                          scale=-0.5)
                em.step("act", rnact, reads=(f"{P}_nq",),
                        writes=(f"{P}_rn", f"{P}_nq"),
                        tag=f"{P}_rn2_{ss}_{kk}", small=True)
                for h in range(NH):
                    t = h % 2
                    j = h // 2
                    rb = f"pb{t}"
                    bb = f"pb{2 + j % 2}"
                    dsta, dtok = slot_of(kk, h, p)
                    em.step("pe", lambda eng, rb_=rb, Rn=Rname, src=dsta:
                            eng.matmul(self.ps[rb_][0:64, :],
                                       self.sb[Rn][:], src,
                                       start=True, stop=True),
                            reads=(dtok, Rname), writes=(rb,),
                            tag=f"{P}_qr{ss}_{kk}_{h}")
                    if t == 0:
                        em.step("pe", lambda eng, j_=j, bb_=bb: eng.matmul(
                            self.ps[bb_][:],
                            self.sb["sec_selbc"][:, j_ * 128:
                                                 (j_ + 1) * 128],
                            s_rn[:], start=True, stop=True),
                            reads=(f"{P}_rn", "sec_selbc"), writes=(bb,),
                            tag=f"{P}_rb{ss}_{kk}_{j}")
                    def comb(eng, t_=t, rb_=rb, bb_=bb, dst=dsta,
                             tA=tA_b, tB=tB_b):
                        eng.tensor_mul(s_sq[t_][:], dst, tA[:])
                        eng.tensor_tensor(
                            dst, self.ps[rb_][0:64, :], tB[:], ALU.mult)
                        eng.tensor_add(dst, dst, s_sq[t_][:])
                        return eng.tensor_mul(
                            dst, dst,
                            self.ps[bb_][t_ * 64:t_ * 64 + 64, :])
                    em.step("dve", comb,
                            reads=(dtok, rb, bb, tAn, tBn),
                            writes=(dtok, f"{P}_sq{t}", rb),
                            tag=f"{P}_qc{ss}_{kk}_{h}")

        def back(w):
            q, wi = divmod(w, WPS)
            g0 = q * L + wi * WS
            ss = w // 2
            p = ss % 2
            w2 = w % 2
            xtok = f"{P}_xsh{p}"
            PB = ("pbA", "pbB")[w2]
            mcol, scol = s_mcol[w2], s_scol[w2]
            et, ao = s_et[w2], s_ao[w2]
            for tb in range(2):
                def vmm(eng, p_=p, w2_=w2, tb_=tb):
                    r = None
                    for kc in range(FC):
                        base = kc * 24 * 128 + 16 * 128
                        stat = s_xsh[p_][:, kc * FS + w2_ * WS + tb_ * 128:
                                         kc * FS + w2_ * WS +
                                         (tb_ + 1) * 128]
                        eng.matmul(
                            self.ps[PB][:, 0:512], stat,
                            self.sb["sec_wqkv"][:, base:base + 512],
                            start=(kc == 0), stop=(kc == FC - 1),
                            skip_group_check=True)
                        r = eng.matmul(
                            self.ps[PB][:, 512:1024], stat,
                            self.sb["sec_wqkv"][:, base + 512:base + 1024],
                            start=(kc == 0), stop=(kc == FC - 1),
                            skip_group_check=True)
                    return r
                em.step("pe", vmm, reads=(xtok, "sec_wqkv"),
                        writes=(PB,), tag=f"{P}_v{w}_{tb}")
                rc = (w * 2 + tb) % 6
                em.step("act", lambda eng, rc_=rc: eng.copy(
                    s_vt[:, rc_ * 1024:(rc_ + 1) * 1024],
                    self.ps[PB][:, 0:1024]),
                    reads=(PB,), writes=(f"{P}_vt{rc}", PB),
                    tag=f"{P}_ve{w}_{tb}")
            units = [(qb, h) for qb in range(2) for h in range(NH)]
            for gi in range(0, len(units), 2):
                grp = units[gi:gi + 2]
                for ui, (qb, h) in enumerate(grp):
                    qb_abs = wi * 2 + qb
                    mv = 2 if qb_abs == 0 else (1 if qb_abs == 1 else 0)
                    def p1mm(eng, h_=h, qb_=qb, mv_=mv, p_=p, w2_=w2,
                             ui_=ui):
                        eng.matmul(
                            self.ps[PB][:, ui_ * 512:ui_ * 512 + 384],
                            s_qaug[p_][0:64,
                                       h_ * FS + w2_ * WS + qb_ * 128:
                                       h_ * FS + w2_ * WS +
                                       (qb_ + 1) * 128],
                            s_kaug[p_][0:64, :].rearrange(
                                "pp (hh t) -> pp hh t", hh=NH)
                            [:, h_, w2_ * WS + qb_ * 128:
                             w2_ * WS + qb_ * 128 + 384],
                            start=True, stop=False, skip_group_check=True)
                        return eng.matmul(
                            self.ps[PB][:, ui_ * 512:ui_ * 512 + 384],
                            self.sb["k_ident_bf"][:],
                            self.sb["sec_maskq"][:, mv_ * 384:
                                                 (mv_ + 1) * 384],
                            start=False, stop=True, skip_group_check=True)
                    em.step("pe", p1mm,
                            reads=(f"{P}_q{p}_{h}", f"{P}_k{p}_{h}",
                                   f"{P}_kones{p}", "sec_maskq",
                                   "k_ident_bf"),
                            writes=(PB,), tag=f"{P}_p1{w}_{qb}_{h}")
                rsteps = []
                for ui, (qb, h) in enumerate(grp):
                    cc = qb * 16 + h
                    last = ui == len(grp) - 1
                    rsteps.append(em.step(
                        "dve", lambda eng, cc_=cc, ui_=ui:
                        eng.tensor_reduce(
                            mcol[:, cc_:cc_ + 1],
                            self.ps[PB][:, ui_ * 512:ui_ * 512 + 384],
                            AX.X, ALU.max, negate=True),
                        reads=(PB,), writes=(f"{P}_mc{w2}_{cc}",),
                        tag=f"{P}_p1r{w}_{qb}_{h}", small=last))
                rlast = rsteps[-1]
                for ui, (qb, h) in enumerate(grp):
                    cc = qb * 16 + h
                    def p1e(eng, cc_=cc, ui_=ui):
                        return eng.activation(
                            et[:, 0:384],
                            self.ps[PB][:, ui_ * 512:ui_ * 512 + 384],
                            AF.Exp, bias=mcol[:, cc_:cc_ + 1],
                            accum_out=scol[:, cc_:cc_ + 1])
                    em.step("act", p1e, reads=(PB,), deps=(rlast,),
                            writes=(f"{P}_sc{w2}_{cc}", f"{P}_et{w2}", PB),
                            tag=f"{P}_p1e{w}_{qb}_{h}",
                            small=(ui == len(grp) - 1))
            mtoks = tuple(f"{P}_mc{w2}_{c}" for c in range(32))
            stoks = tuple(f"{P}_sc{w2}_{c}" for c in range(32))
            em.step("act", lambda eng: eng.activation(
                s_r12[:, 32:64], scol[:], AF.Ln),
                reads=stoks, writes=(f"{P}_r12",),
                tag=f"{P}_c1_{w}", small=True)
            def cfn2(eng):
                eng.tensor_sub(s_r12[:, 32:64], mcol[:], s_r12[:, 32:64])
                eng.drain()
                eng.tensor_copy(s_r12[:, 0:32], s_r12[:, 32:64])
                eng.drain()
                return eng.tensor_sub(s_r12[:, 32:64], s_r12[:, 32:64],
                                      s_r12[:, 0:32])
            em.step("dve", cfn2, reads=mtoks + (f"{P}_r12",),
                    writes=(f"{P}_r12",),
                    tag=f"{P}_c2_{w}", small=True)
            em.step("pe", lambda eng: eng.transpose(
                self.ps[PB][0:64, 0:128], s_r12[:],
                self.sb["k_ident_f32"][:]),
                reads=(f"{P}_r12", "k_ident_f32"), writes=(PB,),
                tag=f"{P}_ct_{w}")
            em.step("act", lambda eng: eng.copy(
                s_rT[:], self.ps[PB][0:64, 0:128]),
                reads=(PB,), writes=(f"{P}_rT", PB),
                tag=f"{P}_cte_{w}", small=True)
            for rr in range(2):
                for qb in range(2):
                    em.dma("pool",
                           s_qaug[p][64 + rr:65 + rr, :].rearrange(
                               "o (hh t) -> o hh t", hh=NH)
                           [:, :, w2 * WS + qb * 128:
                            w2 * WS + (qb + 1) * 128],
                           s_rT[rr * 32 + qb * 16:rr * 32 + (qb + 1) * 16, :],
                           reads=(f"{P}_rT",),
                           writes=(f"{P}_qr_{p}_{w2}",),
                           tag=f"{P}_cd{rr}{qb}_{w}")
            for h in range(NH):
                tiles = [(ki, kb, c) for ki, kb in
                         enumerate([wi - 1, wi]) if kb >= 0
                         for c in range(2)]
                for ti, (ki, kb, c) in enumerate(tiles):
                    koff = w2 * WS + ki * 256 + c * 128
                    mslot = ki * 2 + c if len(tiles) == 4 else 2 + c
                    es = ki * 2 + c
                    def p2(eng, h_=h, koff_=koff, mslot_=mslot, p_=p,
                           w2_=w2, es_=es):
                        eng.matmul(
                            self.ps[PB][:, es_ * 256:es_ * 256 + 256],
                            s_kaug[p_][0:66, :].rearrange(
                                "pp (hh t) -> pp hh t", hh=NH)
                            [:, h_, koff_:koff_ + 128],
                            s_qaug[p_][0:66, h_ * FS + w2_ * WS:
                                       h_ * FS + (w2_ + 1) * WS],
                            start=True, stop=False, skip_group_check=True)
                        return eng.matmul(
                            self.ps[PB][:, es_ * 256:es_ * 256 + 256],
                            self.sb["k_ident_bf"][:],
                            self.sb["sec_maskt"][:, mslot_ * 256:
                                                 (mslot_ + 1) * 256],
                            start=False, stop=True, skip_group_check=True)
                    em.step("pe", p2,
                            reads=(f"{P}_q{p}_{h}", f"{P}_qr_{p}_{w2}",
                                   f"{P}_k{p}_{h}", f"{P}_kones{p}",
                                   "sec_maskt", "k_ident_bf"),
                            writes=(PB,), tag=f"{P}_p2{w}_{h}_{ki}_{c}")
                kbs = [kb for kb in (wi - 1, wi) if kb >= 0]
                for ci, kb in enumerate(kbs):
                    ki = 1 - (wi - kb)
                    lo = ki * 2 * 256
                    def p2e(eng, lo_=lo):
                        return eng.activation(
                            et[:, 0:512],
                            self.ps[PB][:, lo_:lo_ + 512], AF.Exp)
                    em.step("act", p2e, reads=(PB,),
                            writes=(f"{P}_et{w2}", PB),
                            tag=f"{P}_p2e{w}_{h}_{ki}")
                    vtoks = tuple(f"{P}_vt{((q * WPS + kb) * 2 + c) % 6}"
                                  for c in range(2))
                    def avc(eng, h_=h, kb_=kb, q_=q, ci_=ci,
                            last=(ci == len(kbs) - 1)):
                        r = None
                        for c in range(2):
                            rc = ((q_ * WPS + kb_) * 2 + c) % 6
                            r = eng.matmul(
                                self.ps[PB][0:64, 0:256],
                                s_vt[:, rc * 1024 + h_ * 64:
                                     rc * 1024 + (h_ + 1) * 64],
                                et[:, c * 256:(c + 1) * 256],
                                start=(ci_ == 0 and c == 0),
                                stop=(last and c == 1),
                                skip_group_check=True)
                        return r
                    em.step("pe", avc, reads=vtoks + (f"{P}_et{w2}",),
                            writes=(PB,), tag=f"{P}_av{w}_{h}_{ci}")
                def ave(eng, h_=h):
                    dst = ao[64 * (h_ % 2):64 * (h_ % 2) + 64,
                             (h_ // 2) * WS:(h_ // 2 + 1) * WS]
                    return eng.copy(dst, self.ps[PB][0:64, 0:256])
                em.step("act", ave, reads=(PB,),
                        writes=(f"{P}_ao{w2}", PB),
                        tag=f"{P}_ave{w}_{h}")
            for oc in range(FC):
                xp2 = oc % 2
                em.dma("sp", s_xr[w2][xp2][:],
                       hxc[:, oc, g0:g0 + WS],
                       writes=(f"{P}_xr{w2}_{xp2}",),
                       tag=f"{P}_lxr{w}_{oc}")
                def womm(eng, oc_=oc):
                    r = None
                    for kc in range(FC):
                        r = eng.matmul(
                            self.ps[PB][:, 512:768],
                            self.sb["sec_wo"][:, (kc * FC + oc_) * 128:
                                              (kc * FC + oc_ + 1) * 128],
                            ao[:, kc * WS:(kc + 1) * WS],
                            start=(kc == 0), stop=(kc == FC - 1))
                    return r
                em.step("pe", womm, reads=(f"{P}_ao{w2}", "sec_wo"),
                        writes=(PB,), tag=f"{P}_wo{w}_{oc}")
                def resid(eng, xp_=xp2):
                    return eng.tensor_add(
                        s_xo[w2][:], self.ps[PB][:, 512:768],
                        s_xr[w2][xp_][:])
                em.step("dve", resid, reads=(PB, f"{P}_xr{w2}_{xp2}"),
                        writes=(f"{P}_xo{w2}", PB),
                        tag=f"{P}_wr{w}_{oc}")
                em.dma("pool", hxn[:, oc, g0:g0 + WS], s_xo[w2][:],
                       reads=(f"{P}_xo{w2}",),
                       writes=(f"hx2n_{w}_{oc}",),
                       tag=f"{P}_wod{w}_{oc}")

        em.begin_collect(); front(0); em.emit_collected(em.end_collect())
        for ss in range(1, NF + 1):
            em.begin_collect()
            if ss < NF:
                front(ss)
            lf = em.end_collect()
            em.begin_collect()
            back(2 * ss - 2)
            le = em.end_collect()
            em.begin_collect()
            back(2 * ss - 1)
            lo = em.end_collect()
            em.emit_collected(Em.merge(Em.merge(le, lo), lf))
        self.pop_scope()
        self.x2cur, self.x2nxt = self.x2nxt, self.x2cur

    def phase_f1(self, li):
        em, dr = self.em, self.dram
        sbn = self._sbuf
        SM = 512
        NS = T // SM
        P = f"f{li}"
        self.push_scope()
        s_xsh = [sbn(f"{P}_xsh{p}", [128, FC * SM], BF16) for p in range(2)]
        s_sqt = [sbn(f"{P}_sqt{p}", [128, SM], BF16) for p in range(2)]
        s_nf = sbn(f"{P}_nf", [128, SM], F32)
        s_xn = sbn(f"{P}_xn", [128, FC * T], BF16)
        lnb = self.ln_bufs(f"{P}ln", SM)
        s_w1 = [sbn(f"{P}_w1{p}", [128, 2 * FC * 128], BF16) for p in range(2)]
        s_gel = [sbn(f"{P}_gel{p}", [128, 4 * SM], BF16) for p in range(2)]
        s_glu = [sbn(f"{P}_glu{p}", [128, 4 * SM], BF16) for p in range(2)]
        s_lng = sbn(f"{P}_lng", [128, FC], F32)
        s_lnb = sbn(f"{P}_lnb", [128, FC], F32)
        s_b1 = sbn(f"{P}_b1", [128, 2 * JP], F32)
        em.dma("sp", s_lng[:], dr["ln_g"][:, li * FC:(li + 1) * FC],
               writes=(f"{P}_lng",), tag=f"{P}_llng")
        em.dma("sp", s_lnb[:], dr["ln_b"][:, li * FC:(li + 1) * FC],
               writes=(f"{P}_lnb",), tag=f"{P}_llnb")
        em.dma("sp", s_b1[:], dr["b1"][:, li * 2 * JP:(li + 1) * 2 * JP],
               writes=(f"{P}_b1",), tag=f"{P}_lb1")
        hxc = dr[self.x2cur].rearrange("(fc p) t -> p fc t", p=128)
        hglu = dr["hglu"].rearrange("(jp p) t -> p jp t", p=128)
        w1r = dr["w1"]

        def lnstripe(ms):
            q, si = divmod(ms, L // SM)
            g0 = q * L + si * SM
            p = ms % 2
            tok = lambda n: f"{P}_{n}{p}"
            xsh = s_xsh[p]
            em.dma("pool", xsh[:].rearrange(
                "pp (fc t) -> pp fc t", fc=FC)[:, 0:4],
                hxc[:, 0:4, g0:g0 + SM], writes=(tok("xsh"),),
                tag=f"{P}_lx1_{ms}")
            if si == 0:
                em.dma("pool", xsh[:].rearrange(
                    "pp (fc t) -> pp fc t", fc=FC)[:, 4:8, 1:SM],
                    hxc[:, 4:8, g0:g0 + SM - 1], writes=(tok("xsh"),),
                    tag=f"{P}_lx2_{ms}")
                em.step("dve", lambda eng, p_=p: eng.memset(
                    s_xsh[p_][:].rearrange(
                        "pp (fc t) -> pp fc t", fc=FC)[:, 4:8, 0:1], 0.0),
                    writes=(tok("xsh"),), tag=f"{P}_z0_{ms}", small=True)
            else:
                em.dma("pool", xsh[:].rearrange(
                    "pp (fc t) -> pp fc t", fc=FC)[:, 4:8],
                    hxc[:, 4:8, g0 - 1:g0 + SM - 1], writes=(tok("xsh"),),
                    tag=f"{P}_lx2_{ms}")
            for fc in range(FC):
                pf = fc % 2
                em.step("dve", lambda eng, p_=p, fc_=fc, pf_=pf:
                        eng.tensor_mul(
                            s_sqt[pf_][:],
                            s_xsh[p_][:, fc_ * SM:(fc_ + 1) * SM],
                            s_xsh[p_][:, fc_ * SM:(fc_ + 1) * SM]),
                        reads=(tok("xsh"),), writes=(f"{P}_sqt{pf}",),
                        tag=f"{P}_sq{ms}_{fc}")
                em.step("pe", lambda eng, p_=p, fc_=fc: eng.matmul(
                    self.ps["pb6"][0:1, :], self.sb["k_ones_128x1"][:],
                    s_xsh[p_][:, fc_ * SM:(fc_ + 1) * SM],
                    start=(fc_ == 0), stop=(fc_ == FC - 1),
                    skip_group_check=True),
                    reads=(tok("xsh"), "k_ones_128x1"), writes=("pb6",),
                    tag=f"{P}_msx{ms}_{fc}")
                em.step("pe", lambda eng, fc_=fc, pf_=pf: eng.matmul(
                    self.ps["pb7"][0:1, :], self.sb["k_ones_128x1"][:],
                    s_sqt[pf_][:], start=(fc_ == 0), stop=(fc_ == FC - 1),
                    skip_group_check=True),
                    reads=(f"{P}_sqt{pf}", "k_ones_128x1"),
                    writes=("pb7",), tag=f"{P}_msq{ms}_{fc}")
            self.emit_ln(f"{P}ln{ms}", self.ps["pb6"][0:1, :],
                         self.ps["pb7"][0:1, :], ("pb6", "pb7"), SM,
                         (self.ps["pb6"][:], "pb6", True),
                         (self.ps["pb7"][:], "pb7", True), lnb)
            for fc in range(FC):
                em.step("dve", lambda eng, p_=p, fc_=fc: eng.tensor_mul(
                    s_nf[:], s_xsh[p_][:, fc_ * SM:(fc_ + 1) * SM],
                    self.ps["pb6"][:]),
                    reads=(tok("xsh"), "pb6"), writes=(f"{P}_nf",),
                    tag=f"{P}_n1{ms}_{fc}")
                em.step("dve", lambda eng, fc_=fc: eng.tensor_sub(
                    s_nf[:], s_nf[:], self.ps["pb7"][:]),
                    reads=(f"{P}_nf", "pb7"), writes=(f"{P}_nf",),
                    tag=f"{P}_n2{ms}_{fc}")
                def aff(eng, fc_=fc, ms_=ms):
                    return eng.tensor_scalar(
                        s_xn[:, fc_ * T + ms_ * SM:fc_ * T + (ms_ + 1) * SM],
                        s_nf[:],
                        s_lng[:, fc_:fc_ + 1],
                        s_lnb[:, fc_:fc_ + 1],
                        ALU.mult, ALU.add)
                em.step("dve", aff,
                        reads=(f"{P}_nf", f"{P}_lng", f"{P}_lnb"),
                        writes=(f"{P}_xn_{ms}",), tag=f"{P}_af{ms}_{fc}")

        for k0 in range(0, NS, 2):
            self.emit_pair(lnstripe, k0)

        for jp in range(JP):
            slot = jp % 2
            self.cast_load_split(
                self.sb[f"{P}_w1{slot}"],
                w1r[(li * JP + jp) * 128:(li * JP + jp + 1) * 128, :],
                None, writes=(f"{P}_w1_{slot}",), tag=f"{P}_lw1_{jp}",
                sub=(0, 2048))
            for half in range(2):
                pass
            for mq in range(2):
                bq = mq % 2
                pg = mq
                for half in (1, 0):
                    for m4 in range(4):
                        m = mq * 4 + m4
                        b = f"pb{(0 if bq == 0 else 4) + m4}"
                        xtok = f"{P}_xn_{m}"
                        def w1mm(eng, jp_=jp, sl=slot, h_=half, m_=m, b_=b):
                            r = None
                            for kc in range(FC):
                                r = eng.matmul(
                                    self.ps[b_][:],
                                    self.sb[f"{P}_w1{sl}"][
                                        :, h_ * FC * 128 + kc * 128:
                                        h_ * FC * 128 + (kc + 1) * 128],
                                    s_xn[:, kc * T + m_ * SM:
                                         kc * T + (m_ + 1) * SM],
                                    start=(kc == 0), stop=(kc == FC - 1))
                            return r
                        em.step("pe", w1mm,
                                reads=(xtok, f"{P}_w1_{slot}"),
                                writes=(b,),
                                tag=f"{P}_w1m{jp}_{half}_{m}")
                        if half == 1:
                            def gelf(eng, jp_=jp, m4_=m4, b_=b,
                                     pg_=pg):
                                return eng.activation(
                                    s_gel[pg_][:, m4_ * SM:(m4_ + 1) * SM],
                                    self.ps[b_][:], AF.Gelu,
                                    bias=s_b1[:, JP + jp_:JP + jp_ + 1])
                            em.step("act", gelf, reads=(b, f"{P}_b1"),
                                    writes=(f"{P}_gel{pg}", b),
                                    tag=f"{P}_gel{jp}_{m4}_{mq}")
                        else:
                            def gluf(eng, jp_=jp, m4_=m4, b_=b,
                                     pg_=pg):
                                return eng.scalar_tensor_tensor(
                                    s_glu[pg_][:, m4_ * SM:(m4_ + 1) * SM],
                                    self.ps[b_][:],
                                    s_b1[:, jp_:jp_ + 1],
                                    s_gel[pg_][:, m4_ * SM:(m4_ + 1) * SM],
                                    ALU.add, ALU.mult)
                            em.step("dve", gluf,
                                    reads=(b, f"{P}_gel{pg}", f"{P}_b1"),
                                    writes=(f"{P}_glu{pg}", b),
                                    tag=f"{P}_glu{jp}_{m4}_{mq}")
                em.dma("sp", hglu[:, jp, mq * 4 * SM:(mq + 1) * 4 * SM],
                       s_glu[pg][:], reads=(f"{P}_glu{pg}",),
                       writes=(f"hglu_{jp}_{mq}",),
                       tag=f"{P}_od{jp}_{mq}")
        self.pop_scope()

    def phase_f2(self, li):
        em, dr = self.em, self.dram
        sbn = self._sbuf
        SM = 512
        G = 1024
        P = f"g{li}"
        self.push_scope()
        s_glu = [sbn(f"{P}_glu{p}", [128, JP * G], BF16) for p in range(2)]
        s_w2 = [sbn(f"{P}_w2{p}", [128, JP * 128], BF16) for p in range(2)]
        s_xr = [sbn(f"{P}_xr{p}", [128, SM], F32) for p in range(2)]
        s_xo = [sbn(f"{P}_xo{p}", [128, SM], F32) for p in range(2)]
        s_b2 = sbn(f"{P}_b2", [128, FC], F32)
        em.dma("sp", s_b2[:], dr["b2"][:, li * FC:(li + 1) * FC],
               writes=(f"{P}_b2",), tag=f"{P}_lb2")
        hxc = dr[self.x2cur].rearrange("(fc p) t -> p fc t", p=128)
        hxn = dr[self.x2nxt].rearrange("(fc p) t -> p fc t", p=128)
        hglu = dr["hglu"].rearrange("(jp p) t -> p jp t", p=128)
        w2r = dr["w2"]

        def group(g):
            g0 = g * G
            gp = g % 2
            em.dma("sp", s_glu[gp][:].rearrange("p (jp t) -> p jp t", jp=JP),
                   hglu[:, :, g0:g0 + G], reads=(f"hglu_fin",),
                   writes=(f"{P}_glu{gp}",), tag=f"{P}_lg{g}")
            for oc in range(FC):
                slot = oc % 2
                self.cast_load_split(
                    self.sb[f"{P}_w2{slot}"],
                    w2r[(li * FC + oc) * 128:(li * FC + oc + 1) * 128, :],
                    None, writes=(f"{P}_w2_{slot}",),
                    tag=f"{P}_lw2_{g}_{oc}", sub=(0, JP * 128))
                for m in range(2):
                    b = f"pb{(oc % 2) * 4 + m}"
                    xp2 = m % 2
                    em.dma("sp", s_xr[xp2][:],
                           hxc[:, oc, g0 + m * SM:g0 + (m + 1) * SM],
                           writes=(f"{P}_xr{xp2}",),
                           tag=f"{P}_lxr{g}_{oc}_{m}")
                    def w2mm(eng, sl=slot, m_=m, b_=b, gp_=gp):
                        r = None
                        for kc in range(JP):
                            r = eng.matmul(
                                self.ps[b_][:],
                                self.sb[f"{P}_w2{sl}"][:, kc * 128:
                                                       (kc + 1) * 128],
                                s_glu[gp_][:, kc * G + m_ * SM:
                                           kc * G + (m_ + 1) * SM],
                                start=(kc == 0), stop=(kc == JP - 1))
                        return r
                    em.step("pe", w2mm,
                            reads=(f"{P}_glu{gp}", f"{P}_w2_{slot}"),
                            writes=(b,), tag=f"{P}_w2m{g}_{oc}_{m}")
                    def resid(eng, oc_=oc, b_=b, xp_=xp2):
                        return eng.scalar_tensor_tensor(
                            s_xo[xp_][:], self.ps[b_][:],
                            s_b2[:, oc_:oc_ + 1],
                            s_xr[xp_][:], ALU.add, ALU.add)
                    em.step("dve", resid,
                            reads=(b, f"{P}_b2", f"{P}_xr{xp2}"),
                            writes=(f"{P}_xo{xp2}", b),
                            tag=f"{P}_r{g}_{oc}_{m}")
                    em.dma("sp", hxn[:, oc, g0 + m * SM:g0 + (m + 1) * SM],
                           s_xo[xp2][:], reads=(f"{P}_xo{xp2}",),
                           writes=(f"hx2f_{g}_{oc}_{m}",),
                           tag=f"{P}_od{g}_{oc}_{m}")

        for k0 in range(0, T // G, 2):
            self.emit_pair(group, k0)
        self.pop_scope()
        self.x2cur, self.x2nxt = self.x2nxt, self.x2cur

    def phase_fin(self):
        em, dr = self.em, self.dram
        sbn = self._sbuf
        SM = 512
        NS = T // SM
        P = "fin"
        self.push_scope()
        s_x2 = [sbn(f"{P}_x2{p}", [128, FC * SM], F32) for p in range(2)]
        s_x1 = [sbn(f"{P}_x1{p}", [128, FC * SM], F32) for p in range(2)]
        s_sqt = [sbn(f"{P}_sqt{p}", [128, SM], BF16) for p in range(2)]
        s_xbt = [sbn(f"{P}_xbt{p}", [128, SM], BF16) for p in range(2)]
        s_nf = sbn(f"{P}_nf", [128, SM], F32)
        s_xf = [sbn(f"{P}_xf{p}", [128, FC * SM], F32) for p in range(2)]
        s_to = [sbn(f"{P}_to{p}", [128, 1024], F32) for p in range(2)]
        lnb = self.ln_bufs(f"{P}ln", SM)
        hxc = dr[self.x2cur].rearrange("(fc p) t -> p fc t", p=128)
        hx1 = dr["hx1"].rearrange("(fc p) t -> p fc t", p=128)
        def stripe(ms):
            q, si = divmod(ms, L // SM)
            g0 = q * L + si * SM
            p = ms % 2
            tok = lambda n: f"{P}_{n}{p}"
            em.dma("sp", s_x2[p][:].rearrange("p (fc t) -> p fc t", fc=FC),
                   hxc[:, :, g0:g0 + SM], writes=(tok("x2"),),
                   tag=f"{P}_lx{ms}")
            em.dma("sp", s_x1[p][:].rearrange("p (fc t) -> p fc t", fc=FC),
                   hx1[:, :, g0:g0 + SM], reads=("hx1_fin",),
                   writes=(tok("x1"),), tag=f"{P}_l1{ms}")
            for fc in range(FC):
                pf = fc % 2
                em.step("dve", lambda eng, p_=p, fc_=fc, pf_=pf:
                        eng.tensor_mul(
                            s_sqt[pf_][:],
                            s_x2[p_][:, fc_ * SM:(fc_ + 1) * SM],
                            s_x2[p_][:, fc_ * SM:(fc_ + 1) * SM]),
                        reads=(tok("x2"),), writes=(f"{P}_sqt{pf}",),
                        tag=f"{P}_sq{ms}_{fc}")
                em.step("dve", lambda eng, p_=p, fc_=fc, pf_=pf:
                        eng.tensor_copy(
                            s_xbt[pf_][:],
                            s_x2[p_][:, fc_ * SM:(fc_ + 1) * SM]),
                        reads=(tok("x2"),), writes=(f"{P}_xbt{pf}",),
                        tag=f"{P}_xb{ms}_{fc}")
                em.step("pe", lambda eng, fc_=fc, pf_=pf: eng.matmul(
                    self.ps["pb0"][0:1, :], self.sb["k_ones_128x1"][:],
                    s_xbt[pf_][:], start=(fc_ == 0), stop=(fc_ == FC - 1),
                    skip_group_check=True),
                    reads=(f"{P}_xbt{pf}", "k_ones_128x1"), writes=("pb0",),
                    tag=f"{P}_msx{ms}_{fc}")
                em.step("pe", lambda eng, fc_=fc, pf_=pf: eng.matmul(
                    self.ps["pb1"][0:1, :], self.sb["k_ones_128x1"][:],
                    s_sqt[pf_][:], start=(fc_ == 0), stop=(fc_ == FC - 1),
                    skip_group_check=True),
                    reads=(f"{P}_sqt{pf}", "k_ones_128x1"), writes=("pb1",),
                    tag=f"{P}_msq{ms}_{fc}")
            self.emit_ln(f"{P}ln{ms}", self.ps["pb0"][0:1, :],
                         self.ps["pb1"][0:1, :], ("pb0", "pb1"), SM,
                         (self.ps["pb2"][:], "pb2", True),
                         (self.ps["pb3"][:], "pb3", True), lnb)
            for fc in range(FC):
                def xffn(eng, p_=p, fc_=fc):
                    eng.tensor_mul(s_nf[:],
                                   s_x2[p_][:, fc_ * SM:(fc_ + 1) * SM],
                                   self.ps["pb2"][:])
                    eng.tensor_sub(s_nf[:], s_nf[:], self.ps["pb3"][:])
                    return eng.tensor_add(
                        s_xf[p_][:, fc_ * SM:(fc_ + 1) * SM], s_nf[:],
                        s_x1[p_][:, fc_ * SM:(fc_ + 1) * SM])
                em.step("dve", xffn,
                        reads=(tok("x2"), tok("x1"), "pb2", "pb3"),
                        writes=(tok("xf"), f"{P}_nf"),
                        tag=f"{P}_xf{ms}_{fc}")
            for tb in range(4):
                bb = 4 + 2 * (tb % 2)
                def trf(eng, p_=p, tb_=tb, bb_=bb):
                    r = None
                    for fc in range(FC):
                        bank = f"pb{bb_ + fc // 4}"
                        r = eng.transpose(
                            self.ps[bank][:, (fc % 4) * 128:
                                          (fc % 4 + 1) * 128],
                            s_xf[p_][:, fc * SM + tb_ * 128:
                                     fc * SM + (tb_ + 1) * 128],
                            self.sb["k_ident_f32"][:])
                    return r
                em.step("pe", trf, reads=(tok("xf"), "k_ident_f32"),
                        writes=(f"pb{bb}", f"pb{bb + 1}"),
                        tag=f"{P}_tr{ms}_{tb}")
                pt = tb % 2
                def tev(eng, bb_=bb, pt_=pt):
                    eng.copy(s_to[pt_][:, 0:512], self.ps[f"pb{bb_}"][:])
                    return eng.copy(s_to[pt_][:, 512:1024],
                                    self.ps[f"pb{bb_ + 1}"][:])
                em.step("act", tev, reads=(f"pb{bb}", f"pb{bb + 1}"),
                        writes=(f"{P}_to{pt}",), tag=f"{P}_te{ms}_{tb}")
                em.dma("sp", dr["out"][g0 + tb * 128:g0 + (tb + 1) * 128, :],
                       s_to[pt][:], reads=(f"{P}_to{pt}",),
                       writes=(f"out_{ms}_{tb}",), tag=f"{P}_od{ms}_{tb}")
        for k0 in range(0, NS, 2):
            self.emit_pair(stripe, k0)
        self.pop_scope()

    def finish(self):
        em = self.em
        em.finalize(self.nc, self.root)
        with self.nc.Block() as block:
            @block.tensor
            def _(eng):
                em.run_engine("pe", eng)
            @block.vector
            def _(eng):
                em.run_engine("dve", eng)
            @block.scalar
            def _(eng):
                em.run_engine("act", eng)
            @block.gpsimd
            def _(eng):
                em.run_engine("pool", eng)
            @block.sync
            def _(eng):
                em.run_engine("sp", eng)
                em.final_barrier(eng)


def build_nc(n_phases=99):
    nc = bass.Bass("TRN2", target_bir_lowering=False,
                   detect_race_conditions=False)
    mk = MK(nc, n_phases)
    mk.load_consts()
    mk.phase_m1()
    if n_phases >= 2:
        mk.phase_m2()
    if n_phases >= 3:
        mk.section_begin()
        for li in range(6):
            if n_phases >= 3 + li * 3:
                mk.phase_attn(li)
            if n_phases >= 4 + li * 3:
                mk.phase_f1(li)
                if li < 5 and n_phases >= 6 + li * 3:
                    mk.load_layer_weights(li + 1)
            if n_phases >= 5 + li * 3:
                mk.phase_f2(li)
        mk.section_end()
    if n_phases >= 21:
        mk.phase_fin()
    mk.finish()
    nc.finalize()
    return mk, nc


def kernel(**inputs):
    consts = _consts()
    shared = prep_shared_inputs(inputs, consts)
    in_maps = []
    for core in range(NCORES):
        m = dict(shared)
        m.update(prep_core_inputs(core, inputs, consts))
        in_maps.append(m)
    mk, nc = build_nc(int(os.environ.get("MK_PHASES", "99")))
    res = run_bass_kernel_spmd(nc, in_maps, core_ids=list(range(NCORES)))
    out = np.concatenate(
        [r["out"].reshape(SEQ, L, D) for r in res.results], axis=0)
    return out.astype(np.float32)
